# version 02
# baseline (speedup 1.0000x reference)
import numpy as np
import ml_dtypes
import concourse.bass as bass
import concourse.mybir as mybir
from concourse.bass_utils import run_bass_kernel_spmd
from contextlib import ExitStack, contextmanager

F32 = mybir.dt.float32
BF16 = mybir.dt.bfloat16
I32 = mybir.dt.int32
U32 = mybir.dt.uint32
AF = mybir.ActivationFunctionType
ALU = mybir.AluOpType
AX = mybir.AxisListType
ts = bass.ts
ds = bass.ds

B_CORE = 2
T = 2048
D = 1024
NH = 16
DH = 64
NC = 8

ENGINES = ("g", "sy", "t", "v", "s")
_ENG_INC = {"g": 16, "sy": 16, "t": 1, "v": 1, "s": 1}
_EXPR_CACHE = {}



_TPB_DONE = {}


def tpb_prep(e):
    if id(e) not in _TPB_DONE:
        r = e.alloc_register64("tpbb")
        e.tpb_base_ld(r)
        _TPB_DONE[id(e)] = r
    return e.nop()


def CE(e, kv, mult, off=0):
    key = (id(e), id(kv), mult, off)
    sv = _EXPR_CACHE.get(key)
    if sv is None:
        sv = e.compute_val(kv * mult + off)
        _EXPR_CACHE[key] = sv
    return sv


class _Op:
    __slots__ = ("eng", "fn", "inc", "pos")

    def __init__(self, eng, fn, inc, pos):
        self.eng, self.fn, self.inc, self.pos = eng, fn, inc, pos


class _LoopRec:
    __slots__ = ("n", "items", "per", "pos")

    def __init__(self, n, items, per, pos):
        self.n, self.items, self.per, self.pos = n, items, per, pos


class Prog:
    def __init__(self):
        self.nc = bass.Bass("TRN2", target_bir_lowering=False, debug=False)
        self.root = []
        self._stack = [(self.root, {e: 0 for e in ENGINES})]

    def _rec(self, ename, fn, inc):
        items, cnt = self._stack[-1]
        items.append(_Op(ename, fn, inc, dict(cnt)))
        cnt[ename] += inc

    def g(self, fn):
        self._rec("g", fn, 16)

    def gc(self, fn):
        self._rec("g", fn, 1)

    def sy(self, fn):
        self._rec("sy", fn, 16)

    def t(self, fn):
        self._rec("t", fn, 1)

    def v(self, fn):
        self._rec("v", fn, 1)

    def s(self, fn):
        self._rec("s", fn, 1)

    @contextmanager
    def Loop(self, n_iters):
        assert n_iters >= 1
        items = []
        cnt = {e: 0 for e in ENGINES}
        self._stack.append((items, cnt))
        yield
        self._stack.pop()
        parent_items, parent_cnt = self._stack[-1]
        rec = _LoopRec(n_iters, items, dict(cnt), dict(parent_cnt))
        parent_items.append(rec)
        for e in ENGINES:
            parent_cnt[e] += n_iters * cnt[e]

    def emit(self, ctx):
        nc = self.nc
        self._final = dict(self._stack[0][1])
        sems = {e: ctx.enter_context(nc.semaphore(f"sem_{e}")) for e in ENGINES}
        block = ctx.enter_context(nc.Block())

        def eng_has_ops(items, ename):
            for it in items:
                if isinstance(it, _Op):
                    if it.eng == ename:
                        return True
                else:
                    if it.per[ename]:
                        return True
            return False

        def emit_stream(eng, ename):
            wscratch = eng.alloc_register("wscr")
            wscr2 = eng.alloc_register("wscr2")
            last_static = {e: 0 for e in ENGINES}
            pending = []

            def emit_wait(e, static_tgt, kstack_pers):
                terms = [(kv, per[e]) for kv, per in kstack_pers if per[e]]
                if not terms:
                    if static_tgt > last_static[e]:
                        eng.wait_ge(sems[e], static_tgt)
                        last_static[e] = static_tgt
                    return
                kv0, per0 = terms[0]
                eng.reg_mul(wscr2, kv0, per0)
                for kvx, perx in terms[1:]:
                    eng.reg_mul(wscratch, kvx, perx)
                    eng.reg_alu(wscr2, wscr2, wscratch, ALU.add)
                eng.reg_add(wscr2, wscr2, static_tgt)
                eng.wait_ge(sems[e], wscr2)

            def flush_pending():
                while pending:
                    d = pending.pop(0)
                    for e, tgt in d.items():
                        if tgt > last_static[e]:
                            eng.wait_ge(sems[e], tgt)
                            last_static[e] = tgt

            def walk(items, base, kstack, prev):
                for it in items:
                    if isinstance(it, _Op):
                        if it.eng == ename:
                            flush_pending()
                            if prev[0] is not None:
                                emit_wait(prev[0], prev[1], prev[2])
                            kv = kstack[-1][0] if kstack else None
                            r = it.fn(eng, kv, [kk for kk, _ in kstack])
                            r.then_inc(sems[ename], it.inc)
                        prev[0] = it.eng
                        prev[1] = base[it.eng] + it.pos[it.eng] + it.inc
                        prev[2] = list(kstack)
                    else:
                        if it.n == 0:
                            continue
                        assert not kstack, "loops only at root level"
                        new_base = {e: base[e] + it.pos[e] for e in ENGINES}
                        if eng_has_ops(it.items, ename):
                            flush_pending()
                            if prev[0] is not None:
                                emit_wait(prev[0], prev[1], prev[2])
                            _EXPR_CACHE.clear()
                            _loop_regs = []
                            _cb = nc.register_alloc_callback(
                                lambda *a: _loop_regs.extend(
                                    x for x in a if type(x).__name__ == "RegisterHandle"))
                            with eng.Fori(0, it.n) as kv:
                                kst = kstack + [(kv, it.per)]
                                for e in ENGINES:
                                    if not it.per[e]:
                                        continue
                                    eng.reg_mul(wscr2, kv, it.per[e])
                                    eng.reg_add(wscr2, wscr2, new_base[e])
                                    eng.wait_ge(sems[e], wscr2)
                                walk(it.items, new_base, kst, prev)
                            nc.unregister_alloc_callback(_cb)
                            for e in ENGINES:
                                last_static[e] = -10 ** 9
                            _EXPR_CACHE.clear()
                            try:
                                h = eng.lower_val(kv)
                                nc.free_register(h)
                            except Exception:
                                pass
                            for h in _loop_regs:
                                try:
                                    nc.free_register(h)
                                except Exception:
                                    pass
                        pending.append(
                            {e: new_base[e] + it.n * it.per[e]
                             for e in ENGINES if it.per[e]})
                        prev[0] = None
                        prev[1] = 0
                        prev[2] = []

            walk(self.root, {e: 0 for e in ENGINES}, [], [None, 0, []])
            flush_pending()
            for e in ENGINES:
                if e != ename and self._final[e] > last_static[e]:
                    eng.wait_ge(sems[e], self._final[e])

        @block.gpsimd
        def _(g):
            emit_stream(g, "g")

        @block.sync
        def _(sy):
            emit_stream(sy, "sy")

        @block.tensor
        def _(t):
            emit_stream(t, "t")

        @block.vector
        def _(v):
            emit_stream(v, "v")

        @block.scalar
        def _(s):
            emit_stream(s, "s")



def _rope_tables():
    i = np.arange(DH // 2, dtype=np.float32)
    theta = (10000.0 ** (-2.0 * (i - 1.0) / DH)).astype(np.float32)
    m = np.arange(T, dtype=np.float32)
    ang = (m[:, None] * theta[None, :]).astype(np.float32)
    c = np.ascontiguousarray(np.cos(ang).astype(np.float32).T)
    s = np.ascontiguousarray(np.sin(ang).astype(np.float32).T)
    cs = np.concatenate([c, s, c, s], axis=0)
    sc = np.concatenate([s, c, s, c], axis=0)
    return np.ascontiguousarray(cs), np.ascontiguousarray(sc)


def _causal_masks():
    kk = np.arange(128)[:, None]
    qq = np.arange(512)[None, :]
    masks = [(kk <= qq - 128 * j).astype(np.float32) for j in range(4)]
    return np.ascontiguousarray(np.concatenate(masks, axis=1))


def _pair_sliced(wT):
    w4 = wT.reshape(8, 128, 8, 128)
    out = w4.transpose(2, 1, 0, 3).reshape(1024, 1024)
    return np.ascontiguousarray(out)


def _chunk_rows(wT):
    n = wT.shape[1]
    w3 = wT.reshape(8, 128, n).transpose(1, 0, 2).reshape(128, 8 * n)
    return np.ascontiguousarray(out_arr(w3))


def out_arr(a):
    return np.ascontiguousarray(a)


def l0_attn_host_inputs(x_full, w_qkv, w_out, b_out, core):
    xT = np.ascontiguousarray(
        x_full[2 * core:2 * core + 2].transpose(0, 2, 1)).astype(np.float32)
    cs, sc = _rope_tables()
    return {
        "xT": xT,
        "wq_ps": _pair_sliced(np.ascontiguousarray(w_qkv[0:D].T)),
        "wk_ps": _pair_sliced(np.ascontiguousarray(w_qkv[D:2 * D].T)),
        "wv_ps": _pair_sliced(np.ascontiguousarray(w_qkv[2 * D:3 * D].T)),
        "wo_cr": _chunk_rows(np.ascontiguousarray(w_out.T)),
        "bout": np.ascontiguousarray(b_out.reshape(8, 128).T),
        "cs_tab": cs, "sc_tab": sc, "cmask": _causal_masks(),
    }



def emit_l0_attn(p, dr):
    nc = p.nc
    xT = dr["xT"]
    wq_ps = dr["wq_ps"]
    wk_ps = dr["wk_ps"]
    wv_ps = dr["wv_ps"]
    wo_cr = dr["wo_cr"]
    boutd = dr["bout"]
    cs_t = dr["cs_tab"]
    sc_t = dr["sc_tab"]
    cmaskd = dr["cmask"]
    x1 = dr["x1"]
    y_dram = nc.dram_tensor("y_dram", [D, T], F32).ap()

    with ExitStack() as ctx:
        def sb(name, shape, dt=BF16):
            return ctx.enter_context(nc.sbuf_tensor(name, shape, dt))

        def pt(name, shape):
            return ctx.enter_context(nc.psum_tensor(name, shape, F32))

        xb = sb("xb", [128, 8, T])
        wqs = sb("wqs", [128, 8, 128])
        wks = sb("wks", [128, 8, 128])
        wvs = sb("wvs", [128, 8, 128])
        wbuf = sb("wbuf", [128, 8, D])
        qp = sb("qp", [128, T]); kp = sb("kp", [128, T])
        qrp = sb("qrp", [128, T]); krp = sb("krp", [128, T])
        m1t = sb("m1t", [128, T]); m2t = sb("m2t", [128, T])
        rtmp = sb("rtmp", [128, T])
        vpair = sb("vpair", [128, T])
        ybufs = sb("ybufs", [128, 8, 512])
        cs_b = sb("cs_b", [128, T]); sc_b = sb("sc_b", [128, T])
        cm_b = sb("cm_b", [128, 2048])
        ones128 = sb("ones128", [128, 1])
        ones1 = sb("ones1", [1, 64])
        exp_s = sb("exp_s", [128, 512])
        exp_n = sb("exp_n", [128, 512])
        ysc = sb("ysc", [64, 512], F32)
        ypair1 = sb("ypair1", [128, T], F32)
        rden_bf = sb("rden_bf", [1, 512])
        bc_sb = sb("bc_sb", [64, 512])
        sum_bf = sb("sum_bf", [128, 8, 512])
        sq_bf = sb("sq_bf", [128, 512])
        bout_s = sb("bout_s", [128, 8], F32)
        xres = sb("xres", [128, 512], F32)
        sprev = sb("sprev", [128, 512], F32)
        outf = sb("outf", [128, 512], F32)
        rden_f = sb("rden_f", [1, 512], F32)
        mrow = sb("mrow", [1, 512], F32)
        m2row = sb("m2row", [1, 512], F32)
        varr = sb("varr", [1, 512], F32)
        mrow_bf = sb("mrow_bf", [1, 512])
        rstd_bf = sb("rstd_bf", [1, 512])
        mb_sb = sb("mb_sb", [128, 512], F32)
        rs_sb = sb("rs_sb", [128, 512], F32)
        dum = sb("dum", [1, 1], F32)

        psA = pt("psA", [128, 512])
        psB = pt("psB", [128, 512])
        psC = pt("psC", [128, 512])
        psD = pt("psD", [1, 512])
        psE = pt("psE", [1, 512])
        psY = pt("psY", [64, 512])

        p.g(lambda e, k, ks: e.dma_start(out=cs_b[:, :], in_=cs_t))
        p.g(lambda e, k, ks: e.dma_start(out=sc_b[:, :], in_=sc_t))
        p.g(lambda e, k, ks: e.dma_start(out=cm_b[:, :], in_=cmaskd))
        p.g(lambda e, k, ks: e.dma_start(out=bout_s[:, :], in_=boutd))
        p.v(lambda e, k, ks: e.memset(ones128[:, :], 1.0))
        p.v(lambda e, k, ks: e.memset(ones1[:, :], 1.0))

        for b in range(B_CORE):
            for dc in range(8):
                p.g(lambda e, k, ks, b=b, dc=dc: e.dma_start(
                    out=xb[:, dc, :], in_=xT[b, ds(dc * 128, 128), :]))

            with p.Loop(8):
                for wsl, wdr in ((wqs, wq_ps), (wks, wk_ps), (wvs, wv_ps)):
                    p.g(lambda e, k, ks, wsl=wsl, wdr=wdr: e.dma_start(
                        out=wsl[:, :, :].rearrange("p c f -> p (c f)"),
                        in_=wdr[ts(k, 128), :]))
                for tc in range(4):
                    for (wsl, dst, ps) in ((wqs, qp, psA), (wks, kp, psB)):
                        for dc in range(8):
                            p.t(lambda e, k, ks, wsl=wsl, ps=ps, dc=dc, tc=tc: e.matmul(
                                ps[:, :], wsl[:, dc, :], xb[:, dc, ds(tc * 512, 512)],
                                start=(dc == 0), stop=(dc == 7)))
                        p.v(lambda e, k, ks, dst=dst, ps=ps, tc=tc: e.tensor_copy(
                            dst[:, ds(tc * 512, 512)], ps[:, :]))
                for tt in range(16):
                    for dc in range(8):
                        p.t(lambda e, k, ks, dc=dc, tt=tt: e.matmul(
                            psC[:, 0:128], xb[:, dc, ds(tt * 128, 128)], wvs[:, dc, :],
                            start=(dc == 0), stop=(dc == 7)))
                    p.v(lambda e, k, ks, tt=tt: e.tensor_copy(
                        vpair[:, ds(tt * 128, 128)], psC[:, 0:128]))
                for (src, dst) in ((qp, qrp), (kp, krp)):
                    p.v(lambda e, k, ks, src=src: e.tensor_tensor(
                        m1t[:, :], src[:, :], cs_b[:, :], ALU.mult))
                    p.v(lambda e, k, ks, src=src: e.tensor_tensor(
                        m2t[:, :], src[:, :], sc_b[:, :], ALU.mult))
                    for h in range(2):
                        b0 = h * 64
                        p.v(lambda e, k, ks, b0=b0: e.tensor_copy(
                            rtmp[b0:b0 + 32, :], m1t[b0 + 32:b0 + 64, :]))
                        p.v(lambda e, k, ks, dst=dst, b0=b0: e.tensor_tensor(
                            dst[b0:b0 + 32, :], m1t[b0:b0 + 32, :],
                            rtmp[b0:b0 + 32, :], ALU.add))
                        p.v(lambda e, k, ks, b0=b0: e.tensor_copy(
                            rtmp[b0 + 32:b0 + 64, :], m2t[b0:b0 + 32, :]))
                        p.v(lambda e, k, ks, dst=dst, b0=b0: e.tensor_tensor(
                            dst[b0 + 32:b0 + 64, :], m2t[b0 + 32:b0 + 64, :],
                            rtmp[b0 + 32:b0 + 64, :], ALU.subtract))

                for h in range(2):
                    hr = h * 64
                    for qc in range(4):
                        for kc in range(16):
                            p.t(lambda e, k, ks, hr=hr, qc=qc, kc=kc: e.matmul(
                                psA[:, :], kp[hr:hr + 64, ds(kc * 128, 128)],
                                qp[hr:hr + 64, ds(qc * 512, 512)], start=True, stop=True))
                            p.s(lambda e, k, ks: e.activation(
                                exp_s[:, :], psA[:, :], AF.Exp, scale=0.125))
                            p.t(lambda e, k, ks, kc=kc: e.matmul(
                                psD[:, :], ones128[:, :], exp_s[:, :],
                                start=(kc == 0), stop=(kc == 15)))
                            if kc <= 4 * qc + 3:
                                p.t(lambda e, k, ks, hr=hr, qc=qc, kc=kc: e.matmul(
                                    psA[:, :], krp[hr:hr + 64, ds(kc * 128, 128)],
                                    qrp[hr:hr + 64, ds(qc * 512, 512)],
                                    start=True, stop=True))
                                p.s(lambda e, k, ks: e.activation(
                                    exp_n[:, :], psA[:, :], AF.Exp, scale=0.125))
                                if kc >= 4 * qc:
                                    j = kc - 4 * qc
                                    p.v(lambda e, k, ks, j=j: e.tensor_tensor(
                                        exp_n[:, :], exp_n[:, :],
                                        cm_b[:, ds(j * 512, 512)], ALU.mult))
                                p.t(lambda e, k, ks, h=h, kc=kc: e.matmul(
                                    psY[:, :], vpair[:, ds(kc * 128 + h * 64, 64)],
                                    exp_n[:, :], start=(kc == 0), stop=False))
                        p.v(lambda e, k, ks: e.reciprocal(rden_f[:, :], psD[:, :]))
                        p.v(lambda e, k, ks: e.tensor_copy(rden_bf[:, :], rden_f[:, :]))
                        p.t(lambda e, k, ks: e.matmul(
                            psB[0:64, :], ones1[:, :], rden_bf[:, :], start=True, stop=True))
                        p.v(lambda e, k, ks: e.tensor_copy(bc_sb[:, :], psB[0:64, :]))
                        p.v(lambda e, k, ks, h=h, qc=qc: e.tensor_tensor(
                            ypair1[h * 64:h * 64 + 64, ds(qc * 512, 512)],
                            psY[:, :], bc_sb[:, :], ALU.mult))
                p.sy(lambda e, k, ks: e.dma_start(
                    out=y_dram[ts(k, 128), :], in_=ypair1[:, :]))

            p.g(lambda e, k, ks: e.dma_start(
                out=wbuf[:, :, :].rearrange("p c f -> p (c f)"), in_=wo_cr))
            for tc in range(4):
                for dc in range(8):
                    p.g(lambda e, k, ks, dc=dc, tc=tc: e.dma_start(
                        out=ybufs[:, dc, :],
                        in_=y_dram[ds(dc * 128, 128), ds(tc * 512, 512)]))
                for oc in range(8):
                    for dc in range(8):
                        p.t(lambda e, k, ks, dc=dc, oc=oc, tc=tc: e.matmul(
                            psA[:, :], wbuf[:, dc, ds(oc * 128, 128)],
                            ybufs[:, dc, :],
                            start=(dc == 0), stop=(dc == 7)))
                    p.g(lambda e, k, ks, b=b, oc=oc, tc=tc: e.dma_start(
                        out=xres[:, :], in_=xT[b, ds(oc * 128, 128), ds(tc * 512, 512)]))
                    p.v(lambda e, k, ks, oc=oc: e.tensor_scalar(
                        sprev[:, :], psA[:, :], bout_s[:, oc:oc + 1], None, ALU.add))
                    p.v(lambda e, k, ks: e.tensor_tensor(
                        sprev[:, :], sprev[:, :], xres[:, :], ALU.add))
                    p.v(lambda e, k, ks, oc=oc: e.tensor_copy(sum_bf[:, oc, :], sprev[:, :]))
                    p.s(lambda e, k, ks: e.activation(sq_bf[:, :], sprev[:, :], AF.Square))
                    p.t(lambda e, k, ks, oc=oc: e.matmul(
                        psD[:, :], ones128[:, :], sum_bf[:, oc, :],
                        start=(oc == 0), stop=(oc == 7)))
                    p.t(lambda e, k, ks, oc=oc: e.matmul(
                        psE[:, :], ones128[:, :], sq_bf[:, :],
                        start=(oc == 0), stop=(oc == 7)))
                p.v(lambda e, k, ks: e.tensor_scalar(
                    mrow[:, :], psD[:, :], 1.0 / D, None, ALU.mult))
                p.v(lambda e, k, ks: e.tensor_scalar(
                    m2row[:, :], psE[:, :], 1.0 / D, None, ALU.mult))
                p.v(lambda e, k, ks: e.tensor_tensor(
                    varr[:, :], mrow[:, :], mrow[:, :], ALU.mult))
                p.v(lambda e, k, ks: e.tensor_tensor(
                    varr[:, :], m2row[:, :], varr[:, :], ALU.subtract))
                p.v(lambda e, k, ks: e.tensor_scalar(varr[:, :], varr[:, :], 1e-5, None, ALU.add))
                p.s(lambda e, k, ks: e.activation(varr[:, :], varr[:, :], AF.Sqrt))
                p.v(lambda e, k, ks: e.reciprocal(varr[:, :], varr[:, :]))
                p.v(lambda e, k, ks: e.tensor_copy(mrow_bf[:, :], mrow[:, :]))
                p.v(lambda e, k, ks: e.tensor_copy(rstd_bf[:, :], varr[:, :]))
                p.t(lambda e, k, ks: e.matmul(
                    psB[0:64, :], ones1[:, :], mrow_bf[:, :], start=True, stop=True))
                p.t(lambda e, k, ks: e.matmul(
                    psC[0:64, :], ones1[:, :], rstd_bf[:, :], start=True, stop=True))
                p.v(lambda e, k, ks: e.tensor_copy(mb_sb[0:64, :], psB[0:64, :]))
                p.v(lambda e, k, ks: e.tensor_copy(rs_sb[0:64, :], psC[0:64, :]))
                p.v(lambda e, k, ks: e.tensor_copy(mb_sb[64:128, :], psB[0:64, :]))
                p.v(lambda e, k, ks: e.tensor_copy(rs_sb[64:128, :], psC[0:64, :]))
                for oc in range(8):
                    p.v(lambda e, k, ks, oc=oc: e.tensor_tensor(
                        outf[:, :], sum_bf[:, oc, :], mb_sb[:, :], ALU.subtract))
                    p.v(lambda e, k, ks: e.tensor_tensor(
                        outf[:, :], outf[:, :], rs_sb[:, :], ALU.mult))
                    p.sy(lambda e, k, ks, b=b, oc=oc, tc=tc: e.dma_start(
                        out=x1[b, ds(oc * 128, 128), ds(tc * 512, 512)], in_=outf[:, :]))

        p.gc(lambda e, k, ks: e.memset(dum[:, :], 0.0))



def build_l0_attn():
    p = Prog()
    nc = p.nc
    xT = nc.dram_tensor("xT", [B_CORE, D, T], F32, kind="ExternalInput").ap()
    wq_ps = nc.dram_tensor("wq_ps", [D, D], F32, kind="ExternalInput").ap()
    wk_ps = nc.dram_tensor("wk_ps", [D, D], F32, kind="ExternalInput").ap()
    wv_ps = nc.dram_tensor("wv_ps", [D, D], F32, kind="ExternalInput").ap()
    wo_cr = nc.dram_tensor("wo_cr", [128, 8 * D], F32, kind="ExternalInput").ap()
    boutd = nc.dram_tensor("bout", [128, 8], F32, kind="ExternalInput").ap()
    cs_t = nc.dram_tensor("cs_tab", [128, T], F32, kind="ExternalInput").ap()
    sc_t = nc.dram_tensor("sc_tab", [128, T], F32, kind="ExternalInput").ap()
    cmaskd = nc.dram_tensor("cmask", [128, 2048], F32, kind="ExternalInput").ap()
    x1 = nc.dram_tensor("x1", [B_CORE, D, T], F32, kind="ExternalOutput").ap()
    y_dram = nc.dram_tensor("y_dram", [D, T], F32).ap()

    dr = {"xT": xT,
          "wq_ps": wq_ps,
          "wk_ps": wk_ps,
          "wv_ps": wv_ps,
          "wo_cr": wo_cr,
          "bout": boutd,
          "cs_tab": cs_t,
          "sc_tab": sc_t,
          "cmask": cmaskd,
          "x1": x1}
    emit_l0_attn(p, dr)
    with ExitStack() as _ctx2:
        p.emit(_ctx2)
    return p.nc



def l0_ffn_host_inputs(x1_unused, inputs, core):
    wg = inputs["L0_ffn_w_gate"]; wu = inputs["L0_ffn_w_up"]; wd = inputs["L0_ffn_w_down"]
    sg = inputs["L0_ffn_shared_gate_w"]; su = inputs["L0_ffn_shared_up_w"]
    sd = inputs["L0_ffn_shared_down_w"]
    def cr(mat):
        return mat.reshape(8, 128, 1024).transpose(1, 0, 2).reshape(128, 8192)
    wg_cr = np.stack([cr(np.ascontiguousarray(wg[e].T)) for e in range(16)]
                     + [cr(np.ascontiguousarray(sg.T))], 0).reshape(17 * 128, 8192)
    wu_cr = np.stack([cr(np.ascontiguousarray(wu[e].T)) for e in range(16)]
                     + [cr(np.ascontiguousarray(su.T))], 0).reshape(17 * 128, 8192)
    wd_cr = np.stack([cr(np.ascontiguousarray(wd[e].T)) for e in range(16)]
                     + [cr(np.ascontiguousarray(sd.T))], 0).reshape(17 * 128, 8192)
    gwT = np.ascontiguousarray(inputs["L0_ffn_gate_w"].T)
    gw_cr = gwT.reshape(8, 128, 16).transpose(1, 0, 2).reshape(128, 128)
    ebias = np.ascontiguousarray(
        np.broadcast_to(inputs["L0_ffn_expert_bias"][None, :], (128, 16)).copy())
    return {"wg_cr": np.ascontiguousarray(wg_cr.astype(np.float32)),
            "wu_cr": np.ascontiguousarray(wu_cr.astype(np.float32)),
            "wd_cr": np.ascontiguousarray(wd_cr.astype(np.float32)),
            "gw_cr": np.ascontiguousarray(gw_cr.astype(np.float32)),
            "ebias": ebias.astype(np.float32)}


TT = 4096


def emit_l0_ffn(p, dr):
    nc = p.nc
    x1d = dr["x1"]
    wg_cr = dr["wg_cr"]
    wu_cr = dr["wu_cr"]
    wd_cr = dr["wd_cr"]
    gw_crd = dr["gw_cr"]
    ebiasd = dr["ebias"]
    x2 = dr["x2"]
    comb_dram = nc.dram_tensor("comb_dram", [17, 4096], F32).ap()
    y_dram = nc.dram_tensor("y_dram2", [128, 8 * 4096], F32).ap()

    with ExitStack() as ctx:
        def sb(name, shape, dt=BF16):
            return ctx.enter_context(nc.sbuf_tensor(name, shape, dt))

        def pt(name, shape):
            return ctx.enter_context(nc.psum_tensor(name, shape, F32))

        xc = sb("xc", [128, 8, TT])
        wgs = sb("wgs", [128, 8, 1024])
        wus = sb("wus", [128, 8, 1024])
        wds = sb("wds", [128, 8, 1024])
        gws = sb("gws", [128, 8, 16])
        hbuf = sb("hbuf", [128, 8, 1024])
        g_sb = sb("g_sb", [16, TT])
        idtb = sb("idtb", [128, 128])
        gtok = sb("gtok", [128, 32 * 16], F32)
        comb_all = sb("comb_all", [128, 32 * 16], F32)
        gwork = sb("gwork", [128, 16], F32)
        ctmp = sb("ctmp", [16, 128], F32)
        ysc = sb("ysc2", [128, 512], F32)
        scores = sb("scores", [128, 16], F32)
        sel = sb("sel", [128, 16], F32)
        msum = sb("msum", [128, 16], F32)
        mx1 = sb("mx1", [128, 1], F32)
        rs1 = sb("rs1", [128, 1], F32)
        ebias_s = sb("ebias_s", [128, 16], F32)
        ce_sb = sb("ce_sb", [1, 4096], F32)
        ce_bf = sb("ce_bf", [1, 4096])
        ce_bc = sb("ce_bc", [128, 512])
        sgm = sb("sgm", [128, 512])
        htmp = sb("htmp", [128, 512])
        hfin = sb("hfin", [128, 512])
        idt = sb("idt", [128, 128], F32)
        ones1_128 = sb("ones1_128", [1, 128])
        ones128 = sb("ones128b", [128, 1])
        sum_bf = sb("sum_bf2", [128, 8, 512])
        sq_bf = hfin
        xres = sb("xres2", [128, 512], F32)
        sprev = sb("sprev2", [128, 512], F32)
        outf = sb("outf2", [128, 512], F32)
        mrow = sb("mrow2", [1, 512], F32)
        m2row = sb("m2row2", [1, 512], F32)
        varr = sb("varr2", [1, 512], F32)
        mrow_bf = sb("mrow_bf2", [1, 512])
        rstd_bf = sb("rstd_bf2", [1, 512])
        mb_sb = sb("mb_sb2", [128, 512])
        rs_sb = sb("rs_sb2", [128, 512])
        ones1 = sb("ones1b", [1, 64])
        dum = sb("dum2", [1, 1], F32)

        psG = pt("psG", [128, 512])
        psU = pt("psU", [128, 512])
        psB2 = pt("psB2", [128, 512])
        psY2 = pt("psY2", [128, 512])
        psS = pt("psS", [1, 512])
        psT = ctx.enter_context(nc.psum_tensor("psT", [128, 16], BF16))
        psQ = pt("psQ", [1, 512])

        p.g(lambda e, k, ks: e.dma_start(out=gws[:, :, :].rearrange("p c f -> p (c f)"), in_=gw_crd))
        p.g(lambda e, k, ks: e.dma_start(out=ebias_s[:, :], in_=ebiasd))
        p.v(lambda e, k, ks: e.memset(ones1_128[:, :], 1.0))
        p.v(lambda e, k, ks: e.memset(ones128[:, :], 1.0))
        p.v(lambda e, k, ks: e.memset(ones1[:, :], 1.0))
        p.gc(lambda e, k, ks: e.iota(idt[:, :], pattern=[[-1, 128]], base=0,
                                     channel_multiplier=1,
                                     allow_small_or_imprecise_dtypes=True))
        p.v(lambda e, k, ks: e.tensor_scalar(idt[:, :], idt[:, :], 0.0, None, ALU.is_equal))
        p.v(lambda e, k, ks: e.tensor_copy(idtb[:, :], idt[:, :]))
        for b in range(B_CORE):
            for dc in range(8):
                p.g(lambda e, k, ks, b=b, dc=dc: e.dma_start(
                    out=xc[:, dc, ds(b * T, T)], in_=x1d[b, ds(dc * 128, 128), :]))
        p.v(lambda e, k, ks: e.memset(ce_sb[:, :], 1.0))
        p.sy(lambda e, k, ks: e.dma_start(out=comb_dram[16:17, :], in_=ce_sb[:, :]))

        with p.Loop(8):
            for dc in range(8):
                p.t(lambda e, k, ks, dc=dc: e.matmul(
                    psB2[0:16, :], gws[:, dc, :], xc[:, dc, ts(k, 512)],
                    start=(dc == 0), stop=(dc == 7)))
            p.v(lambda e, k, ks: e.tensor_copy(g_sb[:, ts(k, 512)], psB2[0:16, :]))
        for tile in range(32):
            p.t(lambda e, k, ks, tile=tile: e.transpose(
                psT[:, :], g_sb[:, ds(tile * 128, 128)], idtb[0:16, 0:16]))
            p.v(lambda e, k, ks, tile=tile: e.tensor_copy(
                gtok[:, ds(tile * 16, 16)], psT[:, :]))
        with p.Loop(32):
            p.v(lambda e, k, ks: e.tensor_copy(gwork[:, :], gtok[:, ts(k, 16)]))
            p.s(lambda e, k, ks: e.activation(scores[:, :], gwork[:, :], AF.Sigmoid))
            p.v(lambda e, k, ks: e.tensor_tensor(sel[:, :], scores[:, :], ebias_s[:, :], ALU.add))
            p.v(lambda e, k, ks: e.memset(msum[:, :], 0.0))
            for i in range(8):
                p.v(lambda e, k, ks: e.tensor_reduce(mx1[:, :], sel[:, :], AX.X, ALU.max))
                p.v(lambda e, k, ks: e.tensor_scalar(gwork[:, :], sel[:, :], mx1[:, 0:1], None, ALU.is_ge))
                p.v(lambda e, k, ks: e.tensor_tensor(msum[:, :], msum[:, :], gwork[:, :], ALU.add))
                p.v(lambda e, k, ks: e.scalar_tensor_tensor(
                    sel[:, :], gwork[:, :], -1e30, sel[:, :], ALU.mult, ALU.add))
            p.v(lambda e, k, ks: e.tensor_tensor(gwork[:, :], scores[:, :], msum[:, :], ALU.mult))
            p.v(lambda e, k, ks: e.tensor_reduce(rs1[:, :], gwork[:, :], AX.X, ALU.add))
            p.v(lambda e, k, ks: e.reciprocal(rs1[:, :], rs1[:, :]))
            p.v(lambda e, k, ks: e.tensor_scalar(gwork[:, :], gwork[:, :], rs1[:, 0:1], 2.826, ALU.mult, op1=ALU.mult))
            p.v(lambda e, k, ks: e.tensor_copy(comb_all[:, ts(k, 16)], gwork[:, :]))
        for tile in range(32):
            p.t(lambda e, k, ks, tile=tile: e.transpose(
                psB2[0:16, 0:128], comb_all[:, ds(tile * 16, 16)], idt[:, :]))
            p.v(lambda e, k, ks, tile=tile: e.tensor_copy(
                ctmp[:, :], psB2[0:16, 0:128]))
            p.sy(lambda e, k, ks, tile=tile: e.dma_start(
                out=comb_dram[0:16, ds(tile * 128, 128)], in_=ctmp[:, :]))

        p.v(lambda e, k, ks: e.memset(ysc[:, :], 0.0))
        for oc in range(8):
            for tcg in range(8):
                p.sy(lambda e, k, ks, oc=oc, tcg=tcg: e.dma_start(
                    out=y_dram[:, ds(oc * 4096 + tcg * 512, 512)], in_=ysc[:, :]))

        with p.Loop(17):
            p.g(lambda e, k, ks: e.dma_start(
                out=wgs[:, :, :].rearrange("p c f -> p (c f)"), in_=wg_cr[ts(k, 128), :]))
            p.g(lambda e, k, ks: e.dma_start(
                out=wus[:, :, :].rearrange("p c f -> p (c f)"), in_=wu_cr[ts(k, 128), :]))
            p.g(lambda e, k, ks: e.dma_start(
                out=wds[:, :, :].rearrange("p c f -> p (c f)"), in_=wd_cr[ts(k, 128), :]))
            p.g(lambda e, k, ks: e.dma_start(out=ce_sb[:, :], in_=comb_dram[ts(k, 1), :]))
            p.v(lambda e, k, ks: e.tensor_copy(ce_bf[:, :], ce_sb[:, :]))
            for tcg in range(8):
                t0 = tcg * 512
                p.t(lambda e, k, ks, t0=t0: e.matmul(
                    psB2[:, :], ones1_128[:, :], ce_bf[:, ds(t0, 512)],
                    start=True, stop=True))
                p.v(lambda e, k, ks: e.tensor_copy(ce_bc[:, :], psB2[:, :]))
                for ic in range(8):
                    for dc in range(8):
                        p.t(lambda e, k, ks, dc=dc, ic=ic, t0=t0: e.matmul(
                            psG[:, :], wgs[:, dc, ds(ic * 128, 128)],
                            xc[:, dc, ds(t0, 512)], start=(dc == 0), stop=(dc == 7)))
                    for dc in range(8):
                        p.t(lambda e, k, ks, dc=dc, ic=ic, t0=t0: e.matmul(
                            psU[:, :], wus[:, dc, ds(ic * 128, 128)],
                            xc[:, dc, ds(t0, 512)], start=(dc == 0), stop=(dc == 7)))
                    p.s(lambda e, k, ks: e.activation(sgm[:, :], psG[:, :], AF.Sigmoid))
                    p.v(lambda e, k, ks: e.tensor_tensor(htmp[:, :], psG[:, :], sgm[:, :], ALU.mult))
                    p.v(lambda e, k, ks: e.tensor_tensor(hfin[:, :], htmp[:, :], psU[:, :], ALU.mult))
                    p.v(lambda e, k, ks, ic=ic: e.tensor_tensor(
                        hbuf[:, ic, 0:512], hfin[:, :], ce_bc[:, :], ALU.mult))
                for oc in range(8):
                    for ic in range(8):
                        p.t(lambda e, k, ks, ic=ic, oc=oc: e.matmul(
                            psY2[:, :], wds[:, ic, ds(oc * 128, 128)],
                            hbuf[:, ic, 0:512], start=(ic == 0), stop=(ic == 7)))
                    p.v(lambda e, k, ks: e.tensor_copy(ysc[:, :], psY2[:, :]))
                    p.g(lambda e, k, ks, oc=oc, t0=t0: e.dma_start(
                        out=y_dram[:, ds(oc * 4096 + t0, 512)], in_=ysc[:, :],
                        accum_op=ALU.add))

        for tcg in range(8):
            t0 = tcg * 512
            b = t0 // T
            toff = t0 % T
            for oc in range(8):
                p.g(lambda e, k, ks, oc=oc, t0=t0: e.dma_start(
                    out=sprev[:, :], in_=y_dram[:, ds(oc * 4096 + t0, 512)]))
                p.g(lambda e, k, ks, b=b, oc=oc, toff=toff: e.dma_start(
                    out=xres[:, :], in_=x1d[b, ds(oc * 128, 128), ds(toff, 512)]))
                p.v(lambda e, k, ks: e.tensor_tensor(sprev[:, :], sprev[:, :], xres[:, :], ALU.add))
                p.v(lambda e, k, ks, oc=oc: e.tensor_copy(sum_bf[:, oc, :], sprev[:, :]))
                p.s(lambda e, k, ks: e.activation(sq_bf[:, :], sprev[:, :], AF.Square))
                p.t(lambda e, k, ks, oc=oc: e.matmul(
                    psS[:, :], ones128[:, :], sum_bf[:, oc, :],
                    start=(oc == 0), stop=(oc == 7)))
                p.t(lambda e, k, ks, oc=oc: e.matmul(
                    psQ[:, :], ones128[:, :], sq_bf[:, :],
                    start=(oc == 0), stop=(oc == 7)))
            p.v(lambda e, k, ks: e.tensor_scalar(mrow[:, :], psS[:, :], 1.0 / D, None, ALU.mult))
            p.v(lambda e, k, ks: e.tensor_scalar(m2row[:, :], psQ[:, :], 1.0 / D, None, ALU.mult))
            p.v(lambda e, k, ks: e.tensor_tensor(varr[:, :], mrow[:, :], mrow[:, :], ALU.mult))
            p.v(lambda e, k, ks: e.tensor_tensor(varr[:, :], m2row[:, :], varr[:, :], ALU.subtract))
            p.v(lambda e, k, ks: e.tensor_scalar(varr[:, :], varr[:, :], 1e-5, None, ALU.add))
            p.s(lambda e, k, ks: e.activation(varr[:, :], varr[:, :], AF.Sqrt))
            p.v(lambda e, k, ks: e.reciprocal(varr[:, :], varr[:, :]))
            p.v(lambda e, k, ks: e.tensor_copy(mrow_bf[:, :], mrow[:, :]))
            p.v(lambda e, k, ks: e.tensor_copy(rstd_bf[:, :], varr[:, :]))
            p.t(lambda e, k, ks: e.matmul(psB2[0:64, :], ones1[:, :], mrow_bf[:, :], start=True, stop=True))
            p.t(lambda e, k, ks: e.matmul(psG[0:64, :], ones1[:, :], rstd_bf[:, :], start=True, stop=True))
            p.v(lambda e, k, ks: e.tensor_copy(mb_sb[0:64, :], psB2[0:64, :]))
            p.v(lambda e, k, ks: e.tensor_copy(rs_sb[0:64, :], psG[0:64, :]))
            p.v(lambda e, k, ks: e.tensor_copy(mb_sb[64:128, :], psB2[0:64, :]))
            p.v(lambda e, k, ks: e.tensor_copy(rs_sb[64:128, :], psG[0:64, :]))
            for oc in range(8):
                p.v(lambda e, k, ks, oc=oc: e.tensor_tensor(
                    outf[:, :], sum_bf[:, oc, :], mb_sb[:, :], ALU.subtract))
                p.v(lambda e, k, ks: e.tensor_tensor(outf[:, :], outf[:, :], rs_sb[:, :], ALU.mult))
                p.sy(lambda e, k, ks, b=b, oc=oc, toff=toff: e.dma_start(
                    out=x2[b, ds(oc * 128, 128), ds(toff, 512)], in_=outf[:, :]))

        p.gc(lambda e, k, ks: e.memset(dum[:, :], 0.0))


def build_l0_ffn():
    p = Prog()
    nc = p.nc
    x1d = nc.dram_tensor("x1", [B_CORE, D, T], F32, kind="ExternalInput").ap()
    wg_cr = nc.dram_tensor("wg_cr", [17 * 128, 8192], F32, kind="ExternalInput").ap()
    wu_cr = nc.dram_tensor("wu_cr", [17 * 128, 8192], F32, kind="ExternalInput").ap()
    wd_cr = nc.dram_tensor("wd_cr", [17 * 128, 8192], F32, kind="ExternalInput").ap()
    gw_crd = nc.dram_tensor("gw_cr", [128, 128], F32, kind="ExternalInput").ap()
    ebiasd = nc.dram_tensor("ebias", [128, 16], F32, kind="ExternalInput").ap()
    x2 = nc.dram_tensor("x2", [B_CORE, D, T], F32, kind="ExternalOutput").ap()
    comb_dram = nc.dram_tensor("comb_dram", [17, 4096], F32).ap()
    y_dram = nc.dram_tensor("y_dram2", [128, 8 * 4096], F32).ap()


    TT = 4096

    dr = {"x1": x1d,
          "wg_cr": wg_cr,
          "wu_cr": wu_cr,
          "wd_cr": wd_cr,
          "gw_cr": gw_crd,
          "ebias": ebiasd,
          "x2": x2}
    emit_l0_ffn(p, dr)
    with ExitStack() as _ctx2:
        p.emit(_ctx2)
    return p.nc



def _l1_band_masks():
    kk = np.arange(128)[:, None]
    qq = np.arange(128)[None, :]
    blocks = []
    for dlt in (-1, 0, 1):
        rel = kk + 128 * dlt - qq
        blocks.append((np.abs(rel) <= 3).astype(np.float32))
    m = np.concatenate(blocks, axis=1)
    mL = m.copy(); mL[:, 0:128] = 0.0
    mR = m.copy(); mR[:, 256:384] = 0.0
    return np.ascontiguousarray(np.concatenate([m, mL, mR], axis=1))


def _edge_weights():
    wh = np.zeros((8, 3), np.float32)
    for q in range(3):
        for r in range(6):
            wh[r, q] = 1.0 if r <= q + 3 else 0.0
        wh[6, q] = 3 - q
    wt = np.zeros((8, 3), np.float32)
    for j in range(3):
        for r in range(6):
            wt[r, j] = 1.0 if r >= j else 0.0
        wt[6, j] = j + 1
    return np.ascontiguousarray(wh), np.ascontiguousarray(wt)


def l1_attn_host_inputs(inputs):
    Wq = inputs["L1_attn_W_qkv"]
    wh, wt = _edge_weights()
    b = inputs["L1_attn_b_qkv"]
    return {
        "wq1": _pair_sliced(np.ascontiguousarray(Wq[0:D].T)),
        "wk1": _pair_sliced(np.ascontiguousarray(Wq[D:2 * D].T)),
        "wv1": _pair_sliced(np.ascontiguousarray(Wq[2 * D:3 * D].T)),
        "bq1": np.ascontiguousarray(b[0:D].reshape(8, 128).T),
        "bk1": np.ascontiguousarray(b[D:2 * D].reshape(8, 128).T),
        "bv1": np.ascontiguousarray(b[2 * D:3 * D].reshape(8, 128).T),
        "wlatT": np.ascontiguousarray(inputs["L1_attn_W_lat"].T),
        "blat": np.ascontiguousarray(inputs["L1_attn_b_lat"].reshape(64, 1)),
        "wproj": _chunk_rows(np.ascontiguousarray(inputs["L1_attn_W_proj"].T)),
        "bproj": np.ascontiguousarray(inputs["L1_attn_b_proj"].reshape(8, 128).T),
        "wg1": _chunk_rows(np.ascontiguousarray(inputs["L1_ffn_Wg"])),
        "bands": _l1_band_masks(),
        "ewh": wh, "ewt": wt,
        "U128": np.triu(np.ones((128, 128), np.float32)),
        "U16s": np.triu(np.ones((16, 16), np.float32), 1),
        "U32s": np.triu(np.ones((32, 32), np.float32), 1),
    }


def emit_l1_attn(p, dr):
    nc = p.nc
    x2d = dr["x2"]
    wq1 = dr["wq1"]
    wk1 = dr["wk1"]
    wv1 = dr["wv1"]
    bq1 = dr["bq1"]
    bk1 = dr["bk1"]
    bv1 = dr["bv1"]
    wlatT = dr["wlatT"]
    blatd = dr["blat"]
    wproj = dr["wproj"]
    bprojd = dr["bproj"]
    wg1 = dr["wg1"]
    bandsd = dr["bands"]
    ewhd = dr["ewh"]
    ewtd = dr["ewt"]
    U128d = dr["U128"]
    U16sd = dr["U16s"]
    U32sd = dr["U32s"]
    x3bf = dr["x3bf"]
    hist_o = dr["hist"]
    gates_o = dr["gates"]
    ohT_o = dr["ohT"]
    locT_o = dr["locT"]
    slots_o = dr["slots"]
    t2e_o = dr["t2e"]
    y_dram = nc.dram_tensor("y_dram3", [D, T], F32).ap()
    cb_dram = nc.dram_tensor("cb_dram", [32, 16], F32).ap()

    with ExitStack() as ctx:
        def sb(name, shape, dt=BF16):
            return ctx.enter_context(nc.sbuf_tensor(name, shape, dt))

        def pt(name, shape, dt=F32):
            return ctx.enter_context(nc.psum_tensor(name, shape, dt))

        xb = sb("xb3", [128, 8, T])
        wqs = sb("wqs3", [128, 8, 128])
        wks = sb("wks3", [128, 8, 128])
        wvs = sb("wvs3", [128, 8, 128])
        wbuf = sb("wbuf3", [128, 8, D])
        x3sb = sb("x3sb", [128, 8, 4096])
        qp = sb("qp3", [128, T]); kp = sb("kp3", [128, T]); vp = sb("vp3", [128, T])
        klat = sb("klat3", [128, T])
        vlt = sb("vlt3", [128, 16 * 64])
        wlat_s = sb("wlat_s", [128, 64])
        qe3 = sb("qe3", [64, 3])
        blat_s = sb("blat_s", [64, 1], F32)
        blat_r = sb("blat_r", [1, 64])
        bq_s = sb("bq_s", [128, 8], F32)
        bk_s = sb("bk_s", [128, 8], F32)
        bv_s = sb("bv_s", [128, 8], F32)
        bproj_s = sb("bproj_s", [128, 8], F32)
        bands = sb("bands_s", [128, 1152])
        mexp = sb("mexp", [128, 384])
        denr = sb("denr", [1, 384], F32)
        rden = sb("rden3", [1, 128], F32)
        rdenb = sb("rdenb3", [1, 128])
        bcy = sb("bcy3", [64, 128])
        ysc = sb("ysc3", [64, 128], F32)
        y_pair = sb("y_pair", [128, T], F32)
        ones128 = sb("ones128c", [128, 1])
        ones1 = sb("ones1c", [1, 64])
        ones8 = sb("ones8", [8, 1])
        ones16c = sb("ones16c", [16, 1])
        onesf1 = sb("onesf1", [1, 128], F32)
        ones1t = sb("ones1t", [1, 128])
        ke = sb("ke", [64, 8])
        ve = sb("ve", [8, 64])
        ewh_s = sb("ewh_s", [8, 3], F32)
        ewt_s = sb("ewt_s", [8, 3], F32)
        ee = sb("ee", [8, 3], F32)
        web = sb("web", [8, 3])
        rde = sb("rde", [1, 3], F32)
        rdeb = sb("rdeb", [1, 3])
        bce = sb("bce", [64, 3])
        ysce = sb("ysce", [64, 3], F32)
        ybufs = sb("ybufs3", [128, 8, 512])
        sum_bf = sb("sum_bf3", [128, 8, 512])
        sq_bf = sb("sq_bf3", [128, 512])
        xres = sb("xres3", [128, 512], F32)
        sprev = sb("sprev3", [128, 512], F32)
        outf = sb("outf3", [128, 512], F32)
        mrow = sb("mrow3", [1, 512], F32)
        m2row = sb("m2row3", [1, 512], F32)
        varr = sb("varr3", [1, 512], F32)
        mrow_bf = sb("mrow_bf3", [1, 512])
        rstd_bf = sb("rstd_bf3", [1, 512])
        mb_sb = sb("mb_sb3", [128, 512])
        rs_sb = sb("rs_sb3", [128, 512])
        idt = sb("idt3", [128, 128], F32)
        idb = sb("idb3", [128, 128])
        xtb = sb("xtb", [128, 128])
        wgs = sb("wgs3", [128, 8, 16])
        U128b = sb("U128b", [128, 128])
        U16sb = sb("U16sb", [16, 16])
        oh_tok = sb("oh_tok", [128, 16])
        oh_all = sb("oh_all", [128, 512])
        nmx = sb("nmx", [128, 1], F32)
        mx1 = sb("mx13", [128, 1], F32)
        el = sb("el3", [128, 16], F32)
        den1 = sb("den13", [128, 1], F32)
        gsmat = sb("gsmat", [128, 32], F32)
        rkmat = sb("rkmat", [128, 32], F32)
        cumT = sb("cumT", [16, 4096], F32)
        ohst = sb("ohst", [16, 128])
        ct16 = sb("ct16", [128, 16], F32)
        rk1 = sb("rk1", [128, 1], F32)
        tot = sb("tot", [16, 32], F32)
        totf = sb("totf", [32, 16], F32)
        U32f = sb("U32f", [32, 32], F32)
        baseC = sb("baseC", [16, 32], F32)
        hist_s = sb("hist_s", [16, 1], F32)
        pada = sb("pada", [16, 1], F32)
        io33 = sb("io33", [16, 33], F32)
        cmp33 = sb("cmp33", [16, 33], F32)
        qcol = sb("qcol", [16, 1], F32)
        padc = sb("padc", [16, 1], F32)
        padb = sb("padb", [16, 1])
        segb = sb("segb", [1, 16], F32)
        segc = sb("segc", [16, 1], F32)
        cbmat = sb("cbmat", [16, 32], F32)
        cbrow = sb("cbrow", [1, 16], F32)
        gath = sb("gath", [128, 16], F32)
        slotc = sb("slotc", [128, 1], F32)
        slots_f = sb("slots_f", [128, 32], F32)
        slots_i = sb("slots_i", [128, 32], I32)
        io48 = sb("io48", [16, 48], F32)
        cmp48 = sb("cmp48", [16, 48])
        t2er = sb("t2er", [1, 48], F32)
        t2ei = sb("t2ei", [1, 48], I32)
        gtmp = sb("gtmp", [32, 128], F32)
        dum = sb("dum3", [1, 1], F32)

        psA = pt("psA3", [128, 512])
        psB = pt("psB3", [128, 512])
        psC = pt("psC3", [128, 512])
        psE3 = pt("psE3", [128, 384])
        psY3 = pt("psY3", [64, 128])
        psD3 = pt("psD3", [1, 384])
        psS = pt("psS3", [1, 512])
        psQ = pt("psQ3", [1, 512])
        psAb = psA.bitcast(BF16)
        psYb = psY3.bitcast(BF16)

        for dst, srcd in ((bands, bandsd), (bq_s, bq1), (bk_s, bk1), (bv_s, bv1),
                          (bproj_s, bprojd), (blat_s, blatd),
                          (ewh_s, ewhd), (ewt_s, ewtd), (U128b, U128d),
                          (U16sb, U16sd)):
            p.g(lambda e, k, ks, dst=dst, srcd=srcd: e.dma_start(out=dst[:, :], in_=srcd))
        p.g(lambda e, k, ks: e.dma_start(out=wlat_s[0:64, :], in_=wlatT))
        p.g(lambda e, k, ks: e.dma_start(out=wlat_s[64:128, :], in_=wlatT))
        p.g(lambda e, k, ks: e.dma_start(out=blat_r[:, :], in_=blatd.rearrange("a b -> b a")))
        p.g(lambda e, k, ks: e.dma_start(
            out=wgs[:, :, :].rearrange("p c f -> p (c f)"), in_=wg1))
        for t in (ones128, ones1, ones8, ones16c, onesf1, ones1t):
            p.v(lambda e, k, ks, t=t: e.memset(t[:, :], 1.0))
        p.v(lambda e, k, ks: e.memset(ke[:, :], 0.0))
        p.v(lambda e, k, ks: e.memset(ve[:, :], 0.0))
        p.gc(lambda e, k, ks: e.iota(idt[:, :], pattern=[[-1, 128]], base=0,
                                     channel_multiplier=1,
                                     allow_small_or_imprecise_dtypes=True))
        p.v(lambda e, k, ks: e.tensor_scalar(idt[:, :], idt[:, :], 0.0, None, ALU.is_equal))
        p.v(lambda e, k, ks: e.tensor_copy(idb[:, :], idt[:, :]))
        p.gc(lambda e, k, ks: e.iota(io48[:, :], pattern=[[1, 48]], base=0,
                                     channel_multiplier=0,
                                     allow_small_or_imprecise_dtypes=True))

        for b in range(B_CORE):
            for dc in range(8):
                p.g(lambda e, k, ks, b=b, dc=dc: e.dma_start(
                    out=xb[:, dc, :], in_=x2d[b, ds(dc * 128, 128), :]))

            with p.Loop(8):
                for wsl, wdr in ((wqs, wq1), (wks, wk1), (wvs, wv1)):
                    p.g(lambda e, k, ks, wsl=wsl, wdr=wdr: e.dma_start(
                        out=wsl[:, :, :].rearrange("p c f -> p (c f)"),
                        in_=wdr[ts(k, 128), :]))
                for tc in range(4):
                    for (wsl, dst, ps, bcol) in ((wqs, qp, psA, bq_s), (wks, kp, psB, bk_s),
                                                 (wvs, vp, psC, bv_s)):
                        for dc in range(8):
                            p.t(lambda e, k, ks, wsl=wsl, ps=ps, dc=dc, tc=tc: e.matmul(
                                ps[:, :], wsl[:, dc, :], xb[:, dc, ds(tc * 512, 512)],
                                start=(dc == 0), stop=(dc == 7)))
                        p.v(lambda e, k, ks, dst=dst, ps=ps, tc=tc, bcol=bcol: e.tensor_scalar(
                            dst[:, ds(tc * 512, 512)], ps[:, :], bcol[:, ds(CE(e, k, 1), 1)],
                            None, ALU.add))

                for h in range(2):
                    hr = h * 64
                    for tc in range(4):
                        p.t(lambda e, k, ks, hr=hr, tc=tc: e.matmul(
                            psA[0:64, :], wlat_s[hr:hr + 64, :],
                            kp[hr:hr + 64, ds(tc * 512, 512)], start=True, stop=True))
                        p.v(lambda e, k, ks, hr=hr, tc=tc: e.tensor_scalar(
                            klat[hr:hr + 64, ds(tc * 512, 512)], psA[0:64, :],
                            blat_s[:, 0:1], None, ALU.add))
                    for tt in range(16):
                        p.t(lambda e, k, ks, hr=hr, tt=tt: e.matmul(
                            psC[:, 0:64], vp[hr:hr + 64, ds(tt * 128, 128)],
                            wlat_s[hr:hr + 64, :], start=True, stop=False))
                        p.t(lambda e, k, ks: e.matmul(
                            psC[:, 0:64], ones1t[:, :], blat_r[:, :],
                            start=False, stop=True))
                        p.v(lambda e, k, ks, tt=tt: e.tensor_copy(
                            vlt[:, ds(tt * 64, 64)], psC[:, 0:64]))

                    for qt in range(16):
                        kts = [qt - 1, qt, qt + 1]
                        variant = 0
                        if qt == 0:
                            kts[0] = 15
                            variant = 1
                        if qt == 15:
                            kts[2] = 0
                            variant = 2
                        for pos in range(3):
                            p.t(lambda e, k, ks, hr=hr, qt=qt, kt=kts[pos], pos=pos: e.matmul(
                                psE3[:, ds(pos * 128, 128)],
                                klat[hr:hr + 64, ds(kt * 128, 128)],
                                qp[hr:hr + 64, ds(qt * 128, 128)], start=True, stop=True))
                        p.s(lambda e, k, ks: e.activation(
                            mexp[:, :], psE3[:, :], AF.Exp, scale=0.125))
                        p.v(lambda e, k, ks, variant=variant: e.tensor_tensor(
                            mexp[:, :], mexp[:, :], bands[:, ds(variant * 384, 384)],
                            ALU.mult))
                        p.t(lambda e, k, ks: e.matmul(
                            psD3[:, :], ones128[:, :], mexp[:, :], start=True, stop=True))
                        p.v(lambda e, k, ks: e.tensor_copy(denr[:, :], psD3[:, :]))
                        p.v(lambda e, k, ks: e.tensor_tensor(
                            rden[:, :], denr[0:1, 0:128], denr[0:1, 128:256], ALU.add))
                        p.v(lambda e, k, ks: e.tensor_tensor(
                            rden[:, :], rden[:, :], denr[0:1, 256:384], ALU.add))
                        p.v(lambda e, k, ks: e.reciprocal(rden[:, :], rden[:, :]))
                        p.v(lambda e, k, ks: e.tensor_copy(rdenb[:, :], rden[:, :]))
                        for pos in range(3):
                            p.t(lambda e, k, ks, kt=kts[pos], pos=pos: e.matmul(
                                psY3[:, :], vlt[:, ds(kt * 64, 64)],
                                mexp[:, ds(pos * 128, 128)],
                                start=(pos == 0), stop=(pos == 2)))
                        p.t(lambda e, k, ks: e.matmul(
                            psB[0:64, 0:128], ones1[:, :], rdenb[:, :], start=True, stop=True))
                        p.v(lambda e, k, ks: e.tensor_copy(bcy[:, :], psB[0:64, 0:128]))
                        p.v(lambda e, k, ks, hr=hr, qt=qt: e.tensor_tensor(
                            y_pair[hr:hr + 64, ds(qt * 128, 128)], psY3[:, :],
                            bcy[:, :], ALU.mult))

                    for which in range(2):
                        if which == 0:
                            qoff, ew = 0, ewh_s
                        else:
                            qoff, ew = 2045, ewt_s
                        if which == 0:
                            p.v(lambda e, k, ks, hr=hr: e.tensor_copy(ke[:, 0:6], klat[hr:hr + 64, 0:6]))
                            p.v(lambda e, k, ks, hr=hr: e.tensor_copy(ke[:, 6:7], klat[hr:hr + 64, 2047:2048]))
                            p.g(lambda e, k, ks: e.dma_start(out=ve[0:6, :], in_=vlt[0:6, 0:64]))
                            p.g(lambda e, k, ks: e.dma_start(out=ve[6:7, :], in_=vlt[127:128, ds(15 * 64, 64)]))
                        else:
                            p.v(lambda e, k, ks, hr=hr: e.tensor_copy(ke[:, 0:6], klat[hr:hr + 64, 2042:2048]))
                            p.v(lambda e, k, ks, hr=hr: e.tensor_copy(ke[:, 6:7], klat[hr:hr + 64, 0:1]))
                            p.g(lambda e, k, ks: e.dma_start(out=ve[0:6, :], in_=vlt[106:112, ds(15 * 64, 64)]))
                            p.g(lambda e, k, ks: e.dma_start(out=ve[6:7, :], in_=vlt[0:1, 0:64]))
                        p.v(lambda e, k, ks, hr=hr, qoff=qoff: e.tensor_copy(
                            qe3[:, :], qp[hr:hr + 64, ds(qoff, 3)]))
                        p.t(lambda e, k, ks: e.matmul(
                            psE3[0:8, 0:3], ke[:, :], qe3[:, :],
                            start=True, stop=True))
                        p.s(lambda e, k, ks: e.activation(
                            ee[:, :], psE3[0:8, 0:3], AF.Exp, scale=0.125))
                        p.v(lambda e, k, ks, ew=ew: e.tensor_tensor(ee[:, :], ee[:, :], ew[:, :], ALU.mult))
                        p.v(lambda e, k, ks: e.tensor_copy(web[:, :], ee[:, :]))
                        p.t(lambda e, k, ks: e.matmul(
                            psD3[0:1, 0:3], ones8[:, :], web[:, :], start=True, stop=True))
                        p.v(lambda e, k, ks: e.reciprocal(rde[:, :], psD3[0:1, 0:3]))
                        p.v(lambda e, k, ks: e.tensor_copy(rdeb[:, :], rde[:, :]))
                        p.t(lambda e, k, ks: e.matmul(
                            psY3[0:64, 0:3], ve[:, :], web[:, :], start=True, stop=True))
                        p.t(lambda e, k, ks: e.matmul(
                            psB[0:64, 0:3], ones1[:, :], rdeb[:, :], start=True, stop=True))
                        p.v(lambda e, k, ks: e.tensor_copy(bce[:, :], psB[0:64, 0:3]))
                        p.v(lambda e, k, ks, hr=hr, qoff=qoff: e.tensor_tensor(
                            y_pair[hr:hr + 64, ds(qoff, 3)], psY3[0:64, 0:3],
                            bce[:, :], ALU.mult))

                p.sy(lambda e, k, ks: e.dma_start(
                    out=y_dram[ts(k, 128), :], in_=y_pair[:, :]))

            p.g(lambda e, k, ks: e.dma_start(
                out=wbuf[:, :, :].rearrange("p c f -> p (c f)"), in_=wproj))
            for tc in range(4):
                for dc in range(8):
                    p.g(lambda e, k, ks, dc=dc, tc=tc: e.dma_start(
                        out=ybufs[:, dc, :],
                        in_=y_dram[ds(dc * 128, 128), ds(tc * 512, 512)]))
                for oc in range(8):
                    for dc in range(8):
                        p.t(lambda e, k, ks, dc=dc, oc=oc, tc=tc: e.matmul(
                            psA[:, :], wbuf[:, dc, ds(oc * 128, 128)],
                            ybufs[:, dc, :], start=(dc == 0), stop=(dc == 7)))
                    p.g(lambda e, k, ks, b=b, oc=oc, tc=tc: e.dma_start(
                        out=xres[:, :], in_=x2d[b, ds(oc * 128, 128), ds(tc * 512, 512)]))
                    p.v(lambda e, k, ks, oc=oc: e.tensor_scalar(
                        sprev[:, :], psA[:, :], bproj_s[:, oc:oc + 1], None, ALU.add))
                    p.v(lambda e, k, ks: e.tensor_tensor(
                        sprev[:, :], sprev[:, :], xres[:, :], ALU.add))
                    p.v(lambda e, k, ks, oc=oc: e.tensor_copy(sum_bf[:, oc, :], sprev[:, :]))
                    p.s(lambda e, k, ks: e.activation(sq_bf[:, :], sprev[:, :], AF.Square))
                    p.t(lambda e, k, ks, oc=oc: e.matmul(
                        psS[:, :], ones128[:, :], sum_bf[:, oc, :],
                        start=(oc == 0), stop=(oc == 7)))
                    p.t(lambda e, k, ks, oc=oc: e.matmul(
                        psQ[:, :], ones128[:, :], sq_bf[:, :],
                        start=(oc == 0), stop=(oc == 7)))
                p.v(lambda e, k, ks: e.tensor_scalar(mrow[:, :], psS[:, :], 1.0 / D, None, ALU.mult))
                p.v(lambda e, k, ks: e.tensor_scalar(m2row[:, :], psQ[:, :], 1.0 / D, None, ALU.mult))
                p.v(lambda e, k, ks: e.tensor_tensor(varr[:, :], mrow[:, :], mrow[:, :], ALU.mult))
                p.v(lambda e, k, ks: e.tensor_tensor(varr[:, :], m2row[:, :], varr[:, :], ALU.subtract))
                p.v(lambda e, k, ks: e.tensor_scalar(varr[:, :], varr[:, :], 1e-5, None, ALU.add))
                p.s(lambda e, k, ks: e.activation(varr[:, :], varr[:, :], AF.Sqrt))
                p.v(lambda e, k, ks: e.reciprocal(varr[:, :], varr[:, :]))
                p.v(lambda e, k, ks: e.tensor_copy(mrow_bf[:, :], mrow[:, :]))
                p.v(lambda e, k, ks: e.tensor_copy(rstd_bf[:, :], varr[:, :]))
                p.t(lambda e, k, ks: e.matmul(psB[0:64, :], ones1[:, :], mrow_bf[:, :], start=True, stop=True))
                p.t(lambda e, k, ks: e.matmul(psC[0:64, :], ones1[:, :], rstd_bf[:, :], start=True, stop=True))
                p.v(lambda e, k, ks: e.tensor_copy(mb_sb[0:64, :], psB[0:64, :]))
                p.v(lambda e, k, ks: e.tensor_copy(rs_sb[0:64, :], psC[0:64, :]))
                p.v(lambda e, k, ks: e.tensor_copy(mb_sb[64:128, :], psB[0:64, :]))
                p.v(lambda e, k, ks: e.tensor_copy(rs_sb[64:128, :], psC[0:64, :]))
                for oc in range(8):
                    p.v(lambda e, k, ks, oc=oc: e.tensor_tensor(
                        outf[:, :], sum_bf[:, oc, :], mb_sb[:, :], ALU.subtract))
                    p.v(lambda e, k, ks: e.tensor_tensor(outf[:, :], outf[:, :], rs_sb[:, :], ALU.mult))
                    p.v(lambda e, k, ks, oc=oc, b=b, tc=tc: e.tensor_copy(
                        x3sb[:, oc, ds(b * T + tc * 512, 512)], outf[:, :]))
            for oc in range(8):
                for tt in range(16):
                    p.t(lambda e, k, ks, oc=oc, tt=tt, b=b: e.transpose(
                        psAb[:, 0:128], x3sb[:, oc, ds(b * T + tt * 128, 128)], idb[:, :]))
                    p.v(lambda e, k, ks: e.tensor_copy(xtb[:, :], psAb[:, 0:128]))
                    p.sy(lambda e, k, ks, b=b, oc=oc, tt=tt: e.dma_start(
                        out=x3bf[ds(b * T + tt * 128, 128), ds(oc * 128, 128)],
                        in_=xtb[:, :]))

        for tile in range(32):
            for dc in range(8):
                p.t(lambda e, k, ks, dc=dc, tile=tile: e.matmul(
                    psB[:, 0:16], x3sb[:, dc, ds(tile * 128, 128)], wgs[:, dc, :],
                    start=(dc == 0), stop=(dc == 7)))
            p.v(lambda e, k, ks: e.tensor_reduce(mx1[:, :], psB[:, 0:16], AX.X, ALU.max))
            p.v(lambda e, k, ks: e.tensor_scalar(nmx[:, :], mx1[:, :], -1.0, None, ALU.mult))
            p.s(lambda e, k, ks: e.activation(el[:, :], psB[:, 0:16], AF.Exp, bias=nmx[:, 0:1]))
            p.v(lambda e, k, ks: e.tensor_reduce(den1[:, :], el[:, :], AX.X, ALU.add))
            p.v(lambda e, k, ks: e.reciprocal(den1[:, :], den1[:, :]))
            p.v(lambda e, k, ks, tile=tile: e.tensor_copy(gsmat[:, tile:tile + 1], den1[:, :]))
            p.v(lambda e, k, ks: e.tensor_scalar(oh_tok[:, :], psB[:, 0:16], mx1[:, 0:1], None, ALU.is_ge))
            p.v(lambda e, k, ks, tile=tile: e.tensor_copy(oh_all[:, ds(tile * 16, 16)], oh_tok[:, :]))
            p.t(lambda e, k, ks: e.transpose(psYb[0:16, 0:128], oh_tok[:, :], idb[:, :]))
            p.v(lambda e, k, ks: e.tensor_copy(ohst[:, :], psYb[0:16, 0:128]))
            p.sy(lambda e, k, ks, tile=tile: e.dma_start(
                out=ohT_o[:, ds(tile * 128, 128)], in_=ohst[:, :]))
            p.t(lambda e, k, ks: e.matmul(
                psC[0:16, 0:128], oh_tok[:, :], U128b[:, :], start=True, stop=True))
            p.v(lambda e, k, ks, tile=tile: e.tensor_copy(cumT[:, ds(tile * 128, 128)], psC[0:16, 0:128]))
            p.t(lambda e, k, ks: e.matmul(
                psB[:, 0:16], U128b[:, :], oh_tok[:, :], start=True, stop=True))
            p.v(lambda e, k, ks: e.tensor_tensor(ct16[:, :], psB[:, 0:16], oh_tok[:, :], ALU.mult))
            p.v(lambda e, k, ks, tile=tile: e.tensor_reduce(
                rkmat[:, ds(tile, 1)], ct16[:, :], AX.X, ALU.add))
        p.v(lambda e, k, ks: e.tensor_copy(
            tot[:, :], cumT.tensor_ap()[0:16, 127:4096:128] if False else cumT[:, 127::128]))
        p.g(lambda e, k, ks: e.dma_start(out=U32f[:, :], in_=U32sd))
        p.t(lambda e, k, ks: e.transpose(psB[0:32, 0:16], tot[:, :], idt[0:16, 0:16]))
        p.v(lambda e, k, ks: e.tensor_copy(totf[:, :], psB[0:32, 0:16]))
        p.t(lambda e, k, ks: e.matmul(
            psC[0:16, 0:32], totf[:, :], U32f[:, :], start=True, stop=True))
        p.v(lambda e, k, ks: e.tensor_copy(baseC[:, :], psC[0:16, 0:32]))
        p.v(lambda e, k, ks: e.tensor_tensor(hist_s[:, :], baseC[:, 31:32], tot[:, 31:32], ALU.add))
        p.sy(lambda e, k, ks: e.dma_start(out=hist_o, in_=hist_s[:, :]))
        p.v(lambda e, k, ks: e.tensor_scalar(pada[:, :], hist_s[:, :], 127.0, None, ALU.add))
        p.gc(lambda e, k, ks: e.iota(io33[:, :], pattern=[[1, 33]], base=1,
                                     channel_multiplier=0,
                                     allow_small_or_imprecise_dtypes=True))
        p.v(lambda e, k, ks: e.tensor_scalar(qcol[:, :], pada[:, :], 1.0 / 128.0, None, ALU.mult))
        p.v(lambda e, k, ks: e.tensor_scalar(cmp33[:, :], io33[:, :], qcol[:, 0:1], None, ALU.is_le))
        p.v(lambda e, k, ks: e.tensor_reduce(padc[:, :], cmp33[:, :], AX.X, ALU.add))
        p.v(lambda e, k, ks: e.tensor_scalar(padc[:, :], padc[:, :], 128.0, None, ALU.mult))
        p.v(lambda e, k, ks: e.tensor_copy(padb[:, :], padc[:, :]))
        p.t(lambda e, k, ks: e.matmul(
            psD3[0:1, 0:16], padb[:, :], U16sb[:, :], start=True, stop=True))
        p.v(lambda e, k, ks: e.tensor_copy(segb[:, :], psD3[0:1, 0:16]))
        p.t(lambda e, k, ks: e.transpose(psC[0:16, 0:1], segb[:, :], idt[0:1, 0:1]))
        p.v(lambda e, k, ks: e.tensor_copy(segc[:, :], psC[0:16, 0:1]))
        p.v(lambda e, k, ks: e.tensor_scalar(cbmat[:, :], baseC[:, :], segc[:, 0:1], None, ALU.add))
        p.t(lambda e, k, ks: e.transpose(psB[0:32, 0:16], cbmat[:, :], idt[0:16, 0:16]))
        p.v(lambda e, k, ks: e.tensor_copy(gtmp[:, 0:16], psB[0:32, 0:16]))
        p.sy(lambda e, k, ks: e.dma_start(out=cb_dram, in_=gtmp[:, 0:16]))
        with p.Loop(32):
            p.v(lambda e, k, ks: e.tensor_scalar(
                cumT[:, ts(k, 128)], cumT[:, ts(k, 128)], baseC[:, ds(k, 1)],
                None, ALU.add))
        p.sy(lambda e, k, ks: e.dma_start(out=locT_o, in_=cumT[:, :]))
        p.t(lambda e, k, ks: e.transpose(psC[0:32, 0:128], gsmat[:, :], idt[:, :]))
        p.v(lambda e, k, ks: e.tensor_copy(gtmp[:, :], psC[0:32, 0:128]))
        p.sy(lambda e, k, ks: e.dma_start(out=gates_o, in_=gtmp[:, :]))
        with p.Loop(32):
            p.g(lambda e, k, ks: e.dma_start(out=cbrow[:, :], in_=cb_dram[ts(k, 1), :]))
            p.t(lambda e, k, ks: e.matmul(
                psB[:, 0:16], onesf1[:, :], cbrow[:, :], start=True, stop=True))
            p.v(lambda e, k, ks: e.tensor_tensor(
                gath[:, :], psB[:, 0:16], oh_all[:, ts(k, 16)], ALU.mult))
            p.v(lambda e, k, ks: e.tensor_reduce(slotc[:, :], gath[:, :], AX.X, ALU.add))
            p.v(lambda e, k, ks: e.tensor_tensor(slotc[:, :], slotc[:, :], rkmat[:, ds(k, 1)], ALU.add))
            p.v(lambda e, k, ks: e.tensor_scalar(
                slots_f[:, ds(k, 1)], slotc[:, :], -1.0, None, ALU.add))
        p.v(lambda e, k, ks: e.tensor_copy(slots_i[:, :], slots_f[:, :]))
        p.sy(lambda e, k, ks: e.dma_start(out=slots_o, in_=slots_i[:, :]))
        p.v(lambda e, k, ks: e.tensor_scalar(
            cmp48[:, :], io48[:, :], 128.0, segc[:, 0:1], ALU.mult, op1=ALU.is_ge))
        p.t(lambda e, k, ks: e.matmul(
            psD3[0:1, 0:48], ones16c[:, :], cmp48[:, :], start=True, stop=True))
        p.v(lambda e, k, ks: e.tensor_scalar(t2er[:, :], psD3[0:1, 0:48], -1.0, 128.0, ALU.add, op1=ALU.mult))
        p.v(lambda e, k, ks: e.tensor_copy(t2ei[:, :], t2er[:, :]))
        p.sy(lambda e, k, ks: e.dma_start(out=t2e_o, in_=t2ei[:, :]))

        p.gc(lambda e, k, ks: e.memset(dum[:, :], 0.0))


def build_l1_attn():
    p = Prog()
    nc = p.nc
    x2d = nc.dram_tensor("x2", [B_CORE, D, T], F32, kind="ExternalInput").ap()
    wq1 = nc.dram_tensor("wq1", [D, D], F32, kind="ExternalInput").ap()
    wk1 = nc.dram_tensor("wk1", [D, D], F32, kind="ExternalInput").ap()
    wv1 = nc.dram_tensor("wv1", [D, D], F32, kind="ExternalInput").ap()
    bq1 = nc.dram_tensor("bq1", [128, 8], F32, kind="ExternalInput").ap()
    bk1 = nc.dram_tensor("bk1", [128, 8], F32, kind="ExternalInput").ap()
    bv1 = nc.dram_tensor("bv1", [128, 8], F32, kind="ExternalInput").ap()
    wlatT = nc.dram_tensor("wlatT", [64, 64], F32, kind="ExternalInput").ap()
    blatd = nc.dram_tensor("blat", [64, 1], F32, kind="ExternalInput").ap()
    wproj = nc.dram_tensor("wproj", [128, 8 * D], F32, kind="ExternalInput").ap()
    bprojd = nc.dram_tensor("bproj", [128, 8], F32, kind="ExternalInput").ap()
    wg1 = nc.dram_tensor("wg1", [128, 128], F32, kind="ExternalInput").ap()
    bandsd = nc.dram_tensor("bands", [128, 1152], F32, kind="ExternalInput").ap()
    ewhd = nc.dram_tensor("ewh", [8, 3], F32, kind="ExternalInput").ap()
    ewtd = nc.dram_tensor("ewt", [8, 3], F32, kind="ExternalInput").ap()
    U128d = nc.dram_tensor("U128", [128, 128], F32, kind="ExternalInput").ap()
    U16sd = nc.dram_tensor("U16s", [16, 16], F32, kind="ExternalInput").ap()
    U32sd = nc.dram_tensor("U32s", [32, 32], F32, kind="ExternalInput").ap()

    x3bf = nc.dram_tensor("x3bf", [4096, D], BF16, kind="ExternalOutput").ap()
    hist_o = nc.dram_tensor("hist", [16, 1], F32, kind="ExternalOutput").ap()
    gates_o = nc.dram_tensor("gates", [32, 128], F32, kind="ExternalOutput").ap()
    ohT_o = nc.dram_tensor("ohT", [16, 4096], BF16, kind="ExternalOutput").ap()
    locT_o = nc.dram_tensor("locT", [16, 4096], F32, kind="ExternalOutput").ap()
    slots_o = nc.dram_tensor("slots", [128, 32], I32, kind="ExternalOutput").ap()
    t2e_o = nc.dram_tensor("t2e", [1, 48], I32, kind="ExternalOutput").ap()
    y_dram = nc.dram_tensor("y_dram3", [D, T], F32).ap()
    cb_dram = nc.dram_tensor("cb_dram", [32, 16], F32).ap()

    dr = {"x2": x2d,
          "wq1": wq1,
          "wk1": wk1,
          "wv1": wv1,
          "bq1": bq1,
          "bk1": bk1,
          "bv1": bv1,
          "wlatT": wlatT,
          "blat": blatd,
          "wproj": wproj,
          "bproj": bprojd,
          "wg1": wg1,
          "bands": bandsd,
          "ewh": ewhd,
          "ewt": ewtd,
          "U128": U128d,
          "U16s": U16sd,
          "U32s": U32sd,
          "x3bf": x3bf,
          "hist": hist_o,
          "gates": gates_o,
          "ohT": ohT_o,
          "locT": locT_o,
          "slots": slots_o,
          "t2e": t2e_o}
    emit_l1_attn(p, dr)
    with ExitStack() as _ctx2:
        p.emit(_ctx2)
    return p.nc



def l1_ffn_host_inputs(inputs):
    w1 = inputs["L1_ffn_w1"]
    w2 = inputs["L1_ffn_w2"]
    w1_cr = w1.reshape(16, 8, 128, 4096).transpose(0, 2, 1, 3).reshape(16 * 128, 8 * 4096)
    w2_cr = w2.reshape(16, 32, 128, 1024).transpose(0, 2, 1, 3).reshape(16 * 128, 32 * 1024)
    return {"w1_cr": np.ascontiguousarray(w1_cr.astype(np.float32)),
            "w2_cr": np.ascontiguousarray(w2_cr.astype(np.float32))}


def emit_l1_ffn(p, dr):
    from concourse.bass_core_rust import IndirectOffsetOnAxis
    nc = p.nc
    x3d = dr["x3bf"]
    slotsd = dr["slots"]
    t2ed = dr["t2e"]
    w1d = dr["w1_cr"]
    w2d = dr["w2_cr"]
    yslots = dr["yslots"]
    xg = nc.dram_tensor("xg_dram", [6144, D], BF16).ap()

    with ExitStack() as ctx:
        def sb(name, shape, dt=BF16):
            return ctx.enter_context(nc.sbuf_tensor(name, shape, dt))

        w1s = sb("w1s4", [128, 8, 4096])
        w2s = sb("w2s4", [128, 32, 1024])
        hbuf = sb("hbuf4", [128, 32, 128])
        xstg = sb("xstg", [128, D])
        xgt = sb("xgt", [128, D])
        xf = sb("xf4", [128, 8, 128])
        yfm = sb("yfm", [128, 128])
        ytok = sb("ytok", [128, 8, 128])
        zt = sb("zt4", [128, D])
        idb4 = sb("idb4", [128, 128])
        idf4 = sb("idf4", [128, 128], F32)
        slots_s = ctx.enter_context(nc.sbuf_tensor("slots_s", [128, 32], I32))
        t2es = ctx.enter_context(nc.sbuf_tensor("t2es", [1, 48], I32))
        ecell = ctx.enter_context(nc.sbuf_tensor("ecell", [1, 1], I32))
        dum = sb("dum4", [1, 1], F32)

        psA = ctx.enter_context(nc.psum_tensor("psA4", [128, 512], F32))
        psB = ctx.enter_context(nc.psum_tensor("psB4", [128, 512], F32))
        psAb = psA.bitcast(BF16)

        p.g(lambda e, k, ks: e.dma_start(out=slots_s[:, :], in_=slotsd))
        p.g(lambda e, k, ks: e.dma_start(out=t2es[:, :], in_=t2ed))
        p.v(lambda e, k, ks: e.memset(zt[:, :], 0.0))
        p.gc(lambda e, k, ks: e.iota(idf4[:, :], pattern=[[-1, 128]], base=0,
                                     channel_multiplier=1,
                                     allow_small_or_imprecise_dtypes=True))
        p.v(lambda e, k, ks: e.tensor_scalar(idf4[:, :], idf4[:, :], 0.0, None, ALU.is_equal))
        p.v(lambda e, k, ks: e.tensor_copy(idb4[:, :], idf4[:, :]))
        for j in range(48):
            p.sy(lambda e, k, ks, j=j: e.dma_start(out=xg[ds(j * 128, 128), :], in_=zt[:, :]))
        for tile in range(32):
            p.g(lambda e, k, ks, tile=tile: e.dma_start(
                out=xstg[:, :], in_=x3d[ds(tile * 128, 128), :]))
            p.g(lambda e, k, ks, tile=tile: e.indirect_dma_start(
                xg, IndirectOffsetOnAxis(slots_s[:, tile:tile + 1], 0),
                xstg[:, :], None))
        _esv = [None]

        def _load_w1(e, k, ks):
            _esv[0] = e.value_load(ecell[0:1, 0:1])
            return e.dma_start(
                out=w1s[:, :, :].rearrange("p c f -> p (c f)"),
                in_=w1d[ds(_esv[0], 128), :])

        with p.Loop(48):
            p.v(lambda e, k, ks: e.tensor_copy(ecell[:, :], t2es[0:1, ds(k, 1)]))
            p.g(_load_w1)
            p.g(lambda e, k, ks: e.dma_start(
                out=w2s[:, :, :].rearrange("p c f -> p (c f)"), in_=w2d[ds(_esv[0], 128), :]))
            p.g(lambda e, k, ks: e.dma_start(out=xgt[:, :], in_=xg[ts(k, 128), :]))
            for dc in range(8):
                p.t(lambda e, k, ks, dc=dc: e.transpose(
                    psAb[:, 0:128], xgt[:, ds(dc * 128, 128)], idb4[:, :]))
                p.v(lambda e, k, ks, dc=dc: e.tensor_copy(xf[:, dc, :], psAb[:, 0:128]))
            for fc in range(32):
                for dc in range(8):
                    p.t(lambda e, k, ks, dc=dc, fc=fc: e.matmul(
                        psB[:, 0:128], w1s[:, dc, ds(fc * 128, 128)], xf[:, dc, :],
                        start=(dc == 0), stop=(dc == 7)))
                p.s(lambda e, k, ks, fc=fc: e.activation(
                    hbuf[:, fc, :], psB[:, 0:128], AF.Relu))
            for oc in range(8):
                for cf in range(32):
                    p.t(lambda e, k, ks, cf=cf, oc=oc: e.matmul(
                        psB[:, 0:128], w2s[:, cf, ds(oc * 128, 128)], hbuf[:, cf, :],
                        start=(cf == 0), stop=(cf == 31)))
                p.v(lambda e, k, ks: e.tensor_copy(yfm[:, :], psB[:, 0:128]))
                p.t(lambda e, k, ks: e.transpose(psAb[:, 0:128], yfm[:, :], idb4[:, :]))
                p.v(lambda e, k, ks, oc=oc: e.tensor_copy(ytok[:, oc, :], psAb[:, 0:128]))
            p.sy(lambda e, k, ks: e.dma_start(
                out=yslots[ts(k, 128), :],
                in_=ytok[:, :, :].rearrange("p c f -> p (c f)")))

        p.gc(lambda e, k, ks: e.memset(dum[:, :], 0.0))


def build_l1_ffn():
    from concourse.bass_core_rust import IndirectOffsetOnAxis
    p = Prog()
    nc = p.nc
    x3d = nc.dram_tensor("x3bf", [4096, D], BF16, kind="ExternalInput").ap()
    slotsd = nc.dram_tensor("slots", [128, 32], I32, kind="ExternalInput").ap()
    t2ed = nc.dram_tensor("t2e", [1, 48], I32, kind="ExternalInput").ap()
    w1d = nc.dram_tensor("w1_cr", [16 * 128, 8 * 4096], F32, kind="ExternalInput").ap()
    w2d = nc.dram_tensor("w2_cr", [16 * 128, 32 * 1024], F32, kind="ExternalInput").ap()
    yslots = nc.dram_tensor("yslots", [6144, D], BF16, kind="ExternalOutput").ap()
    xg = nc.dram_tensor("xg_dram", [6144, D], BF16).ap()

    dr = {"x3bf": x3d,
          "slots": slotsd,
          "t2e": t2ed,
          "w1_cr": w1d,
          "w2_cr": w2d,
          "yslots": yslots}
    emit_l1_ffn(p, dr)
    with ExitStack() as _ctx2:
        p.emit(_ctx2)
    return p.nc



def build_finale():
    from concourse.bass_core_rust import IndirectOffsetOnAxis
    p = Prog()
    nc = p.nc
    x3d = nc.dram_tensor("x3bf", [4096, D], BF16, kind="ExternalInput").ap()
    ysd = nc.dram_tensor("yslots", [6144, D], BF16, kind="ExternalInput").ap()
    slotsd = nc.dram_tensor("slots", [128, 32], I32, kind="ExternalInput").ap()
    gatesd = nc.dram_tensor("gates", [32, 128], F32, kind="ExternalInput").ap()
    ohTd = nc.dram_tensor("ohT", [16, 4096], BF16, kind="ExternalInput").ap()
    locTd = nc.dram_tensor("locT", [16, 4096], F32, kind="ExternalInput").ap()
    histsd = nc.dram_tensor("hists", [8, 16], F32, kind="ExternalInput").ap()
    outd = nc.dram_tensor("out", [4096, D], F32, kind="ExternalOutput").ap()
    pid_dram = nc.partition_id_tensor.ap()

    with ExitStack() as ctx:
        def sb(name, shape, dt=F32):
            return ctx.enter_context(nc.sbuf_tensor(name, shape, dt))

        hsb = sb("hsb", [8, 16])
        pidu = ctx.enter_context(nc.sbuf_tensor("pidu5", [1, 1], U32))
        pidf = sb("pidf5", [1, 1])
        onesf8 = sb("onesf8", [1, 8])
        onesf1 = sb("onesf15", [1, 128])
        ones16 = sb("ones16_5", [16, 1], BF16)
        iota8 = sb("iota85", [8, 1])
        pid8 = sb("pid85", [8, 1])
        cmp8 = sb("cmp85", [8, 1])
        bprev = sb("bprev", [16, 1])
        locs = sb("locs5", [16, 4096])
        ohs = ctx.enter_context(nc.sbuf_tensor("ohs5", [16, 4096], BF16))
        valt = sb("valt", [16, 4096])
        prodb = ctx.enter_context(nc.sbuf_tensor("prodb", [16, 4096], BF16))
        vrow = sb("vrow", [1, 4096])
        grow = sb("grow", [1, 4096])
        wrow = sb("wrow", [1, 4096])
        wcols = sb("wcols", [128, 32])
        id1 = sb("id15", [1, 1])
        slots_s = ctx.enter_context(nc.sbuf_tensor("slots_s5", [128, 32], I32))
        ytile = ctx.enter_context(nc.sbuf_tensor("ytile", [128, D], BF16))
        xt5 = ctx.enter_context(nc.sbuf_tensor("xt5", [128, D], BF16))
        sum5 = sb("sum5", [128, D])
        sqs = sb("sqs", [128, D])
        srow = sb("srow", [128, 1])
        ssq = sb("ssq", [128, 1])
        mean5 = sb("mean5", [128, 1])
        nmean = sb("nmean5", [128, 1])
        var5 = sb("var5", [128, 1])
        outt = sb("outt", [128, D])
        dum = sb("dum5", [1, 1])

        psA = ctx.enter_context(nc.psum_tensor("psA5", [128, 512], F32))
        psS = ctx.enter_context(nc.psum_tensor("psS5", [1, 512], F32))

        p.g(lambda e, k, ks: e.dma_start(out=hsb[:, :], in_=histsd))
        p.g(lambda e, k, ks: e.dma_start(out=pidu[:, :], in_=pid_dram))
        p.g(lambda e, k, ks: e.dma_start(out=locs[:, :], in_=locTd))
        p.g(lambda e, k, ks: e.dma_start(out=ohs[:, :], in_=ohTd))
        p.g(lambda e, k, ks: e.dma_start(out=slots_s[:, :], in_=slotsd))
        p.g(lambda e, k, ks: e.dma_start(out=grow[:, :],
                                         in_=gatesd.rearrange("a b -> (a b)")))
        p.v(lambda e, k, ks: e.memset(onesf8[:, :], 1.0))
        p.v(lambda e, k, ks: e.memset(onesf1[:, :], 1.0))
        p.v(lambda e, k, ks: e.memset(ones16[:, :], 1.0))
        p.v(lambda e, k, ks: e.memset(id1[:, :], 1.0))
        p.gc(lambda e, k, ks: e.iota(iota8[:, :], pattern=[[0, 1]], base=0,
                                     channel_multiplier=1,
                                     allow_small_or_imprecise_dtypes=True))
        p.v(lambda e, k, ks: e.tensor_copy(pidf[:, :], pidu[:, :]))
        p.t(lambda e, k, ks: e.matmul(psA[0:8, 0:1], onesf8[:, :], pidf[:, :],
                                      start=True, stop=True))
        p.v(lambda e, k, ks: e.tensor_copy(pid8[:, :], psA[0:8, 0:1]))
        p.v(lambda e, k, ks: e.tensor_tensor(cmp8[:, :], iota8[:, :], pid8[:, :], ALU.is_lt))
        p.t(lambda e, k, ks: e.matmul(psA[0:16, 0:1], hsb[:, :], cmp8[:, :],
                                      start=True, stop=True))
        p.v(lambda e, k, ks: e.tensor_copy(bprev[:, :], psA[0:16, 0:1]))
        p.v(lambda e, k, ks: e.tensor_scalar(valt[:, :], locs[:, :], bprev[:, 0:1], None, ALU.add))
        p.v(lambda e, k, ks: e.tensor_scalar(valt[:, :], valt[:, :], 2048.0, None, ALU.is_le))
        p.v(lambda e, k, ks: e.tensor_tensor(prodb[:, :], ohs[:, :], valt[:, :], ALU.mult))
        with p.Loop(8):
            p.t(lambda e, k, ks: e.matmul(psS[:, :], ones16[:, :],
                                          prodb[:, ts(k, 512)], start=True, stop=True))
            p.v(lambda e, k, ks: e.tensor_copy(vrow[:, ts(k, 512)], psS[:, :]))
        p.v(lambda e, k, ks: e.tensor_tensor(wrow[:, :], vrow[:, :], grow[:, :], ALU.mult))
        for tile in range(32):
            p.t(lambda e, k, ks, tile=tile: e.transpose(
                psA[:, 0:1], wrow[0:1, ds(tile * 128, 128)], id1[:, :]))
            p.v(lambda e, k, ks, tile=tile: e.tensor_copy(wcols[:, tile:tile + 1], psA[:, 0:1]))
        for tile in range(32):
            p.g(lambda e, k, ks, tile=tile: e.indirect_dma_start(
                ytile[:, :], None, ysd, IndirectOffsetOnAxis(slots_s[:, tile:tile + 1], 0)))
            p.g(lambda e, k, ks, tile=tile: e.dma_start(
                out=xt5[:, :], in_=x3d[ds(tile * 128, 128), :]))
            p.v(lambda e, k, ks, tile=tile: e.tensor_scalar(
                sum5[:, :], ytile[:, :], wcols[:, tile:tile + 1], None, ALU.mult))
            p.v(lambda e, k, ks: e.tensor_tensor(sum5[:, :], sum5[:, :], xt5[:, :], ALU.add))
            p.v(lambda e, k, ks: e.tensor_reduce(srow[:, :], sum5[:, :], AX.X, ALU.add))
            p.v(lambda e, k, ks: e.tensor_scalar(mean5[:, :], srow[:, :], 1.0 / D, None, ALU.mult))
            p.s(lambda e, k, ks: e.activation(sqs[:, :], sum5[:, :], AF.Square, accum_out=ssq[:, :]))
            p.v(lambda e, k, ks: e.tensor_scalar(var5[:, :], ssq[:, :], 1.0 / D, None, ALU.mult))
            p.v(lambda e, k, ks: e.tensor_tensor(nmean[:, :], mean5[:, :], mean5[:, :], ALU.mult))
            p.v(lambda e, k, ks: e.tensor_tensor(var5[:, :], var5[:, :], nmean[:, :], ALU.subtract))
            p.v(lambda e, k, ks: e.tensor_scalar(var5[:, :], var5[:, :], 1e-5, None, ALU.add))
            p.s(lambda e, k, ks: e.activation(var5[:, :], var5[:, :], AF.Sqrt))
            p.v(lambda e, k, ks: e.reciprocal(var5[:, :], var5[:, :]))
            p.v(lambda e, k, ks: e.tensor_scalar(nmean[:, :], mean5[:, :], -1.0, None, ALU.mult))
            p.v(lambda e, k, ks: e.tensor_scalar(outt[:, :], sum5[:, :], nmean[:, 0:1], None, ALU.add))
            p.v(lambda e, k, ks: e.tensor_scalar(outt[:, :], outt[:, :], var5[:, 0:1], None, ALU.mult))
            p.sy(lambda e, k, ks, tile=tile: e.dma_start(
                out=outd[ds(tile * 128, 128), :], in_=outt[:, :]))

        p.gc(lambda e, k, ks: e.memset(dum[:, :], 0.0))
        p.emit(ctx)
    return p.nc



_PROGS = {}


def _get_prog(name, builder):
    if name not in _PROGS:
        _PROGS[name] = builder()
    return _PROGS[name]


def kernel5(**inputs):
    cores = list(range(NC))
    x = inputs["L0_attn_x"]

    nc1 = _get_prog("l0attn", build_l0_attn)
    in1 = [l0_attn_host_inputs(x, inputs["L0_attn_w_qkv"], inputs["L0_attn_w_out"],
                               inputs["L0_attn_b_out"], c) for c in cores]
    r1 = run_bass_kernel_spmd(nc1, in1, core_ids=cores).results

    nc2 = _get_prog("l0ffn", build_l0_ffn)
    wm2 = l0_ffn_host_inputs(None, inputs, 0)
    in2 = [{**wm2, "x1": r1[c]["x1"]} for c in cores]
    r2 = run_bass_kernel_spmd(nc2, in2, core_ids=cores).results

    nc3 = _get_prog("l1attn", build_l1_attn)
    wm3 = l1_attn_host_inputs(inputs)
    in3 = [{**wm3, "x2": r2[c]["x2"]} for c in cores]
    r3 = run_bass_kernel_spmd(nc3, in3, core_ids=cores).results

    nc4 = _get_prog("l1ffn", build_l1_ffn)
    wm4 = l1_ffn_host_inputs(inputs)
    in4 = [{**wm4, "x3bf": r3[c]["x3bf"], "slots": r3[c]["slots"],
            "t2e": r3[c]["t2e"]} for c in cores]
    r4 = run_bass_kernel_spmd(nc4, in4, core_ids=cores).results

    hists = np.ascontiguousarray(
        np.stack([r3[c]["hist"][:, 0] for c in cores], 0)).astype(np.float32)
    nc5 = _get_prog("finale", build_finale)
    in5 = [{"x3bf": r3[c]["x3bf"], "yslots": r4[c]["yslots"],
            "slots": r3[c]["slots"], "gates": r3[c]["gates"],
            "ohT": r3[c]["ohT"], "locT": r3[c]["locT"], "hists": hists}
           for c in cores]
    r5 = run_bass_kernel_spmd(nc5, in5, core_ids=cores).results

    out = np.concatenate([r5[c]["out"].reshape(2, T, D) for c in cores], axis=0)
    return out.astype(np.float32)



def build_mega():
    p = Prog()
    nc = p.nc
    dr = {}

    def ein(name, shape, dt=F32):
        dr[name] = nc.dram_tensor(name, shape, dt, kind="ExternalInput").ap()

    def eout(name, shape, dt=F32):
        dr[name] = nc.dram_tensor(name, shape, dt, kind="ExternalOutput").ap()

    def eint(name, shape, dt=F32):
        dr[name] = nc.dram_tensor(name, shape, dt).ap()

    ein("xT", [B_CORE, D, T]); ein("wq_ps", [D, D]); ein("wk_ps", [D, D])
    ein("wv_ps", [D, D]); ein("wo_cr", [128, 8 * D]); ein("bout", [128, 8])
    ein("cs_tab", [128, T]); ein("sc_tab", [128, T]); ein("cmask", [128, 2048])
    eint("x1", [B_CORE, D, T])
    ein("wg_cr", [17 * 128, 8192]); ein("wu_cr", [17 * 128, 8192])
    ein("wd_cr", [17 * 128, 8192]); ein("gw_cr", [128, 128]); ein("ebias", [128, 16])
    eint("x2", [B_CORE, D, T])
    ein("wq1", [D, D]); ein("wk1", [D, D]); ein("wv1", [D, D])
    ein("bq1", [128, 8]); ein("bk1", [128, 8]); ein("bv1", [128, 8])
    ein("wlatT", [64, 64]); ein("blat", [64, 1]); ein("wproj", [128, 8 * D])
    ein("bproj", [128, 8]); ein("wg1", [128, 128]); ein("bands", [128, 1152])
    ein("ewh", [8, 3]); ein("ewt", [8, 3]); ein("U128", [128, 128])
    ein("U16s", [16, 16]); ein("U32s", [32, 32])
    eout("x3bf", [4096, D], BF16); eout("hist", [16, 1]); eout("gates", [32, 128])
    eout("ohT", [16, 4096], BF16); eout("locT", [16, 4096])
    eout("slots", [128, 32], I32); eout("t2e", [1, 48], I32)
    ein("w1_cr", [16 * 128, 8 * 4096]); ein("w2_cr", [16 * 128, 32 * 1024])
    eout("yslots", [6144, D], BF16)

    emit_l0_attn(p, dr)
    emit_l0_ffn(p, dr)
    emit_l1_attn(p, dr)
    emit_l1_ffn(p, dr)
    with ExitStack() as ctx:
        p.emit(ctx)
    return p.nc


def kernel(**inputs):
    cores = list(range(NC))
    x = inputs["L0_attn_x"]
    ncm = _get_prog("mega", build_mega)
    wm2 = l0_ffn_host_inputs(None, inputs, 0)
    wm3 = l1_attn_host_inputs(inputs)
    wm4 = l1_ffn_host_inputs(inputs)
    in_m = []
    for c in cores:
        m = dict(l0_attn_host_inputs(x, inputs["L0_attn_w_qkv"], inputs["L0_attn_w_out"],
                                     inputs["L0_attn_b_out"], c))
        m.update(wm2); m.update(wm3); m.update(wm4)
        in_m.append(m)
    rm = run_bass_kernel_spmd(ncm, in_m, core_ids=cores).results
    hists = np.ascontiguousarray(
        np.stack([rm[c]["hist"][:, 0] for c in cores], 0)).astype(np.float32)
    nc5 = _get_prog("finale", build_finale)
    in5 = [{"x3bf": rm[c]["x3bf"], "yslots": rm[c]["yslots"],
            "slots": rm[c]["slots"], "gates": rm[c]["gates"],
            "ohT": rm[c]["ohT"], "locT": rm[c]["locT"], "hists": hists}
           for c in cores]
    r5 = run_bass_kernel_spmd(nc5, in5, core_ids=cores).results
    out = np.concatenate([r5[c]["out"].reshape(2, T, D) for c in cores], axis=0)
    return out.astype(np.float32)


# version 04
# speedup vs baseline: 1.3765x; 1.3765x over previous
import numpy as np
import ml_dtypes
import concourse.bass as bass
import concourse.mybir as mybir
from concourse.bass_utils import run_bass_kernel_spmd
from contextlib import ExitStack, contextmanager

F32 = mybir.dt.float32
BF16 = mybir.dt.bfloat16
I32 = mybir.dt.int32
U32 = mybir.dt.uint32
AF = mybir.ActivationFunctionType
ALU = mybir.AluOpType
AX = mybir.AxisListType
ts = bass.ts
ds = bass.ds

B_CORE = 2
T = 2048
D = 1024
NH = 16
DH = 64
NC = 8

ENGINES = ("g", "sy", "t", "v", "s")
_ENG_INC = {"g": 16, "sy": 16, "t": 1, "v": 1, "s": 1}
_EXPR_CACHE = {}



_TPB_DONE = {}


def tpb_prep(e):
    if id(e) not in _TPB_DONE:
        r = e.alloc_register64("tpbb")
        e.tpb_base_ld(r)
        _TPB_DONE[id(e)] = r
    return e.nop()


def CE(e, kv, mult, off=0):
    key = (id(e), id(kv), mult, off)
    sv = _EXPR_CACHE.get(key)
    if sv is None:
        sv = e.compute_val(kv * mult + off)
        _EXPR_CACHE[key] = sv
    return sv


class _Op:
    __slots__ = ("eng", "fn", "inc", "pos")

    def __init__(self, eng, fn, inc, pos):
        self.eng, self.fn, self.inc, self.pos = eng, fn, inc, pos


class _LoopRec:
    __slots__ = ("n", "items", "per", "pos")

    def __init__(self, n, items, per, pos):
        self.n, self.items, self.per, self.pos = n, items, per, pos


class Prog:
    def __init__(self):
        self.nc = bass.Bass("TRN2", target_bir_lowering=False, debug=False)
        self.root = []
        self._stack = [(self.root, {e: 0 for e in ENGINES})]

    def _rec(self, ename, fn, inc):
        items, cnt = self._stack[-1]
        items.append(_Op(ename, fn, inc, dict(cnt)))
        cnt[ename] += inc

    def g(self, fn):
        self._rec("g", fn, 16)

    def gc(self, fn):
        self._rec("g", fn, 1)

    def sy(self, fn):
        self._rec("sy", fn, 16)

    def t(self, fn):
        self._rec("t", fn, 1)

    def v(self, fn):
        self._rec("v", fn, 1)

    def s(self, fn):
        self._rec("s", fn, 1)

    @contextmanager
    def Loop(self, n_iters):
        assert n_iters >= 1
        items = []
        cnt = {e: 0 for e in ENGINES}
        self._stack.append((items, cnt))
        yield
        self._stack.pop()
        parent_items, parent_cnt = self._stack[-1]
        rec = _LoopRec(n_iters, items, dict(cnt), dict(parent_cnt))
        parent_items.append(rec)
        for e in ENGINES:
            parent_cnt[e] += n_iters * cnt[e]

    def emit(self, ctx):
        nc = self.nc
        self._final = dict(self._stack[0][1])
        sems = {e: ctx.enter_context(nc.semaphore(f"sem_{e}")) for e in ENGINES}
        block = ctx.enter_context(nc.Block())

        def eng_has_ops(items, ename):
            for it in items:
                if isinstance(it, _Op):
                    if it.eng == ename:
                        return True
                else:
                    if it.per[ename]:
                        return True
            return False

        def emit_stream(eng, ename):
            wscratch = eng.alloc_register("wscr")
            wscr2 = eng.alloc_register("wscr2")
            last_static = {e: 0 for e in ENGINES}
            pending = []

            def emit_wait(e, static_tgt, kstack_pers):
                terms = [(kv, per[e]) for kv, per in kstack_pers if per[e]]
                if not terms:
                    if static_tgt > last_static[e]:
                        eng.wait_ge(sems[e], static_tgt)
                        last_static[e] = static_tgt
                    return
                kv0, per0 = terms[0]
                eng.reg_mul(wscr2, kv0, per0)
                for kvx, perx in terms[1:]:
                    eng.reg_mul(wscratch, kvx, perx)
                    eng.reg_alu(wscr2, wscr2, wscratch, ALU.add)
                eng.reg_add(wscr2, wscr2, static_tgt)
                eng.wait_ge(sems[e], wscr2)

            def flush_pending():
                while pending:
                    d = pending.pop(0)
                    for e, tgt in d.items():
                        if tgt > last_static[e]:
                            eng.wait_ge(sems[e], tgt)
                            last_static[e] = tgt

            RELAXED = ename in ("t",)

            def walk(items, base, kstack, prev):
                for it in items:
                    if isinstance(it, _Op):
                        if it.eng == ename:
                            flush_pending()
                            if RELAXED:
                                if prev[3] is not None:
                                    emit_wait(prev[3], prev[4], prev[5])
                            else:
                                if prev[0] is not None:
                                    emit_wait(prev[0], prev[1], prev[2])
                            kv = kstack[-1][0] if kstack else None
                            r = it.fn(eng, kv, [kk for kk, _ in kstack])
                            r.then_inc(sems[ename], it.inc)
                        prev[0] = it.eng
                        prev[1] = base[it.eng] + it.pos[it.eng] + it.inc
                        prev[2] = list(kstack)
                        if it.eng != ename:
                            prev[3] = it.eng
                            prev[4] = base[it.eng] + it.pos[it.eng] + it.inc
                            prev[5] = list(kstack)
                    else:
                        if it.n == 0:
                            continue
                        assert not kstack, "loops only at root level"
                        new_base = {e: base[e] + it.pos[e] for e in ENGINES}
                        if eng_has_ops(it.items, ename):
                            flush_pending()
                            if RELAXED:
                                if prev[3] is not None:
                                    emit_wait(prev[3], prev[4], prev[5])
                            elif prev[0] is not None:
                                emit_wait(prev[0], prev[1], prev[2])
                            _EXPR_CACHE.clear()
                            _loop_regs = []
                            _cb = nc.register_alloc_callback(
                                lambda *a: _loop_regs.extend(
                                    x for x in a if type(x).__name__ == "RegisterHandle"))
                            with eng.Fori(0, it.n) as kv:
                                kst = kstack + [(kv, it.per)]
                                for e in ENGINES:
                                    if not it.per[e]:
                                        continue
                                    eng.reg_mul(wscr2, kv, it.per[e])
                                    eng.reg_add(wscr2, wscr2, new_base[e])
                                    eng.wait_ge(sems[e], wscr2)
                                walk(it.items, new_base, kst, prev)
                            nc.unregister_alloc_callback(_cb)
                            for e in ENGINES:
                                last_static[e] = -10 ** 9
                            _EXPR_CACHE.clear()
                            try:
                                h = eng.lower_val(kv)
                                nc.free_register(h)
                            except Exception:
                                pass
                            for h in _loop_regs:
                                try:
                                    nc.free_register(h)
                                except Exception:
                                    pass
                        pending.append(
                            {e: new_base[e] + it.n * it.per[e]
                             for e in ENGINES if it.per[e]})
                        prev[0] = None
                        prev[1] = 0
                        prev[2] = []
                        prev[3] = None
                        prev[4] = 0
                        prev[5] = []

            walk(self.root, {e: 0 for e in ENGINES}, [], [None, 0, [], None, 0, []])
            flush_pending()
            for e in ENGINES:
                if e != ename and self._final[e] > last_static[e]:
                    eng.wait_ge(sems[e], self._final[e])

        @block.gpsimd
        def _(g):
            emit_stream(g, "g")

        @block.sync
        def _(sy):
            emit_stream(sy, "sy")

        @block.tensor
        def _(t):
            emit_stream(t, "t")

        @block.vector
        def _(v):
            emit_stream(v, "v")

        @block.scalar
        def _(s):
            emit_stream(s, "s")



def _rope_tables():
    i = np.arange(DH // 2, dtype=np.float32)
    theta = (10000.0 ** (-2.0 * (i - 1.0) / DH)).astype(np.float32)
    m = np.arange(T, dtype=np.float32)
    ang = (m[:, None] * theta[None, :]).astype(np.float32)
    c = np.ascontiguousarray(np.cos(ang).astype(np.float32).T)
    s = np.ascontiguousarray(np.sin(ang).astype(np.float32).T)
    cs = np.concatenate([c, s, c, s], axis=0)
    sc = np.concatenate([s, c, s, c], axis=0)
    return np.ascontiguousarray(cs), np.ascontiguousarray(sc)


def _causal_masks():
    kk = np.arange(128)[:, None]
    qq = np.arange(512)[None, :]
    masks = [(kk <= qq - 128 * j).astype(np.float32) for j in range(4)]
    return np.ascontiguousarray(np.concatenate(masks, axis=1))


def _pair_sliced(wT):
    w4 = wT.reshape(8, 128, 8, 128)
    out = w4.transpose(2, 1, 0, 3).reshape(1024, 1024)
    return np.ascontiguousarray(out)


def _chunk_rows(wT):
    n = wT.shape[1]
    w3 = wT.reshape(8, 128, n).transpose(1, 0, 2).reshape(128, 8 * n)
    return np.ascontiguousarray(out_arr(w3))


def out_arr(a):
    return np.ascontiguousarray(a)


def l0_attn_host_inputs(x_full, w_qkv, w_out, b_out, core):
    xT = np.ascontiguousarray(
        x_full[2 * core:2 * core + 2].transpose(0, 2, 1)).astype(np.float32)
    cs, sc = _rope_tables()
    return {
        "xT": xT,
        "wq_ps": _pair_sliced(np.ascontiguousarray(w_qkv[0:D].T)),
        "wk_ps": _pair_sliced(np.ascontiguousarray(w_qkv[D:2 * D].T)),
        "wv_ps": _pair_sliced(np.ascontiguousarray(w_qkv[2 * D:3 * D].T)),
        "wo_cr": _chunk_rows(np.ascontiguousarray(w_out.T)),
        "bout": np.ascontiguousarray(b_out.reshape(8, 128).T),
        "cs_tab": cs, "sc_tab": sc, "cmask": _causal_masks(),
    }



def emit_l0_attn(p, dr):
    nc = p.nc
    xT = dr["xT"]
    wq_ps = dr["wq_ps"]
    wk_ps = dr["wk_ps"]
    wv_ps = dr["wv_ps"]
    wo_cr = dr["wo_cr"]
    boutd = dr["bout"]
    cs_t = dr["cs_tab"]
    sc_t = dr["sc_tab"]
    cmaskd = dr["cmask"]
    x1 = dr["x1"]
    y_dram = nc.dram_tensor("y_dram", [D, T], F32).ap()

    with ExitStack() as ctx:
        def sb(name, shape, dt=BF16):
            return ctx.enter_context(nc.sbuf_tensor(name, shape, dt))

        def pt(name, shape):
            return ctx.enter_context(nc.psum_tensor(name, shape, F32))

        xb = sb("xb", [128, 8, T])
        wqs = sb("wqs", [128, 8, 128])
        wks = sb("wks", [128, 8, 128])
        wvs = sb("wvs", [128, 8, 128])
        wbuf = sb("wbuf", [128, 8, D])
        qp = sb("qp", [128, T]); kp = sb("kp", [128, T])
        qrp = sb("qrp", [128, T]); krp = sb("krp", [128, T])
        m1t = sb("m1t", [128, T]); m2t = sb("m2t", [128, T])
        rtmp = sb("rtmp", [128, T])
        vpair = sb("vpair", [128, T])
        ybufs = sb("ybufs", [128, 8, 512])
        cs_b = sb("cs_b", [128, T]); sc_b = sb("sc_b", [128, T])
        cm_b = sb("cm_b", [128, 2048])
        ones128 = sb("ones128", [128, 1])
        ones1 = sb("ones1", [1, 64])
        exp_s = sb("exp_s", [128, 512])
        exp_n = sb("exp_n", [128, 512])
        ysc = sb("ysc", [64, 512], F32)
        ypair1 = sb("ypair1", [128, T], F32)
        rden_bf = sb("rden_bf", [1, 512])
        bc_sb = sb("bc_sb", [64, 512])
        sum_bf = sb("sum_bf", [128, 8, 512])
        sq_bf = sb("sq_bf", [128, 512])
        bout_s = sb("bout_s", [128, 8], F32)
        xres = sb("xres", [128, 512], F32)
        sprev = sb("sprev", [128, 512], F32)
        outf = sb("outf", [128, 512], F32)
        rden_f = sb("rden_f", [1, 512], F32)
        mrow = sb("mrow", [1, 512], F32)
        m2row = sb("m2row", [1, 512], F32)
        varr = sb("varr", [1, 512], F32)
        mrow_bf = sb("mrow_bf", [1, 512])
        rstd_bf = sb("rstd_bf", [1, 512])
        mb_sb = sb("mb_sb", [128, 512], F32)
        rs_sb = sb("rs_sb", [128, 512], F32)
        dum = sb("dum", [1, 1], F32)

        psA = pt("psA", [128, 512])
        psB = pt("psB", [128, 512])
        psC = pt("psC", [128, 512])
        psD = pt("psD", [1, 512])
        psE = pt("psE", [1, 512])
        psY = pt("psY", [64, 512])

        p.g(lambda e, k, ks: e.dma_start(out=cs_b[:, :], in_=cs_t))
        p.g(lambda e, k, ks: e.dma_start(out=sc_b[:, :], in_=sc_t))
        p.g(lambda e, k, ks: e.dma_start(out=cm_b[:, :], in_=cmaskd))
        p.g(lambda e, k, ks: e.dma_start(out=bout_s[:, :], in_=boutd))
        p.v(lambda e, k, ks: e.memset(ones128[:, :], 1.0))
        p.v(lambda e, k, ks: e.memset(ones1[:, :], 1.0))

        for b in range(B_CORE):
            for dc in range(8):
                p.g(lambda e, k, ks, b=b, dc=dc: e.dma_start(
                    out=xb[:, dc, :], in_=xT[b, ds(dc * 128, 128), :]))

            with p.Loop(8):
                for wsl, wdr in ((wqs, wq_ps), (wks, wk_ps), (wvs, wv_ps)):
                    p.g(lambda e, k, ks, wsl=wsl, wdr=wdr: e.dma_start(
                        out=wsl[:, :, :].rearrange("p c f -> p (c f)"),
                        in_=wdr[ts(k, 128), :]))
                for tc in range(4):
                    for (wsl, dst, ps) in ((wqs, qp, psA), (wks, kp, psB)):
                        for dc in range(8):
                            p.t(lambda e, k, ks, wsl=wsl, ps=ps, dc=dc, tc=tc: e.matmul(
                                ps[:, :], wsl[:, dc, :], xb[:, dc, ds(tc * 512, 512)],
                                start=(dc == 0), stop=(dc == 7)))
                        p.v(lambda e, k, ks, dst=dst, ps=ps, tc=tc: e.tensor_copy(
                            dst[:, ds(tc * 512, 512)], ps[:, :]))
                for tt in range(16):
                    for dc in range(8):
                        p.t(lambda e, k, ks, dc=dc, tt=tt: e.matmul(
                            psC[:, 0:128], xb[:, dc, ds(tt * 128, 128)], wvs[:, dc, :],
                            start=(dc == 0), stop=(dc == 7)))
                    p.v(lambda e, k, ks, tt=tt: e.tensor_copy(
                        vpair[:, ds(tt * 128, 128)], psC[:, 0:128]))
                for (src, dst) in ((qp, qrp), (kp, krp)):
                    p.v(lambda e, k, ks, src=src: e.tensor_tensor(
                        m1t[:, :], src[:, :], cs_b[:, :], ALU.mult))
                    p.v(lambda e, k, ks, src=src: e.tensor_tensor(
                        m2t[:, :], src[:, :], sc_b[:, :], ALU.mult))
                    for h in range(2):
                        b0 = h * 64
                        p.v(lambda e, k, ks, b0=b0: e.tensor_copy(
                            rtmp[b0:b0 + 32, :], m1t[b0 + 32:b0 + 64, :]))
                        p.v(lambda e, k, ks, dst=dst, b0=b0: e.tensor_tensor(
                            dst[b0:b0 + 32, :], m1t[b0:b0 + 32, :],
                            rtmp[b0:b0 + 32, :], ALU.add))
                        p.v(lambda e, k, ks, b0=b0: e.tensor_copy(
                            rtmp[b0 + 32:b0 + 64, :], m2t[b0:b0 + 32, :]))
                        p.v(lambda e, k, ks, dst=dst, b0=b0: e.tensor_tensor(
                            dst[b0 + 32:b0 + 64, :], m2t[b0 + 32:b0 + 64, :],
                            rtmp[b0 + 32:b0 + 64, :], ALU.subtract))

                for h in range(2):
                    hr = h * 64
                    for qc in range(4):
                        for kc in range(16):
                            p.t(lambda e, k, ks, hr=hr, qc=qc, kc=kc: e.matmul(
                                psA[:, :], kp[hr:hr + 64, ds(kc * 128, 128)],
                                qp[hr:hr + 64, ds(qc * 512, 512)], start=True, stop=True))
                            p.s(lambda e, k, ks: e.activation(
                                exp_s[:, :], psA[:, :], AF.Exp, scale=0.125))
                            p.t(lambda e, k, ks, kc=kc: e.matmul(
                                psD[:, :], ones128[:, :], exp_s[:, :],
                                start=(kc == 0), stop=(kc == 15)))
                            if kc <= 4 * qc + 3:
                                p.t(lambda e, k, ks, hr=hr, qc=qc, kc=kc: e.matmul(
                                    psA[:, :], krp[hr:hr + 64, ds(kc * 128, 128)],
                                    qrp[hr:hr + 64, ds(qc * 512, 512)],
                                    start=True, stop=True))
                                p.s(lambda e, k, ks: e.activation(
                                    exp_n[:, :], psA[:, :], AF.Exp, scale=0.125))
                                if kc >= 4 * qc:
                                    j = kc - 4 * qc
                                    p.v(lambda e, k, ks, j=j: e.tensor_tensor(
                                        exp_n[:, :], exp_n[:, :],
                                        cm_b[:, ds(j * 512, 512)], ALU.mult))
                                p.t(lambda e, k, ks, h=h, kc=kc: e.matmul(
                                    psY[:, :], vpair[:, ds(kc * 128 + h * 64, 64)],
                                    exp_n[:, :], start=(kc == 0), stop=False))
                        p.v(lambda e, k, ks: e.reciprocal(rden_f[:, :], psD[:, :]))
                        p.v(lambda e, k, ks: e.tensor_copy(rden_bf[:, :], rden_f[:, :]))
                        p.t(lambda e, k, ks: e.matmul(
                            psB[0:64, :], ones1[:, :], rden_bf[:, :], start=True, stop=True))
                        p.v(lambda e, k, ks: e.tensor_copy(bc_sb[:, :], psB[0:64, :]))
                        p.v(lambda e, k, ks, h=h, qc=qc: e.tensor_tensor(
                            ypair1[h * 64:h * 64 + 64, ds(qc * 512, 512)],
                            psY[:, :], bc_sb[:, :], ALU.mult))
                p.sy(lambda e, k, ks: e.dma_start(
                    out=y_dram[ts(k, 128), :], in_=ypair1[:, :]))

            p.g(lambda e, k, ks: e.dma_start(
                out=wbuf[:, :, :].rearrange("p c f -> p (c f)"), in_=wo_cr))
            for tc in range(4):
                for dc in range(8):
                    p.g(lambda e, k, ks, dc=dc, tc=tc: e.dma_start(
                        out=ybufs[:, dc, :],
                        in_=y_dram[ds(dc * 128, 128), ds(tc * 512, 512)]))
                for oc in range(8):
                    for dc in range(8):
                        p.t(lambda e, k, ks, dc=dc, oc=oc, tc=tc: e.matmul(
                            psA[:, :], wbuf[:, dc, ds(oc * 128, 128)],
                            ybufs[:, dc, :],
                            start=(dc == 0), stop=(dc == 7)))
                    p.g(lambda e, k, ks, b=b, oc=oc, tc=tc: e.dma_start(
                        out=xres[:, :], in_=xT[b, ds(oc * 128, 128), ds(tc * 512, 512)]))
                    p.v(lambda e, k, ks, oc=oc: e.tensor_scalar(
                        sprev[:, :], psA[:, :], bout_s[:, oc:oc + 1], None, ALU.add))
                    p.v(lambda e, k, ks: e.tensor_tensor(
                        sprev[:, :], sprev[:, :], xres[:, :], ALU.add))
                    p.v(lambda e, k, ks, oc=oc: e.tensor_copy(sum_bf[:, oc, :], sprev[:, :]))
                    p.s(lambda e, k, ks: e.activation(sq_bf[:, :], sprev[:, :], AF.Square))
                    p.t(lambda e, k, ks, oc=oc: e.matmul(
                        psD[:, :], ones128[:, :], sum_bf[:, oc, :],
                        start=(oc == 0), stop=(oc == 7)))
                    p.t(lambda e, k, ks, oc=oc: e.matmul(
                        psE[:, :], ones128[:, :], sq_bf[:, :],
                        start=(oc == 0), stop=(oc == 7)))
                p.v(lambda e, k, ks: e.tensor_scalar(
                    mrow[:, :], psD[:, :], 1.0 / D, None, ALU.mult))
                p.v(lambda e, k, ks: e.tensor_scalar(
                    m2row[:, :], psE[:, :], 1.0 / D, None, ALU.mult))
                p.v(lambda e, k, ks: e.tensor_tensor(
                    varr[:, :], mrow[:, :], mrow[:, :], ALU.mult))
                p.v(lambda e, k, ks: e.tensor_tensor(
                    varr[:, :], m2row[:, :], varr[:, :], ALU.subtract))
                p.v(lambda e, k, ks: e.tensor_scalar(varr[:, :], varr[:, :], 1e-5, None, ALU.add))
                p.s(lambda e, k, ks: e.activation(varr[:, :], varr[:, :], AF.Sqrt))
                p.v(lambda e, k, ks: e.reciprocal(varr[:, :], varr[:, :]))
                p.v(lambda e, k, ks: e.tensor_copy(mrow_bf[:, :], mrow[:, :]))
                p.v(lambda e, k, ks: e.tensor_copy(rstd_bf[:, :], varr[:, :]))
                p.t(lambda e, k, ks: e.matmul(
                    psB[0:64, :], ones1[:, :], mrow_bf[:, :], start=True, stop=True))
                p.t(lambda e, k, ks: e.matmul(
                    psC[0:64, :], ones1[:, :], rstd_bf[:, :], start=True, stop=True))
                p.v(lambda e, k, ks: e.tensor_copy(mb_sb[0:64, :], psB[0:64, :]))
                p.v(lambda e, k, ks: e.tensor_copy(rs_sb[0:64, :], psC[0:64, :]))
                p.v(lambda e, k, ks: e.tensor_copy(mb_sb[64:128, :], psB[0:64, :]))
                p.v(lambda e, k, ks: e.tensor_copy(rs_sb[64:128, :], psC[0:64, :]))
                for oc in range(8):
                    p.v(lambda e, k, ks, oc=oc: e.tensor_tensor(
                        outf[:, :], sum_bf[:, oc, :], mb_sb[:, :], ALU.subtract))
                    p.v(lambda e, k, ks: e.tensor_tensor(
                        outf[:, :], outf[:, :], rs_sb[:, :], ALU.mult))
                    p.sy(lambda e, k, ks, b=b, oc=oc, tc=tc: e.dma_start(
                        out=x1[b, ds(oc * 128, 128), ds(tc * 512, 512)], in_=outf[:, :]))

        p.gc(lambda e, k, ks: e.memset(dum[:, :], 0.0))



def build_l0_attn():
    p = Prog()
    nc = p.nc
    xT = nc.dram_tensor("xT", [B_CORE, D, T], F32, kind="ExternalInput").ap()
    wq_ps = nc.dram_tensor("wq_ps", [D, D], F32, kind="ExternalInput").ap()
    wk_ps = nc.dram_tensor("wk_ps", [D, D], F32, kind="ExternalInput").ap()
    wv_ps = nc.dram_tensor("wv_ps", [D, D], F32, kind="ExternalInput").ap()
    wo_cr = nc.dram_tensor("wo_cr", [128, 8 * D], F32, kind="ExternalInput").ap()
    boutd = nc.dram_tensor("bout", [128, 8], F32, kind="ExternalInput").ap()
    cs_t = nc.dram_tensor("cs_tab", [128, T], F32, kind="ExternalInput").ap()
    sc_t = nc.dram_tensor("sc_tab", [128, T], F32, kind="ExternalInput").ap()
    cmaskd = nc.dram_tensor("cmask", [128, 2048], F32, kind="ExternalInput").ap()
    x1 = nc.dram_tensor("x1", [B_CORE, D, T], F32, kind="ExternalOutput").ap()
    y_dram = nc.dram_tensor("y_dram", [D, T], F32).ap()

    dr = {"xT": xT,
          "wq_ps": wq_ps,
          "wk_ps": wk_ps,
          "wv_ps": wv_ps,
          "wo_cr": wo_cr,
          "bout": boutd,
          "cs_tab": cs_t,
          "sc_tab": sc_t,
          "cmask": cmaskd,
          "x1": x1}
    emit_l0_attn(p, dr)
    with ExitStack() as _ctx2:
        p.emit(_ctx2)
    return p.nc



def l0_ffn_host_inputs(x1_unused, inputs, core):
    wg = inputs["L0_ffn_w_gate"]; wu = inputs["L0_ffn_w_up"]; wd = inputs["L0_ffn_w_down"]
    sg = inputs["L0_ffn_shared_gate_w"]; su = inputs["L0_ffn_shared_up_w"]
    sd = inputs["L0_ffn_shared_down_w"]
    def cr(mat):
        return mat.reshape(8, 128, 1024).transpose(1, 0, 2).reshape(128, 8192)
    wg_cr = np.stack([cr(np.ascontiguousarray(wg[e].T)) for e in range(16)]
                     + [cr(np.ascontiguousarray(sg.T))], 0).reshape(17 * 128, 8192)
    wu_cr = np.stack([cr(np.ascontiguousarray(wu[e].T)) for e in range(16)]
                     + [cr(np.ascontiguousarray(su.T))], 0).reshape(17 * 128, 8192)
    wd_cr = np.stack([cr(np.ascontiguousarray(wd[e].T)) for e in range(16)]
                     + [cr(np.ascontiguousarray(sd.T))], 0).reshape(17 * 128, 8192)
    gwT = np.ascontiguousarray(inputs["L0_ffn_gate_w"].T)
    gw_cr = gwT.reshape(8, 128, 16).transpose(1, 0, 2).reshape(128, 128)
    ebias = np.ascontiguousarray(
        np.broadcast_to(inputs["L0_ffn_expert_bias"][None, :], (128, 16)).copy())
    return {"wg_cr": np.ascontiguousarray(wg_cr.astype(np.float32)),
            "wu_cr": np.ascontiguousarray(wu_cr.astype(np.float32)),
            "wd_cr": np.ascontiguousarray(wd_cr.astype(np.float32)),
            "gw_cr": np.ascontiguousarray(gw_cr.astype(np.float32)),
            "ebias": ebias.astype(np.float32)}


TT = 4096


def emit_l0_ffn(p, dr):
    nc = p.nc
    x1d = dr["x1"]
    wg_cr = dr["wg_cr"]
    wu_cr = dr["wu_cr"]
    wd_cr = dr["wd_cr"]
    gw_crd = dr["gw_cr"]
    ebiasd = dr["ebias"]
    x2 = dr["x2"]
    comb_dram = nc.dram_tensor("comb_dram", [17, 4096], F32).ap()
    y_dram = nc.dram_tensor("y_dram2", [128, 8 * 4096], F32).ap()

    with ExitStack() as ctx:
        def sb(name, shape, dt=BF16):
            return ctx.enter_context(nc.sbuf_tensor(name, shape, dt))

        def pt(name, shape):
            return ctx.enter_context(nc.psum_tensor(name, shape, F32))

        xc = sb("xc", [128, 8, TT])
        wgs = sb("wgs", [128, 8, 1024])
        wus = sb("wus", [128, 8, 1024])
        wds = sb("wds", [128, 8, 1024])
        gws = sb("gws", [128, 8, 16])
        hbuf = sb("hbuf", [128, 8, 1024])
        g_sb = sb("g_sb", [16, TT])
        idtb = sb("idtb", [128, 128])
        gtok = sb("gtok", [128, 32 * 16], F32)
        comb_all = sb("comb_all", [128, 32 * 16], F32)
        gwork = sb("gwork", [128, 16], F32)
        ctmp = sb("ctmp", [16, 128], F32)
        ysc = sb("ysc2", [128, 512], F32)
        scores = sb("scores", [128, 16], F32)
        sel = sb("sel", [128, 16], F32)
        msum = sb("msum", [128, 16], F32)
        mx1 = sb("mx1", [128, 1], F32)
        rs1 = sb("rs1", [128, 1], F32)
        ebias_s = sb("ebias_s", [128, 16], F32)
        ce_sb = sb("ce_sb", [1, 4096], F32)
        ce_bf = sb("ce_bf", [1, 4096])
        ce_bc = sb("ce_bc", [128, 512])
        sgm = sb("sgm", [128, 512])
        htmp = sb("htmp", [128, 512])
        hfin = sb("hfin", [128, 512])
        idt = sb("idt", [128, 128], F32)
        ones1_128 = sb("ones1_128", [1, 128])
        ones128 = sb("ones128b", [128, 1])
        sum_bf = sb("sum_bf2", [128, 8, 512])
        sq_bf = hfin
        xres = sb("xres2", [128, 512], F32)
        sprev = sb("sprev2", [128, 512], F32)
        outf = sb("outf2", [128, 512], F32)
        mrow = sb("mrow2", [1, 512], F32)
        m2row = sb("m2row2", [1, 512], F32)
        varr = sb("varr2", [1, 512], F32)
        mrow_bf = sb("mrow_bf2", [1, 512])
        rstd_bf = sb("rstd_bf2", [1, 512])
        mb_sb = sb("mb_sb2", [128, 512])
        rs_sb = sb("rs_sb2", [128, 512])
        ones1 = sb("ones1b", [1, 64])
        dum = sb("dum2", [1, 1], F32)

        psG = pt("psG", [128, 512])
        psU = pt("psU", [128, 512])
        psB2 = pt("psB2", [128, 512])
        psY2 = pt("psY2", [128, 512])
        psS = pt("psS", [1, 512])
        psT = ctx.enter_context(nc.psum_tensor("psT", [128, 16], BF16))
        psQ = pt("psQ", [1, 512])

        p.g(lambda e, k, ks: e.dma_start(out=gws[:, :, :].rearrange("p c f -> p (c f)"), in_=gw_crd))
        p.g(lambda e, k, ks: e.dma_start(out=ebias_s[:, :], in_=ebiasd))
        p.v(lambda e, k, ks: e.memset(ones1_128[:, :], 1.0))
        p.v(lambda e, k, ks: e.memset(ones128[:, :], 1.0))
        p.v(lambda e, k, ks: e.memset(ones1[:, :], 1.0))
        p.gc(lambda e, k, ks: e.iota(idt[:, :], pattern=[[-1, 128]], base=0,
                                     channel_multiplier=1,
                                     allow_small_or_imprecise_dtypes=True))
        p.v(lambda e, k, ks: e.tensor_scalar(idt[:, :], idt[:, :], 0.0, None, ALU.is_equal))
        p.v(lambda e, k, ks: e.tensor_copy(idtb[:, :], idt[:, :]))
        for b in range(B_CORE):
            for dc in range(8):
                p.g(lambda e, k, ks, b=b, dc=dc: e.dma_start(
                    out=xc[:, dc, ds(b * T, T)], in_=x1d[b, ds(dc * 128, 128), :]))
        p.v(lambda e, k, ks: e.memset(ce_sb[:, :], 1.0))
        p.sy(lambda e, k, ks: e.dma_start(out=comb_dram[16:17, :], in_=ce_sb[:, :]))

        with p.Loop(8):
            for dc in range(8):
                p.t(lambda e, k, ks, dc=dc: e.matmul(
                    psB2[0:16, :], gws[:, dc, :], xc[:, dc, ts(k, 512)],
                    start=(dc == 0), stop=(dc == 7)))
            p.v(lambda e, k, ks: e.tensor_copy(g_sb[:, ts(k, 512)], psB2[0:16, :]))
        for tile in range(32):
            p.t(lambda e, k, ks, tile=tile: e.transpose(
                psT[:, :], g_sb[:, ds(tile * 128, 128)], idtb[0:16, 0:16]))
            p.v(lambda e, k, ks, tile=tile: e.tensor_copy(
                gtok[:, ds(tile * 16, 16)], psT[:, :]))
        with p.Loop(32):
            p.v(lambda e, k, ks: e.tensor_copy(gwork[:, :], gtok[:, ts(k, 16)]))
            p.s(lambda e, k, ks: e.activation(scores[:, :], gwork[:, :], AF.Sigmoid))
            p.v(lambda e, k, ks: e.tensor_tensor(sel[:, :], scores[:, :], ebias_s[:, :], ALU.add))
            p.v(lambda e, k, ks: e.memset(msum[:, :], 0.0))
            for i in range(8):
                p.v(lambda e, k, ks: e.tensor_reduce(mx1[:, :], sel[:, :], AX.X, ALU.max))
                p.v(lambda e, k, ks: e.tensor_scalar(gwork[:, :], sel[:, :], mx1[:, 0:1], None, ALU.is_ge))
                p.v(lambda e, k, ks: e.tensor_tensor(msum[:, :], msum[:, :], gwork[:, :], ALU.add))
                p.v(lambda e, k, ks: e.scalar_tensor_tensor(
                    sel[:, :], gwork[:, :], -1e30, sel[:, :], ALU.mult, ALU.add))
            p.v(lambda e, k, ks: e.tensor_tensor(gwork[:, :], scores[:, :], msum[:, :], ALU.mult))
            p.v(lambda e, k, ks: e.tensor_reduce(rs1[:, :], gwork[:, :], AX.X, ALU.add))
            p.v(lambda e, k, ks: e.reciprocal(rs1[:, :], rs1[:, :]))
            p.v(lambda e, k, ks: e.tensor_scalar(gwork[:, :], gwork[:, :], rs1[:, 0:1], 2.826, ALU.mult, op1=ALU.mult))
            p.v(lambda e, k, ks: e.tensor_copy(comb_all[:, ts(k, 16)], gwork[:, :]))
        for tile in range(32):
            p.t(lambda e, k, ks, tile=tile: e.transpose(
                psB2[0:16, 0:128], comb_all[:, ds(tile * 16, 16)], idt[:, :]))
            p.v(lambda e, k, ks, tile=tile: e.tensor_copy(
                ctmp[:, :], psB2[0:16, 0:128]))
            p.sy(lambda e, k, ks, tile=tile: e.dma_start(
                out=comb_dram[0:16, ds(tile * 128, 128)], in_=ctmp[:, :]))

        p.v(lambda e, k, ks: e.memset(ysc[:, :], 0.0))
        for oc in range(8):
            for tcg in range(8):
                p.sy(lambda e, k, ks, oc=oc, tcg=tcg: e.dma_start(
                    out=y_dram[:, ds(oc * 4096 + tcg * 512, 512)], in_=ysc[:, :]))

        with p.Loop(17):
            p.g(lambda e, k, ks: e.dma_start(
                out=wgs[:, :, :].rearrange("p c f -> p (c f)"), in_=wg_cr[ts(k, 128), :]))
            p.g(lambda e, k, ks: e.dma_start(
                out=wus[:, :, :].rearrange("p c f -> p (c f)"), in_=wu_cr[ts(k, 128), :]))
            p.g(lambda e, k, ks: e.dma_start(
                out=wds[:, :, :].rearrange("p c f -> p (c f)"), in_=wd_cr[ts(k, 128), :]))
            p.g(lambda e, k, ks: e.dma_start(out=ce_sb[:, :], in_=comb_dram[ts(k, 1), :]))
            p.v(lambda e, k, ks: e.tensor_copy(ce_bf[:, :], ce_sb[:, :]))
            for tcg in range(8):
                t0 = tcg * 512
                p.t(lambda e, k, ks, t0=t0: e.matmul(
                    psB2[:, :], ones1_128[:, :], ce_bf[:, ds(t0, 512)],
                    start=True, stop=True))
                p.v(lambda e, k, ks: e.tensor_copy(ce_bc[:, :], psB2[:, :]))
                for ic in range(8):
                    for dc in range(8):
                        p.t(lambda e, k, ks, dc=dc, ic=ic, t0=t0: e.matmul(
                            psG[:, :], wgs[:, dc, ds(ic * 128, 128)],
                            xc[:, dc, ds(t0, 512)], start=(dc == 0), stop=(dc == 7)))
                    for dc in range(8):
                        p.t(lambda e, k, ks, dc=dc, ic=ic, t0=t0: e.matmul(
                            psU[:, :], wus[:, dc, ds(ic * 128, 128)],
                            xc[:, dc, ds(t0, 512)], start=(dc == 0), stop=(dc == 7)))
                    p.s(lambda e, k, ks: e.activation(sgm[:, :], psG[:, :], AF.Sigmoid))
                    p.v(lambda e, k, ks: e.tensor_tensor(htmp[:, :], psG[:, :], sgm[:, :], ALU.mult))
                    p.v(lambda e, k, ks: e.tensor_tensor(hfin[:, :], htmp[:, :], psU[:, :], ALU.mult))
                    p.v(lambda e, k, ks, ic=ic: e.tensor_tensor(
                        hbuf[:, ic, 0:512], hfin[:, :], ce_bc[:, :], ALU.mult))
                for oc in range(8):
                    for ic in range(8):
                        p.t(lambda e, k, ks, ic=ic, oc=oc: e.matmul(
                            psY2[:, :], wds[:, ic, ds(oc * 128, 128)],
                            hbuf[:, ic, 0:512], start=(ic == 0), stop=(ic == 7)))
                    p.v(lambda e, k, ks: e.tensor_copy(ysc[:, :], psY2[:, :]))
                    p.g(lambda e, k, ks, oc=oc, t0=t0: e.dma_start(
                        out=y_dram[:, ds(oc * 4096 + t0, 512)], in_=ysc[:, :],
                        accum_op=ALU.add))

        for tcg in range(8):
            t0 = tcg * 512
            b = t0 // T
            toff = t0 % T
            for oc in range(8):
                p.g(lambda e, k, ks, oc=oc, t0=t0: e.dma_start(
                    out=sprev[:, :], in_=y_dram[:, ds(oc * 4096 + t0, 512)]))
                p.g(lambda e, k, ks, b=b, oc=oc, toff=toff: e.dma_start(
                    out=xres[:, :], in_=x1d[b, ds(oc * 128, 128), ds(toff, 512)]))
                p.v(lambda e, k, ks: e.tensor_tensor(sprev[:, :], sprev[:, :], xres[:, :], ALU.add))
                p.v(lambda e, k, ks, oc=oc: e.tensor_copy(sum_bf[:, oc, :], sprev[:, :]))
                p.s(lambda e, k, ks: e.activation(sq_bf[:, :], sprev[:, :], AF.Square))
                p.t(lambda e, k, ks, oc=oc: e.matmul(
                    psS[:, :], ones128[:, :], sum_bf[:, oc, :],
                    start=(oc == 0), stop=(oc == 7)))
                p.t(lambda e, k, ks, oc=oc: e.matmul(
                    psQ[:, :], ones128[:, :], sq_bf[:, :],
                    start=(oc == 0), stop=(oc == 7)))
            p.v(lambda e, k, ks: e.tensor_scalar(mrow[:, :], psS[:, :], 1.0 / D, None, ALU.mult))
            p.v(lambda e, k, ks: e.tensor_scalar(m2row[:, :], psQ[:, :], 1.0 / D, None, ALU.mult))
            p.v(lambda e, k, ks: e.tensor_tensor(varr[:, :], mrow[:, :], mrow[:, :], ALU.mult))
            p.v(lambda e, k, ks: e.tensor_tensor(varr[:, :], m2row[:, :], varr[:, :], ALU.subtract))
            p.v(lambda e, k, ks: e.tensor_scalar(varr[:, :], varr[:, :], 1e-5, None, ALU.add))
            p.s(lambda e, k, ks: e.activation(varr[:, :], varr[:, :], AF.Sqrt))
            p.v(lambda e, k, ks: e.reciprocal(varr[:, :], varr[:, :]))
            p.v(lambda e, k, ks: e.tensor_copy(mrow_bf[:, :], mrow[:, :]))
            p.v(lambda e, k, ks: e.tensor_copy(rstd_bf[:, :], varr[:, :]))
            p.t(lambda e, k, ks: e.matmul(psB2[0:64, :], ones1[:, :], mrow_bf[:, :], start=True, stop=True))
            p.t(lambda e, k, ks: e.matmul(psG[0:64, :], ones1[:, :], rstd_bf[:, :], start=True, stop=True))
            p.v(lambda e, k, ks: e.tensor_copy(mb_sb[0:64, :], psB2[0:64, :]))
            p.v(lambda e, k, ks: e.tensor_copy(rs_sb[0:64, :], psG[0:64, :]))
            p.v(lambda e, k, ks: e.tensor_copy(mb_sb[64:128, :], psB2[0:64, :]))
            p.v(lambda e, k, ks: e.tensor_copy(rs_sb[64:128, :], psG[0:64, :]))
            for oc in range(8):
                p.v(lambda e, k, ks, oc=oc: e.tensor_tensor(
                    outf[:, :], sum_bf[:, oc, :], mb_sb[:, :], ALU.subtract))
                p.v(lambda e, k, ks: e.tensor_tensor(outf[:, :], outf[:, :], rs_sb[:, :], ALU.mult))
                p.sy(lambda e, k, ks, b=b, oc=oc, toff=toff: e.dma_start(
                    out=x2[b, ds(oc * 128, 128), ds(toff, 512)], in_=outf[:, :]))

        p.gc(lambda e, k, ks: e.memset(dum[:, :], 0.0))


def build_l0_ffn():
    p = Prog()
    nc = p.nc
    x1d = nc.dram_tensor("x1", [B_CORE, D, T], F32, kind="ExternalInput").ap()
    wg_cr = nc.dram_tensor("wg_cr", [17 * 128, 8192], F32, kind="ExternalInput").ap()
    wu_cr = nc.dram_tensor("wu_cr", [17 * 128, 8192], F32, kind="ExternalInput").ap()
    wd_cr = nc.dram_tensor("wd_cr", [17 * 128, 8192], F32, kind="ExternalInput").ap()
    gw_crd = nc.dram_tensor("gw_cr", [128, 128], F32, kind="ExternalInput").ap()
    ebiasd = nc.dram_tensor("ebias", [128, 16], F32, kind="ExternalInput").ap()
    x2 = nc.dram_tensor("x2", [B_CORE, D, T], F32, kind="ExternalOutput").ap()
    comb_dram = nc.dram_tensor("comb_dram", [17, 4096], F32).ap()
    y_dram = nc.dram_tensor("y_dram2", [128, 8 * 4096], F32).ap()


    TT = 4096

    dr = {"x1": x1d,
          "wg_cr": wg_cr,
          "wu_cr": wu_cr,
          "wd_cr": wd_cr,
          "gw_cr": gw_crd,
          "ebias": ebiasd,
          "x2": x2}
    emit_l0_ffn(p, dr)
    with ExitStack() as _ctx2:
        p.emit(_ctx2)
    return p.nc



def _l1_band_masks():
    kk = np.arange(128)[:, None]
    qq = np.arange(128)[None, :]
    blocks = []
    for dlt in (-1, 0, 1):
        rel = kk + 128 * dlt - qq
        blocks.append((np.abs(rel) <= 3).astype(np.float32))
    m = np.concatenate(blocks, axis=1)
    mL = m.copy(); mL[:, 0:128] = 0.0
    mR = m.copy(); mR[:, 256:384] = 0.0
    return np.ascontiguousarray(np.concatenate([m, mL, mR], axis=1))


def _edge_weights():
    wh = np.zeros((8, 3), np.float32)
    for q in range(3):
        for r in range(6):
            wh[r, q] = 1.0 if r <= q + 3 else 0.0
        wh[6, q] = 3 - q
    wt = np.zeros((8, 3), np.float32)
    for j in range(3):
        for r in range(6):
            wt[r, j] = 1.0 if r >= j else 0.0
        wt[6, j] = j + 1
    return np.ascontiguousarray(wh), np.ascontiguousarray(wt)


def l1_attn_host_inputs(inputs):
    Wq = inputs["L1_attn_W_qkv"]
    wh, wt = _edge_weights()
    b = inputs["L1_attn_b_qkv"]
    return {
        "wq1": _pair_sliced(np.ascontiguousarray(Wq[0:D].T)),
        "wk1": _pair_sliced(np.ascontiguousarray(Wq[D:2 * D].T)),
        "wv1": _pair_sliced(np.ascontiguousarray(Wq[2 * D:3 * D].T)),
        "bq1": np.ascontiguousarray(b[0:D].reshape(8, 128).T),
        "bk1": np.ascontiguousarray(b[D:2 * D].reshape(8, 128).T),
        "bv1": np.ascontiguousarray(b[2 * D:3 * D].reshape(8, 128).T),
        "wlatT": np.ascontiguousarray(inputs["L1_attn_W_lat"].T),
        "blat": np.ascontiguousarray(inputs["L1_attn_b_lat"].reshape(64, 1)),
        "wproj": _chunk_rows(np.ascontiguousarray(inputs["L1_attn_W_proj"].T)),
        "bproj": np.ascontiguousarray(inputs["L1_attn_b_proj"].reshape(8, 128).T),
        "wg1": _chunk_rows(np.ascontiguousarray(inputs["L1_ffn_Wg"])),
        "bands": _l1_band_masks(),
        "ewh": wh, "ewt": wt,
        "U128": np.triu(np.ones((128, 128), np.float32)),
        "U16s": np.triu(np.ones((16, 16), np.float32), 1),
        "U32s": np.triu(np.ones((32, 32), np.float32), 1),
    }


def emit_l1_attn(p, dr):
    nc = p.nc
    x2d = dr["x2"]
    wq1 = dr["wq1"]
    wk1 = dr["wk1"]
    wv1 = dr["wv1"]
    bq1 = dr["bq1"]
    bk1 = dr["bk1"]
    bv1 = dr["bv1"]
    wlatT = dr["wlatT"]
    blatd = dr["blat"]
    wproj = dr["wproj"]
    bprojd = dr["bproj"]
    wg1 = dr["wg1"]
    bandsd = dr["bands"]
    ewhd = dr["ewh"]
    ewtd = dr["ewt"]
    U128d = dr["U128"]
    U16sd = dr["U16s"]
    U32sd = dr["U32s"]
    x3bf = dr["x3bf"]
    hist_o = dr["hist"]
    gates_o = dr["gates"]
    ohT_o = dr["ohT"]
    locT_o = dr["locT"]
    slots_o = dr["slots"]
    t2e_o = dr["t2e"]
    y_dram = nc.dram_tensor("y_dram3", [D, T], F32).ap()
    cb_dram = nc.dram_tensor("cb_dram", [32, 16], F32).ap()

    with ExitStack() as ctx:
        def sb(name, shape, dt=BF16):
            return ctx.enter_context(nc.sbuf_tensor(name, shape, dt))

        def pt(name, shape, dt=F32):
            return ctx.enter_context(nc.psum_tensor(name, shape, dt))

        xb = sb("xb3", [128, 8, T])
        wqs = sb("wqs3", [128, 8, 128])
        wks = sb("wks3", [128, 8, 128])
        wvs = sb("wvs3", [128, 8, 128])
        wbuf = sb("wbuf3", [128, 8, D])
        x3sb = sb("x3sb", [128, 8, 4096])
        qp = sb("qp3", [128, T]); kp = sb("kp3", [128, T]); vp = sb("vp3", [128, T])
        klat = sb("klat3", [128, T])
        vlt = sb("vlt3", [128, 16 * 64])
        wlat_s = sb("wlat_s", [128, 64])
        qe3 = sb("qe3", [64, 3])
        blat_s = sb("blat_s", [64, 1], F32)
        blat_r = sb("blat_r", [1, 64])
        bq_s = sb("bq_s", [128, 8], F32)
        bk_s = sb("bk_s", [128, 8], F32)
        bv_s = sb("bv_s", [128, 8], F32)
        bproj_s = sb("bproj_s", [128, 8], F32)
        bands = sb("bands_s", [128, 1152])
        mexp = sb("mexp", [128, 384])
        denr = sb("denr", [1, 384], F32)
        rden = sb("rden3", [1, 128], F32)
        rdenb = sb("rdenb3", [1, 128])
        bcy = sb("bcy3", [64, 128])
        ysc = sb("ysc3", [64, 128], F32)
        y_pair = sb("y_pair", [128, T], F32)
        ones128 = sb("ones128c", [128, 1])
        ones1 = sb("ones1c", [1, 64])
        ones8 = sb("ones8", [8, 1])
        ones16c = sb("ones16c", [16, 1])
        onesf1 = sb("onesf1", [1, 128], F32)
        ones1t = sb("ones1t", [1, 128])
        ke = sb("ke", [64, 8])
        ve = sb("ve", [8, 64])
        ewh_s = sb("ewh_s", [8, 3], F32)
        ewt_s = sb("ewt_s", [8, 3], F32)
        ee = sb("ee", [8, 3], F32)
        web = sb("web", [8, 3])
        rde = sb("rde", [1, 3], F32)
        rdeb = sb("rdeb", [1, 3])
        bce = sb("bce", [64, 3])
        ysce = sb("ysce", [64, 3], F32)
        ybufs = sb("ybufs3", [128, 8, 512])
        sum_bf = sb("sum_bf3", [128, 8, 512])
        sq_bf = sb("sq_bf3", [128, 512])
        xres = sb("xres3", [128, 512], F32)
        sprev = sb("sprev3", [128, 512], F32)
        outf = sb("outf3", [128, 512], F32)
        mrow = sb("mrow3", [1, 512], F32)
        m2row = sb("m2row3", [1, 512], F32)
        varr = sb("varr3", [1, 512], F32)
        mrow_bf = sb("mrow_bf3", [1, 512])
        rstd_bf = sb("rstd_bf3", [1, 512])
        mb_sb = sb("mb_sb3", [128, 512])
        rs_sb = sb("rs_sb3", [128, 512])
        idt = sb("idt3", [128, 128], F32)
        idb = sb("idb3", [128, 128])
        xtb = sb("xtb", [128, 128])
        wgs = sb("wgs3", [128, 8, 16])
        U128b = sb("U128b", [128, 128])
        U16sb = sb("U16sb", [16, 16])
        oh_tok = sb("oh_tok", [128, 16])
        oh_all = sb("oh_all", [128, 512])
        nmx = sb("nmx", [128, 1], F32)
        mx1 = sb("mx13", [128, 1], F32)
        el = sb("el3", [128, 16], F32)
        den1 = sb("den13", [128, 1], F32)
        gsmat = sb("gsmat", [128, 32], F32)
        rkmat = sb("rkmat", [128, 32], F32)
        cumT = sb("cumT", [16, 4096], F32)
        ohst = sb("ohst", [16, 128])
        ct16 = sb("ct16", [128, 16], F32)
        rk1 = sb("rk1", [128, 1], F32)
        tot = sb("tot", [16, 32], F32)
        totf = sb("totf", [32, 16], F32)
        U32f = sb("U32f", [32, 32], F32)
        baseC = sb("baseC", [16, 32], F32)
        hist_s = sb("hist_s", [16, 1], F32)
        pada = sb("pada", [16, 1], F32)
        io33 = sb("io33", [16, 33], F32)
        cmp33 = sb("cmp33", [16, 33], F32)
        qcol = sb("qcol", [16, 1], F32)
        padc = sb("padc", [16, 1], F32)
        padb = sb("padb", [16, 1])
        segb = sb("segb", [1, 16], F32)
        segc = sb("segc", [16, 1], F32)
        cbmat = sb("cbmat", [16, 32], F32)
        cbrow = sb("cbrow", [1, 16], F32)
        gath = sb("gath", [128, 16], F32)
        slotc = sb("slotc", [128, 1], F32)
        slots_f = sb("slots_f", [128, 32], F32)
        slots_i = sb("slots_i", [128, 32], I32)
        io48 = sb("io48", [16, 48], F32)
        cmp48 = sb("cmp48", [16, 48])
        t2er = sb("t2er", [1, 48], F32)
        t2ei = sb("t2ei", [1, 48], I32)
        gtmp = sb("gtmp", [32, 128], F32)
        dum = sb("dum3", [1, 1], F32)

        psA = pt("psA3", [128, 512])
        psB = pt("psB3", [128, 512])
        psC = pt("psC3", [128, 512])
        psE3 = pt("psE3", [128, 384])
        psY3 = pt("psY3", [64, 128])
        psD3 = pt("psD3", [1, 384])
        psS = pt("psS3", [1, 512])
        psQ = pt("psQ3", [1, 512])
        psAb = psA.bitcast(BF16)
        psYb = psY3.bitcast(BF16)

        for dst, srcd in ((bands, bandsd), (bq_s, bq1), (bk_s, bk1), (bv_s, bv1),
                          (bproj_s, bprojd), (blat_s, blatd),
                          (ewh_s, ewhd), (ewt_s, ewtd), (U128b, U128d),
                          (U16sb, U16sd)):
            p.g(lambda e, k, ks, dst=dst, srcd=srcd: e.dma_start(out=dst[:, :], in_=srcd))
        p.g(lambda e, k, ks: e.dma_start(out=wlat_s[0:64, :], in_=wlatT))
        p.g(lambda e, k, ks: e.dma_start(out=wlat_s[64:128, :], in_=wlatT))
        p.g(lambda e, k, ks: e.dma_start(out=blat_r[:, :], in_=blatd.rearrange("a b -> b a")))
        p.g(lambda e, k, ks: e.dma_start(
            out=wgs[:, :, :].rearrange("p c f -> p (c f)"), in_=wg1))
        for t in (ones128, ones1, ones8, ones16c, onesf1, ones1t):
            p.v(lambda e, k, ks, t=t: e.memset(t[:, :], 1.0))
        p.v(lambda e, k, ks: e.memset(ke[:, :], 0.0))
        p.v(lambda e, k, ks: e.memset(ve[:, :], 0.0))
        p.gc(lambda e, k, ks: e.iota(idt[:, :], pattern=[[-1, 128]], base=0,
                                     channel_multiplier=1,
                                     allow_small_or_imprecise_dtypes=True))
        p.v(lambda e, k, ks: e.tensor_scalar(idt[:, :], idt[:, :], 0.0, None, ALU.is_equal))
        p.v(lambda e, k, ks: e.tensor_copy(idb[:, :], idt[:, :]))
        p.gc(lambda e, k, ks: e.iota(io48[:, :], pattern=[[1, 48]], base=0,
                                     channel_multiplier=0,
                                     allow_small_or_imprecise_dtypes=True))

        for b in range(B_CORE):
            for dc in range(8):
                p.g(lambda e, k, ks, b=b, dc=dc: e.dma_start(
                    out=xb[:, dc, :], in_=x2d[b, ds(dc * 128, 128), :]))

            with p.Loop(8):
                for wsl, wdr in ((wqs, wq1), (wks, wk1), (wvs, wv1)):
                    p.g(lambda e, k, ks, wsl=wsl, wdr=wdr: e.dma_start(
                        out=wsl[:, :, :].rearrange("p c f -> p (c f)"),
                        in_=wdr[ts(k, 128), :]))
                for tc in range(4):
                    for (wsl, dst, ps, bcol) in ((wqs, qp, psA, bq_s), (wks, kp, psB, bk_s),
                                                 (wvs, vp, psC, bv_s)):
                        for dc in range(8):
                            p.t(lambda e, k, ks, wsl=wsl, ps=ps, dc=dc, tc=tc: e.matmul(
                                ps[:, :], wsl[:, dc, :], xb[:, dc, ds(tc * 512, 512)],
                                start=(dc == 0), stop=(dc == 7)))
                        p.v(lambda e, k, ks, dst=dst, ps=ps, tc=tc, bcol=bcol: e.tensor_scalar(
                            dst[:, ds(tc * 512, 512)], ps[:, :], bcol[:, ds(CE(e, k, 1), 1)],
                            None, ALU.add))

                for h in range(2):
                    hr = h * 64
                    for tc in range(4):
                        p.t(lambda e, k, ks, hr=hr, tc=tc: e.matmul(
                            psA[0:64, :], wlat_s[hr:hr + 64, :],
                            kp[hr:hr + 64, ds(tc * 512, 512)], start=True, stop=True))
                        p.v(lambda e, k, ks, hr=hr, tc=tc: e.tensor_scalar(
                            klat[hr:hr + 64, ds(tc * 512, 512)], psA[0:64, :],
                            blat_s[:, 0:1], None, ALU.add))
                    for tt in range(16):
                        p.t(lambda e, k, ks, hr=hr, tt=tt: e.matmul(
                            psC[:, 0:64], vp[hr:hr + 64, ds(tt * 128, 128)],
                            wlat_s[hr:hr + 64, :], start=True, stop=False))
                        p.t(lambda e, k, ks: e.matmul(
                            psC[:, 0:64], ones1t[:, :], blat_r[:, :],
                            start=False, stop=True))
                        p.v(lambda e, k, ks, tt=tt: e.tensor_copy(
                            vlt[:, ds(tt * 64, 64)], psC[:, 0:64]))

                    for qt in range(16):
                        kts = [qt - 1, qt, qt + 1]
                        variant = 0
                        if qt == 0:
                            kts[0] = 15
                            variant = 1
                        if qt == 15:
                            kts[2] = 0
                            variant = 2
                        for pos in range(3):
                            p.t(lambda e, k, ks, hr=hr, qt=qt, kt=kts[pos], pos=pos: e.matmul(
                                psE3[:, ds(pos * 128, 128)],
                                klat[hr:hr + 64, ds(kt * 128, 128)],
                                qp[hr:hr + 64, ds(qt * 128, 128)], start=True, stop=True))
                        p.s(lambda e, k, ks: e.activation(
                            mexp[:, :], psE3[:, :], AF.Exp, scale=0.125))
                        p.v(lambda e, k, ks, variant=variant: e.tensor_tensor(
                            mexp[:, :], mexp[:, :], bands[:, ds(variant * 384, 384)],
                            ALU.mult))
                        p.t(lambda e, k, ks: e.matmul(
                            psD3[:, :], ones128[:, :], mexp[:, :], start=True, stop=True))
                        p.v(lambda e, k, ks: e.tensor_copy(denr[:, :], psD3[:, :]))
                        p.v(lambda e, k, ks: e.tensor_tensor(
                            rden[:, :], denr[0:1, 0:128], denr[0:1, 128:256], ALU.add))
                        p.v(lambda e, k, ks: e.tensor_tensor(
                            rden[:, :], rden[:, :], denr[0:1, 256:384], ALU.add))
                        p.v(lambda e, k, ks: e.reciprocal(rden[:, :], rden[:, :]))
                        p.v(lambda e, k, ks: e.tensor_copy(rdenb[:, :], rden[:, :]))
                        for pos in range(3):
                            p.t(lambda e, k, ks, kt=kts[pos], pos=pos: e.matmul(
                                psY3[:, :], vlt[:, ds(kt * 64, 64)],
                                mexp[:, ds(pos * 128, 128)],
                                start=(pos == 0), stop=(pos == 2)))
                        p.t(lambda e, k, ks: e.matmul(
                            psB[0:64, 0:128], ones1[:, :], rdenb[:, :], start=True, stop=True))
                        p.v(lambda e, k, ks: e.tensor_copy(bcy[:, :], psB[0:64, 0:128]))
                        p.v(lambda e, k, ks, hr=hr, qt=qt: e.tensor_tensor(
                            y_pair[hr:hr + 64, ds(qt * 128, 128)], psY3[:, :],
                            bcy[:, :], ALU.mult))

                    for which in range(2):
                        if which == 0:
                            qoff, ew = 0, ewh_s
                        else:
                            qoff, ew = 2045, ewt_s
                        if which == 0:
                            p.v(lambda e, k, ks, hr=hr: e.tensor_copy(ke[:, 0:6], klat[hr:hr + 64, 0:6]))
                            p.v(lambda e, k, ks, hr=hr: e.tensor_copy(ke[:, 6:7], klat[hr:hr + 64, 2047:2048]))
                            p.g(lambda e, k, ks: e.dma_start(out=ve[0:6, :], in_=vlt[0:6, 0:64]))
                            p.g(lambda e, k, ks: e.dma_start(out=ve[6:7, :], in_=vlt[127:128, ds(15 * 64, 64)]))
                        else:
                            p.v(lambda e, k, ks, hr=hr: e.tensor_copy(ke[:, 0:6], klat[hr:hr + 64, 2042:2048]))
                            p.v(lambda e, k, ks, hr=hr: e.tensor_copy(ke[:, 6:7], klat[hr:hr + 64, 0:1]))
                            p.g(lambda e, k, ks: e.dma_start(out=ve[0:6, :], in_=vlt[106:112, ds(15 * 64, 64)]))
                            p.g(lambda e, k, ks: e.dma_start(out=ve[6:7, :], in_=vlt[0:1, 0:64]))
                        p.v(lambda e, k, ks, hr=hr, qoff=qoff: e.tensor_copy(
                            qe3[:, :], qp[hr:hr + 64, ds(qoff, 3)]))
                        p.t(lambda e, k, ks: e.matmul(
                            psE3[0:8, 0:3], ke[:, :], qe3[:, :],
                            start=True, stop=True))
                        p.s(lambda e, k, ks: e.activation(
                            ee[:, :], psE3[0:8, 0:3], AF.Exp, scale=0.125))
                        p.v(lambda e, k, ks, ew=ew: e.tensor_tensor(ee[:, :], ee[:, :], ew[:, :], ALU.mult))
                        p.v(lambda e, k, ks: e.tensor_copy(web[:, :], ee[:, :]))
                        p.t(lambda e, k, ks: e.matmul(
                            psD3[0:1, 0:3], ones8[:, :], web[:, :], start=True, stop=True))
                        p.v(lambda e, k, ks: e.reciprocal(rde[:, :], psD3[0:1, 0:3]))
                        p.v(lambda e, k, ks: e.tensor_copy(rdeb[:, :], rde[:, :]))
                        p.t(lambda e, k, ks: e.matmul(
                            psY3[0:64, 0:3], ve[:, :], web[:, :], start=True, stop=True))
                        p.t(lambda e, k, ks: e.matmul(
                            psB[0:64, 0:3], ones1[:, :], rdeb[:, :], start=True, stop=True))
                        p.v(lambda e, k, ks: e.tensor_copy(bce[:, :], psB[0:64, 0:3]))
                        p.v(lambda e, k, ks, hr=hr, qoff=qoff: e.tensor_tensor(
                            y_pair[hr:hr + 64, ds(qoff, 3)], psY3[0:64, 0:3],
                            bce[:, :], ALU.mult))

                p.sy(lambda e, k, ks: e.dma_start(
                    out=y_dram[ts(k, 128), :], in_=y_pair[:, :]))

            p.g(lambda e, k, ks: e.dma_start(
                out=wbuf[:, :, :].rearrange("p c f -> p (c f)"), in_=wproj))
            for tc in range(4):
                for dc in range(8):
                    p.g(lambda e, k, ks, dc=dc, tc=tc: e.dma_start(
                        out=ybufs[:, dc, :],
                        in_=y_dram[ds(dc * 128, 128), ds(tc * 512, 512)]))
                for oc in range(8):
                    for dc in range(8):
                        p.t(lambda e, k, ks, dc=dc, oc=oc, tc=tc: e.matmul(
                            psA[:, :], wbuf[:, dc, ds(oc * 128, 128)],
                            ybufs[:, dc, :], start=(dc == 0), stop=(dc == 7)))
                    p.g(lambda e, k, ks, b=b, oc=oc, tc=tc: e.dma_start(
                        out=xres[:, :], in_=x2d[b, ds(oc * 128, 128), ds(tc * 512, 512)]))
                    p.v(lambda e, k, ks, oc=oc: e.tensor_scalar(
                        sprev[:, :], psA[:, :], bproj_s[:, oc:oc + 1], None, ALU.add))
                    p.v(lambda e, k, ks: e.tensor_tensor(
                        sprev[:, :], sprev[:, :], xres[:, :], ALU.add))
                    p.v(lambda e, k, ks, oc=oc: e.tensor_copy(sum_bf[:, oc, :], sprev[:, :]))
                    p.s(lambda e, k, ks: e.activation(sq_bf[:, :], sprev[:, :], AF.Square))
                    p.t(lambda e, k, ks, oc=oc: e.matmul(
                        psS[:, :], ones128[:, :], sum_bf[:, oc, :],
                        start=(oc == 0), stop=(oc == 7)))
                    p.t(lambda e, k, ks, oc=oc: e.matmul(
                        psQ[:, :], ones128[:, :], sq_bf[:, :],
                        start=(oc == 0), stop=(oc == 7)))
                p.v(lambda e, k, ks: e.tensor_scalar(mrow[:, :], psS[:, :], 1.0 / D, None, ALU.mult))
                p.v(lambda e, k, ks: e.tensor_scalar(m2row[:, :], psQ[:, :], 1.0 / D, None, ALU.mult))
                p.v(lambda e, k, ks: e.tensor_tensor(varr[:, :], mrow[:, :], mrow[:, :], ALU.mult))
                p.v(lambda e, k, ks: e.tensor_tensor(varr[:, :], m2row[:, :], varr[:, :], ALU.subtract))
                p.v(lambda e, k, ks: e.tensor_scalar(varr[:, :], varr[:, :], 1e-5, None, ALU.add))
                p.s(lambda e, k, ks: e.activation(varr[:, :], varr[:, :], AF.Sqrt))
                p.v(lambda e, k, ks: e.reciprocal(varr[:, :], varr[:, :]))
                p.v(lambda e, k, ks: e.tensor_copy(mrow_bf[:, :], mrow[:, :]))
                p.v(lambda e, k, ks: e.tensor_copy(rstd_bf[:, :], varr[:, :]))
                p.t(lambda e, k, ks: e.matmul(psB[0:64, :], ones1[:, :], mrow_bf[:, :], start=True, stop=True))
                p.t(lambda e, k, ks: e.matmul(psC[0:64, :], ones1[:, :], rstd_bf[:, :], start=True, stop=True))
                p.v(lambda e, k, ks: e.tensor_copy(mb_sb[0:64, :], psB[0:64, :]))
                p.v(lambda e, k, ks: e.tensor_copy(rs_sb[0:64, :], psC[0:64, :]))
                p.v(lambda e, k, ks: e.tensor_copy(mb_sb[64:128, :], psB[0:64, :]))
                p.v(lambda e, k, ks: e.tensor_copy(rs_sb[64:128, :], psC[0:64, :]))
                for oc in range(8):
                    p.v(lambda e, k, ks, oc=oc: e.tensor_tensor(
                        outf[:, :], sum_bf[:, oc, :], mb_sb[:, :], ALU.subtract))
                    p.v(lambda e, k, ks: e.tensor_tensor(outf[:, :], outf[:, :], rs_sb[:, :], ALU.mult))
                    p.v(lambda e, k, ks, oc=oc, b=b, tc=tc: e.tensor_copy(
                        x3sb[:, oc, ds(b * T + tc * 512, 512)], outf[:, :]))
            for oc in range(8):
                for tt in range(16):
                    p.t(lambda e, k, ks, oc=oc, tt=tt, b=b: e.transpose(
                        psAb[:, 0:128], x3sb[:, oc, ds(b * T + tt * 128, 128)], idb[:, :]))
                    p.v(lambda e, k, ks: e.tensor_copy(xtb[:, :], psAb[:, 0:128]))
                    p.sy(lambda e, k, ks, b=b, oc=oc, tt=tt: e.dma_start(
                        out=x3bf[ds(b * T + tt * 128, 128), ds(oc * 128, 128)],
                        in_=xtb[:, :]))

        for tile in range(32):
            for dc in range(8):
                p.t(lambda e, k, ks, dc=dc, tile=tile: e.matmul(
                    psB[:, 0:16], x3sb[:, dc, ds(tile * 128, 128)], wgs[:, dc, :],
                    start=(dc == 0), stop=(dc == 7)))
            p.v(lambda e, k, ks: e.tensor_reduce(mx1[:, :], psB[:, 0:16], AX.X, ALU.max))
            p.v(lambda e, k, ks: e.tensor_scalar(nmx[:, :], mx1[:, :], -1.0, None, ALU.mult))
            p.s(lambda e, k, ks: e.activation(el[:, :], psB[:, 0:16], AF.Exp, bias=nmx[:, 0:1]))
            p.v(lambda e, k, ks: e.tensor_reduce(den1[:, :], el[:, :], AX.X, ALU.add))
            p.v(lambda e, k, ks: e.reciprocal(den1[:, :], den1[:, :]))
            p.v(lambda e, k, ks, tile=tile: e.tensor_copy(gsmat[:, tile:tile + 1], den1[:, :]))
            p.v(lambda e, k, ks: e.tensor_scalar(oh_tok[:, :], psB[:, 0:16], mx1[:, 0:1], None, ALU.is_ge))
            p.v(lambda e, k, ks, tile=tile: e.tensor_copy(oh_all[:, ds(tile * 16, 16)], oh_tok[:, :]))
            p.t(lambda e, k, ks: e.transpose(psYb[0:16, 0:128], oh_tok[:, :], idb[:, :]))
            p.v(lambda e, k, ks: e.tensor_copy(ohst[:, :], psYb[0:16, 0:128]))
            p.sy(lambda e, k, ks, tile=tile: e.dma_start(
                out=ohT_o[:, ds(tile * 128, 128)], in_=ohst[:, :]))
            p.t(lambda e, k, ks: e.matmul(
                psC[0:16, 0:128], oh_tok[:, :], U128b[:, :], start=True, stop=True))
            p.v(lambda e, k, ks, tile=tile: e.tensor_copy(cumT[:, ds(tile * 128, 128)], psC[0:16, 0:128]))
            p.t(lambda e, k, ks: e.matmul(
                psB[:, 0:16], U128b[:, :], oh_tok[:, :], start=True, stop=True))
            p.v(lambda e, k, ks: e.tensor_tensor(ct16[:, :], psB[:, 0:16], oh_tok[:, :], ALU.mult))
            p.v(lambda e, k, ks, tile=tile: e.tensor_reduce(
                rkmat[:, ds(tile, 1)], ct16[:, :], AX.X, ALU.add))
        p.v(lambda e, k, ks: e.tensor_copy(
            tot[:, :], cumT.tensor_ap()[0:16, 127:4096:128] if False else cumT[:, 127::128]))
        p.g(lambda e, k, ks: e.dma_start(out=U32f[:, :], in_=U32sd))
        p.t(lambda e, k, ks: e.transpose(psB[0:32, 0:16], tot[:, :], idt[0:16, 0:16]))
        p.v(lambda e, k, ks: e.tensor_copy(totf[:, :], psB[0:32, 0:16]))
        p.t(lambda e, k, ks: e.matmul(
            psC[0:16, 0:32], totf[:, :], U32f[:, :], start=True, stop=True))
        p.v(lambda e, k, ks: e.tensor_copy(baseC[:, :], psC[0:16, 0:32]))
        p.v(lambda e, k, ks: e.tensor_tensor(hist_s[:, :], baseC[:, 31:32], tot[:, 31:32], ALU.add))
        p.sy(lambda e, k, ks: e.dma_start(out=hist_o, in_=hist_s[:, :]))
        p.v(lambda e, k, ks: e.tensor_scalar(pada[:, :], hist_s[:, :], 127.0, None, ALU.add))
        p.gc(lambda e, k, ks: e.iota(io33[:, :], pattern=[[1, 33]], base=1,
                                     channel_multiplier=0,
                                     allow_small_or_imprecise_dtypes=True))
        p.v(lambda e, k, ks: e.tensor_scalar(qcol[:, :], pada[:, :], 1.0 / 128.0, None, ALU.mult))
        p.v(lambda e, k, ks: e.tensor_scalar(cmp33[:, :], io33[:, :], qcol[:, 0:1], None, ALU.is_le))
        p.v(lambda e, k, ks: e.tensor_reduce(padc[:, :], cmp33[:, :], AX.X, ALU.add))
        p.v(lambda e, k, ks: e.tensor_scalar(padc[:, :], padc[:, :], 128.0, None, ALU.mult))
        p.v(lambda e, k, ks: e.tensor_copy(padb[:, :], padc[:, :]))
        p.t(lambda e, k, ks: e.matmul(
            psD3[0:1, 0:16], padb[:, :], U16sb[:, :], start=True, stop=True))
        p.v(lambda e, k, ks: e.tensor_copy(segb[:, :], psD3[0:1, 0:16]))
        p.t(lambda e, k, ks: e.transpose(psC[0:16, 0:1], segb[:, :], idt[0:1, 0:1]))
        p.v(lambda e, k, ks: e.tensor_copy(segc[:, :], psC[0:16, 0:1]))
        p.v(lambda e, k, ks: e.tensor_scalar(cbmat[:, :], baseC[:, :], segc[:, 0:1], None, ALU.add))
        p.t(lambda e, k, ks: e.transpose(psB[0:32, 0:16], cbmat[:, :], idt[0:16, 0:16]))
        p.v(lambda e, k, ks: e.tensor_copy(gtmp[:, 0:16], psB[0:32, 0:16]))
        p.sy(lambda e, k, ks: e.dma_start(out=cb_dram, in_=gtmp[:, 0:16]))
        with p.Loop(32):
            p.v(lambda e, k, ks: e.tensor_scalar(
                cumT[:, ts(k, 128)], cumT[:, ts(k, 128)], baseC[:, ds(k, 1)],
                None, ALU.add))
        p.sy(lambda e, k, ks: e.dma_start(out=locT_o, in_=cumT[:, :]))
        p.t(lambda e, k, ks: e.transpose(psC[0:32, 0:128], gsmat[:, :], idt[:, :]))
        p.v(lambda e, k, ks: e.tensor_copy(gtmp[:, :], psC[0:32, 0:128]))
        p.sy(lambda e, k, ks: e.dma_start(out=gates_o, in_=gtmp[:, :]))
        with p.Loop(32):
            p.g(lambda e, k, ks: e.dma_start(out=cbrow[:, :], in_=cb_dram[ts(k, 1), :]))
            p.t(lambda e, k, ks: e.matmul(
                psB[:, 0:16], onesf1[:, :], cbrow[:, :], start=True, stop=True))
            p.v(lambda e, k, ks: e.tensor_tensor(
                gath[:, :], psB[:, 0:16], oh_all[:, ts(k, 16)], ALU.mult))
            p.v(lambda e, k, ks: e.tensor_reduce(slotc[:, :], gath[:, :], AX.X, ALU.add))
            p.v(lambda e, k, ks: e.tensor_tensor(slotc[:, :], slotc[:, :], rkmat[:, ds(k, 1)], ALU.add))
            p.v(lambda e, k, ks: e.tensor_scalar(
                slots_f[:, ds(k, 1)], slotc[:, :], -1.0, None, ALU.add))
        p.v(lambda e, k, ks: e.tensor_copy(slots_i[:, :], slots_f[:, :]))
        p.sy(lambda e, k, ks: e.dma_start(out=slots_o, in_=slots_i[:, :]))
        p.v(lambda e, k, ks: e.tensor_scalar(
            cmp48[:, :], io48[:, :], 128.0, segc[:, 0:1], ALU.mult, op1=ALU.is_ge))
        p.t(lambda e, k, ks: e.matmul(
            psD3[0:1, 0:48], ones16c[:, :], cmp48[:, :], start=True, stop=True))
        p.v(lambda e, k, ks: e.tensor_scalar(t2er[:, :], psD3[0:1, 0:48], -1.0, 128.0, ALU.add, op1=ALU.mult))
        p.v(lambda e, k, ks: e.tensor_copy(t2ei[:, :], t2er[:, :]))
        p.sy(lambda e, k, ks: e.dma_start(out=t2e_o, in_=t2ei[:, :]))

        p.gc(lambda e, k, ks: e.memset(dum[:, :], 0.0))


def build_l1_attn():
    p = Prog()
    nc = p.nc
    x2d = nc.dram_tensor("x2", [B_CORE, D, T], F32, kind="ExternalInput").ap()
    wq1 = nc.dram_tensor("wq1", [D, D], F32, kind="ExternalInput").ap()
    wk1 = nc.dram_tensor("wk1", [D, D], F32, kind="ExternalInput").ap()
    wv1 = nc.dram_tensor("wv1", [D, D], F32, kind="ExternalInput").ap()
    bq1 = nc.dram_tensor("bq1", [128, 8], F32, kind="ExternalInput").ap()
    bk1 = nc.dram_tensor("bk1", [128, 8], F32, kind="ExternalInput").ap()
    bv1 = nc.dram_tensor("bv1", [128, 8], F32, kind="ExternalInput").ap()
    wlatT = nc.dram_tensor("wlatT", [64, 64], F32, kind="ExternalInput").ap()
    blatd = nc.dram_tensor("blat", [64, 1], F32, kind="ExternalInput").ap()
    wproj = nc.dram_tensor("wproj", [128, 8 * D], F32, kind="ExternalInput").ap()
    bprojd = nc.dram_tensor("bproj", [128, 8], F32, kind="ExternalInput").ap()
    wg1 = nc.dram_tensor("wg1", [128, 128], F32, kind="ExternalInput").ap()
    bandsd = nc.dram_tensor("bands", [128, 1152], F32, kind="ExternalInput").ap()
    ewhd = nc.dram_tensor("ewh", [8, 3], F32, kind="ExternalInput").ap()
    ewtd = nc.dram_tensor("ewt", [8, 3], F32, kind="ExternalInput").ap()
    U128d = nc.dram_tensor("U128", [128, 128], F32, kind="ExternalInput").ap()
    U16sd = nc.dram_tensor("U16s", [16, 16], F32, kind="ExternalInput").ap()
    U32sd = nc.dram_tensor("U32s", [32, 32], F32, kind="ExternalInput").ap()

    x3bf = nc.dram_tensor("x3bf", [4096, D], BF16, kind="ExternalOutput").ap()
    hist_o = nc.dram_tensor("hist", [16, 1], F32, kind="ExternalOutput").ap()
    gates_o = nc.dram_tensor("gates", [32, 128], F32, kind="ExternalOutput").ap()
    ohT_o = nc.dram_tensor("ohT", [16, 4096], BF16, kind="ExternalOutput").ap()
    locT_o = nc.dram_tensor("locT", [16, 4096], F32, kind="ExternalOutput").ap()
    slots_o = nc.dram_tensor("slots", [128, 32], I32, kind="ExternalOutput").ap()
    t2e_o = nc.dram_tensor("t2e", [1, 48], I32, kind="ExternalOutput").ap()
    y_dram = nc.dram_tensor("y_dram3", [D, T], F32).ap()
    cb_dram = nc.dram_tensor("cb_dram", [32, 16], F32).ap()

    dr = {"x2": x2d,
          "wq1": wq1,
          "wk1": wk1,
          "wv1": wv1,
          "bq1": bq1,
          "bk1": bk1,
          "bv1": bv1,
          "wlatT": wlatT,
          "blat": blatd,
          "wproj": wproj,
          "bproj": bprojd,
          "wg1": wg1,
          "bands": bandsd,
          "ewh": ewhd,
          "ewt": ewtd,
          "U128": U128d,
          "U16s": U16sd,
          "U32s": U32sd,
          "x3bf": x3bf,
          "hist": hist_o,
          "gates": gates_o,
          "ohT": ohT_o,
          "locT": locT_o,
          "slots": slots_o,
          "t2e": t2e_o}
    emit_l1_attn(p, dr)
    with ExitStack() as _ctx2:
        p.emit(_ctx2)
    return p.nc



def l1_ffn_host_inputs(inputs):
    w1 = inputs["L1_ffn_w1"]
    w2 = inputs["L1_ffn_w2"]
    w1_cr = w1.reshape(16, 8, 128, 4096).transpose(0, 2, 1, 3).reshape(16 * 128, 8 * 4096)
    w2_cr = w2.reshape(16, 32, 128, 1024).transpose(0, 2, 1, 3).reshape(16 * 128, 32 * 1024)
    return {"w1_cr": np.ascontiguousarray(w1_cr.astype(np.float32)),
            "w2_cr": np.ascontiguousarray(w2_cr.astype(np.float32))}


def emit_l1_ffn(p, dr):
    from concourse.bass_core_rust import IndirectOffsetOnAxis
    nc = p.nc
    x3d = dr["x3bf"]
    slotsd = dr["slots"]
    t2ed = dr["t2e"]
    w1d = dr["w1_cr"]
    w2d = dr["w2_cr"]
    yslots = dr["yslots"]
    xg = nc.dram_tensor("xg_dram", [6144, D], BF16).ap()

    with ExitStack() as ctx:
        def sb(name, shape, dt=BF16):
            return ctx.enter_context(nc.sbuf_tensor(name, shape, dt))

        w1s = sb("w1s4", [128, 8, 4096])
        w2s = sb("w2s4", [128, 32, 1024])
        hbuf = sb("hbuf4", [128, 32, 128])
        xstg = sb("xstg", [128, D])
        xgt = sb("xgt", [128, D])
        xf = sb("xf4", [128, 8, 128])
        yfm = sb("yfm", [128, 128])
        ytok = sb("ytok", [128, 8, 128])
        zt = sb("zt4", [128, D])
        idb4 = sb("idb4", [128, 128])
        idf4 = sb("idf4", [128, 128], F32)
        slots_s = ctx.enter_context(nc.sbuf_tensor("slots_s", [128, 32], I32))
        t2es = ctx.enter_context(nc.sbuf_tensor("t2es", [1, 48], I32))
        ecell = ctx.enter_context(nc.sbuf_tensor("ecell", [1, 1], I32))
        dum = sb("dum4", [1, 1], F32)

        psA = ctx.enter_context(nc.psum_tensor("psA4", [128, 512], F32))
        psB = ctx.enter_context(nc.psum_tensor("psB4", [128, 512], F32))
        psAb = psA.bitcast(BF16)

        p.g(lambda e, k, ks: e.dma_start(out=slots_s[:, :], in_=slotsd))
        p.g(lambda e, k, ks: e.dma_start(out=t2es[:, :], in_=t2ed))
        p.v(lambda e, k, ks: e.memset(zt[:, :], 0.0))
        p.gc(lambda e, k, ks: e.iota(idf4[:, :], pattern=[[-1, 128]], base=0,
                                     channel_multiplier=1,
                                     allow_small_or_imprecise_dtypes=True))
        p.v(lambda e, k, ks: e.tensor_scalar(idf4[:, :], idf4[:, :], 0.0, None, ALU.is_equal))
        p.v(lambda e, k, ks: e.tensor_copy(idb4[:, :], idf4[:, :]))
        for j in range(48):
            p.sy(lambda e, k, ks, j=j: e.dma_start(out=xg[ds(j * 128, 128), :], in_=zt[:, :]))
        for tile in range(32):
            p.g(lambda e, k, ks, tile=tile: e.dma_start(
                out=xstg[:, :], in_=x3d[ds(tile * 128, 128), :]))
            p.g(lambda e, k, ks, tile=tile: e.indirect_dma_start(
                xg, IndirectOffsetOnAxis(slots_s[:, tile:tile + 1], 0),
                xstg[:, :], None))
        _esv = [None]

        def _load_w1(e, k, ks):
            _esv[0] = e.value_load(ecell[0:1, 0:1])
            return e.dma_start(
                out=w1s[:, :, :].rearrange("p c f -> p (c f)"),
                in_=w1d[ds(_esv[0], 128), :])

        with p.Loop(48):
            p.v(lambda e, k, ks: e.tensor_copy(ecell[:, :], t2es[0:1, ds(k, 1)]))
            p.g(_load_w1)
            p.g(lambda e, k, ks: e.dma_start(
                out=w2s[:, :, :].rearrange("p c f -> p (c f)"), in_=w2d[ds(_esv[0], 128), :]))
            p.g(lambda e, k, ks: e.dma_start(out=xgt[:, :], in_=xg[ts(k, 128), :]))
            for dc in range(8):
                p.t(lambda e, k, ks, dc=dc: e.transpose(
                    psAb[:, 0:128], xgt[:, ds(dc * 128, 128)], idb4[:, :]))
                p.v(lambda e, k, ks, dc=dc: e.tensor_copy(xf[:, dc, :], psAb[:, 0:128]))
            for fc in range(32):
                for dc in range(8):
                    p.t(lambda e, k, ks, dc=dc, fc=fc: e.matmul(
                        psB[:, 0:128], w1s[:, dc, ds(fc * 128, 128)], xf[:, dc, :],
                        start=(dc == 0), stop=(dc == 7)))
                p.s(lambda e, k, ks, fc=fc: e.activation(
                    hbuf[:, fc, :], psB[:, 0:128], AF.Relu))
            for oc in range(8):
                for cf in range(32):
                    p.t(lambda e, k, ks, cf=cf, oc=oc: e.matmul(
                        psB[:, 0:128], w2s[:, cf, ds(oc * 128, 128)], hbuf[:, cf, :],
                        start=(cf == 0), stop=(cf == 31)))
                p.v(lambda e, k, ks: e.tensor_copy(yfm[:, :], psB[:, 0:128]))
                p.t(lambda e, k, ks: e.transpose(psAb[:, 0:128], yfm[:, :], idb4[:, :]))
                p.v(lambda e, k, ks, oc=oc: e.tensor_copy(ytok[:, oc, :], psAb[:, 0:128]))
            p.sy(lambda e, k, ks: e.dma_start(
                out=yslots[ts(k, 128), :],
                in_=ytok[:, :, :].rearrange("p c f -> p (c f)")))

        p.gc(lambda e, k, ks: e.memset(dum[:, :], 0.0))


def build_l1_ffn():
    from concourse.bass_core_rust import IndirectOffsetOnAxis
    p = Prog()
    nc = p.nc
    x3d = nc.dram_tensor("x3bf", [4096, D], BF16, kind="ExternalInput").ap()
    slotsd = nc.dram_tensor("slots", [128, 32], I32, kind="ExternalInput").ap()
    t2ed = nc.dram_tensor("t2e", [1, 48], I32, kind="ExternalInput").ap()
    w1d = nc.dram_tensor("w1_cr", [16 * 128, 8 * 4096], F32, kind="ExternalInput").ap()
    w2d = nc.dram_tensor("w2_cr", [16 * 128, 32 * 1024], F32, kind="ExternalInput").ap()
    yslots = nc.dram_tensor("yslots", [6144, D], BF16, kind="ExternalOutput").ap()
    xg = nc.dram_tensor("xg_dram", [6144, D], BF16).ap()

    dr = {"x3bf": x3d,
          "slots": slotsd,
          "t2e": t2ed,
          "w1_cr": w1d,
          "w2_cr": w2d,
          "yslots": yslots}
    emit_l1_ffn(p, dr)
    with ExitStack() as _ctx2:
        p.emit(_ctx2)
    return p.nc



def build_finale():
    from concourse.bass_core_rust import IndirectOffsetOnAxis
    p = Prog()
    nc = p.nc
    x3d = nc.dram_tensor("x3bf", [4096, D], BF16, kind="ExternalInput").ap()
    ysd = nc.dram_tensor("yslots", [6144, D], BF16, kind="ExternalInput").ap()
    slotsd = nc.dram_tensor("slots", [128, 32], I32, kind="ExternalInput").ap()
    gatesd = nc.dram_tensor("gates", [32, 128], F32, kind="ExternalInput").ap()
    ohTd = nc.dram_tensor("ohT", [16, 4096], BF16, kind="ExternalInput").ap()
    locTd = nc.dram_tensor("locT", [16, 4096], F32, kind="ExternalInput").ap()
    histsd = nc.dram_tensor("hists", [8, 16], F32, kind="ExternalInput").ap()
    outd = nc.dram_tensor("out", [4096, D], F32, kind="ExternalOutput").ap()
    pid_dram = nc.partition_id_tensor.ap()

    with ExitStack() as ctx:
        def sb(name, shape, dt=F32):
            return ctx.enter_context(nc.sbuf_tensor(name, shape, dt))

        hsb = sb("hsb", [8, 16])
        pidu = ctx.enter_context(nc.sbuf_tensor("pidu5", [1, 1], U32))
        pidf = sb("pidf5", [1, 1])
        onesf8 = sb("onesf8", [1, 8])
        onesf1 = sb("onesf15", [1, 128])
        ones16 = sb("ones16_5", [16, 1], BF16)
        iota8 = sb("iota85", [8, 1])
        pid8 = sb("pid85", [8, 1])
        cmp8 = sb("cmp85", [8, 1])
        bprev = sb("bprev", [16, 1])
        locs = sb("locs5", [16, 4096])
        ohs = ctx.enter_context(nc.sbuf_tensor("ohs5", [16, 4096], BF16))
        valt = sb("valt", [16, 4096])
        prodb = ctx.enter_context(nc.sbuf_tensor("prodb", [16, 4096], BF16))
        vrow = sb("vrow", [1, 4096])
        grow = sb("grow", [1, 4096])
        wrow = sb("wrow", [1, 4096])
        wcols = sb("wcols", [128, 32])
        id1 = sb("id15", [1, 1])
        slots_s = ctx.enter_context(nc.sbuf_tensor("slots_s5", [128, 32], I32))
        ytile = ctx.enter_context(nc.sbuf_tensor("ytile", [128, D], BF16))
        xt5 = ctx.enter_context(nc.sbuf_tensor("xt5", [128, D], BF16))
        sum5 = sb("sum5", [128, D])
        sqs = sb("sqs", [128, D])
        srow = sb("srow", [128, 1])
        ssq = sb("ssq", [128, 1])
        mean5 = sb("mean5", [128, 1])
        nmean = sb("nmean5", [128, 1])
        var5 = sb("var5", [128, 1])
        outt = sb("outt", [128, D])
        dum = sb("dum5", [1, 1])

        psA = ctx.enter_context(nc.psum_tensor("psA5", [128, 512], F32))
        psS = ctx.enter_context(nc.psum_tensor("psS5", [1, 512], F32))

        p.g(lambda e, k, ks: e.dma_start(out=hsb[:, :], in_=histsd))
        p.g(lambda e, k, ks: e.dma_start(out=pidu[:, :], in_=pid_dram))
        p.g(lambda e, k, ks: e.dma_start(out=locs[:, :], in_=locTd))
        p.g(lambda e, k, ks: e.dma_start(out=ohs[:, :], in_=ohTd))
        p.g(lambda e, k, ks: e.dma_start(out=slots_s[:, :], in_=slotsd))
        p.g(lambda e, k, ks: e.dma_start(out=grow[:, :],
                                         in_=gatesd.rearrange("a b -> (a b)")))
        p.v(lambda e, k, ks: e.memset(onesf8[:, :], 1.0))
        p.v(lambda e, k, ks: e.memset(onesf1[:, :], 1.0))
        p.v(lambda e, k, ks: e.memset(ones16[:, :], 1.0))
        p.v(lambda e, k, ks: e.memset(id1[:, :], 1.0))
        p.gc(lambda e, k, ks: e.iota(iota8[:, :], pattern=[[0, 1]], base=0,
                                     channel_multiplier=1,
                                     allow_small_or_imprecise_dtypes=True))
        p.v(lambda e, k, ks: e.tensor_copy(pidf[:, :], pidu[:, :]))
        p.t(lambda e, k, ks: e.matmul(psA[0:8, 0:1], onesf8[:, :], pidf[:, :],
                                      start=True, stop=True))
        p.v(lambda e, k, ks: e.tensor_copy(pid8[:, :], psA[0:8, 0:1]))
        p.v(lambda e, k, ks: e.tensor_tensor(cmp8[:, :], iota8[:, :], pid8[:, :], ALU.is_lt))
        p.t(lambda e, k, ks: e.matmul(psA[0:16, 0:1], hsb[:, :], cmp8[:, :],
                                      start=True, stop=True))
        p.v(lambda e, k, ks: e.tensor_copy(bprev[:, :], psA[0:16, 0:1]))
        p.v(lambda e, k, ks: e.tensor_scalar(valt[:, :], locs[:, :], bprev[:, 0:1], None, ALU.add))
        p.v(lambda e, k, ks: e.tensor_scalar(valt[:, :], valt[:, :], 2048.0, None, ALU.is_le))
        p.v(lambda e, k, ks: e.tensor_tensor(prodb[:, :], ohs[:, :], valt[:, :], ALU.mult))
        with p.Loop(8):
            p.t(lambda e, k, ks: e.matmul(psS[:, :], ones16[:, :],
                                          prodb[:, ts(k, 512)], start=True, stop=True))
            p.v(lambda e, k, ks: e.tensor_copy(vrow[:, ts(k, 512)], psS[:, :]))
        p.v(lambda e, k, ks: e.tensor_tensor(wrow[:, :], vrow[:, :], grow[:, :], ALU.mult))
        for tile in range(32):
            p.t(lambda e, k, ks, tile=tile: e.transpose(
                psA[:, 0:1], wrow[0:1, ds(tile * 128, 128)], id1[:, :]))
            p.v(lambda e, k, ks, tile=tile: e.tensor_copy(wcols[:, tile:tile + 1], psA[:, 0:1]))
        for tile in range(32):
            p.g(lambda e, k, ks, tile=tile: e.indirect_dma_start(
                ytile[:, :], None, ysd, IndirectOffsetOnAxis(slots_s[:, tile:tile + 1], 0)))
            p.g(lambda e, k, ks, tile=tile: e.dma_start(
                out=xt5[:, :], in_=x3d[ds(tile * 128, 128), :]))
            p.v(lambda e, k, ks, tile=tile: e.tensor_scalar(
                sum5[:, :], ytile[:, :], wcols[:, tile:tile + 1], None, ALU.mult))
            p.v(lambda e, k, ks: e.tensor_tensor(sum5[:, :], sum5[:, :], xt5[:, :], ALU.add))
            p.v(lambda e, k, ks: e.tensor_reduce(srow[:, :], sum5[:, :], AX.X, ALU.add))
            p.v(lambda e, k, ks: e.tensor_scalar(mean5[:, :], srow[:, :], 1.0 / D, None, ALU.mult))
            p.s(lambda e, k, ks: e.activation(sqs[:, :], sum5[:, :], AF.Square, accum_out=ssq[:, :]))
            p.v(lambda e, k, ks: e.tensor_scalar(var5[:, :], ssq[:, :], 1.0 / D, None, ALU.mult))
            p.v(lambda e, k, ks: e.tensor_tensor(nmean[:, :], mean5[:, :], mean5[:, :], ALU.mult))
            p.v(lambda e, k, ks: e.tensor_tensor(var5[:, :], var5[:, :], nmean[:, :], ALU.subtract))
            p.v(lambda e, k, ks: e.tensor_scalar(var5[:, :], var5[:, :], 1e-5, None, ALU.add))
            p.s(lambda e, k, ks: e.activation(var5[:, :], var5[:, :], AF.Sqrt))
            p.v(lambda e, k, ks: e.reciprocal(var5[:, :], var5[:, :]))
            p.v(lambda e, k, ks: e.tensor_scalar(nmean[:, :], mean5[:, :], -1.0, None, ALU.mult))
            p.v(lambda e, k, ks: e.tensor_scalar(outt[:, :], sum5[:, :], nmean[:, 0:1], None, ALU.add))
            p.v(lambda e, k, ks: e.tensor_scalar(outt[:, :], outt[:, :], var5[:, 0:1], None, ALU.mult))
            p.sy(lambda e, k, ks, tile=tile: e.dma_start(
                out=outd[ds(tile * 128, 128), :], in_=outt[:, :]))

        p.gc(lambda e, k, ks: e.memset(dum[:, :], 0.0))
        p.emit(ctx)
    return p.nc



_PROGS = {}


def _get_prog(name, builder):
    if name not in _PROGS:
        _PROGS[name] = builder()
    return _PROGS[name]


def kernel5(**inputs):
    cores = list(range(NC))
    x = inputs["L0_attn_x"]

    nc1 = _get_prog("l0attn", build_l0_attn)
    in1 = [l0_attn_host_inputs(x, inputs["L0_attn_w_qkv"], inputs["L0_attn_w_out"],
                               inputs["L0_attn_b_out"], c) for c in cores]
    r1 = run_bass_kernel_spmd(nc1, in1, core_ids=cores).results

    nc2 = _get_prog("l0ffn", build_l0_ffn)
    wm2 = l0_ffn_host_inputs(None, inputs, 0)
    in2 = [{**wm2, "x1": r1[c]["x1"]} for c in cores]
    r2 = run_bass_kernel_spmd(nc2, in2, core_ids=cores).results

    nc3 = _get_prog("l1attn", build_l1_attn)
    wm3 = l1_attn_host_inputs(inputs)
    in3 = [{**wm3, "x2": r2[c]["x2"]} for c in cores]
    r3 = run_bass_kernel_spmd(nc3, in3, core_ids=cores).results

    nc4 = _get_prog("l1ffn", build_l1_ffn)
    wm4 = l1_ffn_host_inputs(inputs)
    in4 = [{**wm4, "x3bf": r3[c]["x3bf"], "slots": r3[c]["slots"],
            "t2e": r3[c]["t2e"]} for c in cores]
    r4 = run_bass_kernel_spmd(nc4, in4, core_ids=cores).results

    hists = np.ascontiguousarray(
        np.stack([r3[c]["hist"][:, 0] for c in cores], 0)).astype(np.float32)
    nc5 = _get_prog("finale", build_finale)
    in5 = [{"x3bf": r3[c]["x3bf"], "yslots": r4[c]["yslots"],
            "slots": r3[c]["slots"], "gates": r3[c]["gates"],
            "ohT": r3[c]["ohT"], "locT": r3[c]["locT"], "hists": hists}
           for c in cores]
    r5 = run_bass_kernel_spmd(nc5, in5, core_ids=cores).results

    out = np.concatenate([r5[c]["out"].reshape(2, T, D) for c in cores], axis=0)
    return out.astype(np.float32)



def build_mega():
    p = Prog()
    nc = p.nc
    dr = {}

    def ein(name, shape, dt=F32):
        dr[name] = nc.dram_tensor(name, shape, dt, kind="ExternalInput").ap()

    def eout(name, shape, dt=F32):
        dr[name] = nc.dram_tensor(name, shape, dt, kind="ExternalOutput").ap()

    def eint(name, shape, dt=F32):
        dr[name] = nc.dram_tensor(name, shape, dt).ap()

    ein("xT", [B_CORE, D, T]); ein("wq_ps", [D, D]); ein("wk_ps", [D, D])
    ein("wv_ps", [D, D]); ein("wo_cr", [128, 8 * D]); ein("bout", [128, 8])
    ein("cs_tab", [128, T]); ein("sc_tab", [128, T]); ein("cmask", [128, 2048])
    eint("x1", [B_CORE, D, T])
    ein("wg_cr", [17 * 128, 8192]); ein("wu_cr", [17 * 128, 8192])
    ein("wd_cr", [17 * 128, 8192]); ein("gw_cr", [128, 128]); ein("ebias", [128, 16])
    eint("x2", [B_CORE, D, T])
    ein("wq1", [D, D]); ein("wk1", [D, D]); ein("wv1", [D, D])
    ein("bq1", [128, 8]); ein("bk1", [128, 8]); ein("bv1", [128, 8])
    ein("wlatT", [64, 64]); ein("blat", [64, 1]); ein("wproj", [128, 8 * D])
    ein("bproj", [128, 8]); ein("wg1", [128, 128]); ein("bands", [128, 1152])
    ein("ewh", [8, 3]); ein("ewt", [8, 3]); ein("U128", [128, 128])
    ein("U16s", [16, 16]); ein("U32s", [32, 32])
    eout("x3bf", [4096, D], BF16); eout("hist", [16, 1]); eout("gates", [32, 128])
    eout("ohT", [16, 4096], BF16); eout("locT", [16, 4096])
    eout("slots", [128, 32], I32); eout("t2e", [1, 48], I32)
    ein("w1_cr", [16 * 128, 8 * 4096]); ein("w2_cr", [16 * 128, 32 * 1024])
    eout("yslots", [6144, D], BF16)

    emit_l0_attn(p, dr)
    emit_l0_ffn(p, dr)
    emit_l1_attn(p, dr)
    emit_l1_ffn(p, dr)
    with ExitStack() as ctx:
        p.emit(ctx)
    return p.nc


def kernel(**inputs):
    cores = list(range(NC))
    x = inputs["L0_attn_x"]
    ncm = _get_prog("mega", build_mega)
    wm2 = l0_ffn_host_inputs(None, inputs, 0)
    wm3 = l1_attn_host_inputs(inputs)
    wm4 = l1_ffn_host_inputs(inputs)
    in_m = []
    for c in cores:
        m = dict(l0_attn_host_inputs(x, inputs["L0_attn_w_qkv"], inputs["L0_attn_w_out"],
                                     inputs["L0_attn_b_out"], c))
        m.update(wm2); m.update(wm3); m.update(wm4)
        in_m.append(m)
    rm = run_bass_kernel_spmd(ncm, in_m, core_ids=cores).results
    hists = np.ascontiguousarray(
        np.stack([rm[c]["hist"][:, 0] for c in cores], 0)).astype(np.float32)
    nc5 = _get_prog("finale", build_finale)
    in5 = [{"x3bf": rm[c]["x3bf"], "yslots": rm[c]["yslots"],
            "slots": rm[c]["slots"], "gates": rm[c]["gates"],
            "ohT": rm[c]["ohT"], "locT": rm[c]["locT"], "hists": hists}
           for c in cores]
    r5 = run_bass_kernel_spmd(nc5, in5, core_ids=cores).results
    out = np.concatenate([r5[c]["out"].reshape(2, T, D) for c in cores], axis=0)
    return out.astype(np.float32)


# version 05
# speedup vs baseline: 1.4259x; 1.0359x over previous
import numpy as np
import ml_dtypes
import concourse.bass as bass
import concourse.mybir as mybir
from concourse.bass_utils import run_bass_kernel_spmd
from contextlib import ExitStack, contextmanager

F32 = mybir.dt.float32
BF16 = mybir.dt.bfloat16
I32 = mybir.dt.int32
U32 = mybir.dt.uint32
AF = mybir.ActivationFunctionType
ALU = mybir.AluOpType
AX = mybir.AxisListType
ts = bass.ts
ds = bass.ds

B_CORE = 2
T = 2048
D = 1024
NH = 16
DH = 64
NC = 8

ENGINES = ("g", "sy", "t", "v", "s")
_ENG_INC = {"g": 16, "sy": 16, "t": 1, "v": 1, "s": 1}
_EXPR_CACHE = {}



_TPB_DONE = {}


def tpb_prep(e):
    if id(e) not in _TPB_DONE:
        r = e.alloc_register64("tpbb")
        e.tpb_base_ld(r)
        _TPB_DONE[id(e)] = r
    return e.nop()


def CE(e, kv, mult, off=0):
    key = (id(e), id(kv), mult, off)
    sv = _EXPR_CACHE.get(key)
    if sv is None:
        sv = e.compute_val(kv * mult + off)
        _EXPR_CACHE[key] = sv
    return sv


class _Op:
    __slots__ = ("eng", "fn", "inc", "pos")

    def __init__(self, eng, fn, inc, pos):
        self.eng, self.fn, self.inc, self.pos = eng, fn, inc, pos


class _LoopRec:
    __slots__ = ("n", "items", "per", "pos")

    def __init__(self, n, items, per, pos):
        self.n, self.items, self.per, self.pos = n, items, per, pos


class Prog:
    def __init__(self):
        self.nc = bass.Bass("TRN2", target_bir_lowering=False, debug=False)
        self.root = []
        self._stack = [(self.root, {e: 0 for e in ENGINES})]

    def _rec(self, ename, fn, inc):
        items, cnt = self._stack[-1]
        items.append(_Op(ename, fn, inc, dict(cnt)))
        cnt[ename] += inc

    def g(self, fn):
        self._rec("g", fn, 16)

    def gc(self, fn):
        self._rec("g", fn, 1)

    def sy(self, fn):
        self._rec("sy", fn, 16)

    def t(self, fn):
        self._rec("t", fn, 1)

    def v(self, fn):
        self._rec("v", fn, 1)

    def s(self, fn):
        self._rec("s", fn, 1)

    @contextmanager
    def Loop(self, n_iters):
        assert n_iters >= 1
        items = []
        cnt = {e: 0 for e in ENGINES}
        self._stack.append((items, cnt))
        yield
        self._stack.pop()
        parent_items, parent_cnt = self._stack[-1]
        rec = _LoopRec(n_iters, items, dict(cnt), dict(parent_cnt))
        parent_items.append(rec)
        for e in ENGINES:
            parent_cnt[e] += n_iters * cnt[e]

    def emit(self, ctx):
        nc = self.nc
        self._final = dict(self._stack[0][1])
        sems = {e: ctx.enter_context(nc.semaphore(f"sem_{e}")) for e in ENGINES}
        block = ctx.enter_context(nc.Block())

        def eng_has_ops(items, ename):
            for it in items:
                if isinstance(it, _Op):
                    if it.eng == ename:
                        return True
                else:
                    if it.per[ename]:
                        return True
            return False

        def emit_stream(eng, ename):
            wscratch = eng.alloc_register("wscr")
            wscr2 = eng.alloc_register("wscr2")
            last_static = {e: 0 for e in ENGINES}
            pending = []

            def emit_wait(e, static_tgt, kstack_pers):
                terms = [(kv, per[e]) for kv, per in kstack_pers if per[e]]
                if not terms:
                    if static_tgt > last_static[e]:
                        eng.wait_ge(sems[e], static_tgt)
                        last_static[e] = static_tgt
                    return
                kv0, per0 = terms[0]
                eng.reg_mul(wscr2, kv0, per0)
                for kvx, perx in terms[1:]:
                    eng.reg_mul(wscratch, kvx, perx)
                    eng.reg_alu(wscr2, wscr2, wscratch, ALU.add)
                eng.reg_add(wscr2, wscr2, static_tgt)
                eng.wait_ge(sems[e], wscr2)

            def flush_pending():
                while pending:
                    d = pending.pop(0)
                    for e, tgt in d.items():
                        if tgt > last_static[e]:
                            eng.wait_ge(sems[e], tgt)
                            last_static[e] = tgt

            RELAXED = ename in ("t",)

            def walk(items, base, kstack, prev):
                for it in items:
                    if isinstance(it, _Op):
                        if it.eng == ename:
                            flush_pending()
                            if RELAXED:
                                if prev[3] is not None:
                                    emit_wait(prev[3], prev[4], prev[5])
                            else:
                                if prev[0] is not None:
                                    emit_wait(prev[0], prev[1], prev[2])
                            kv = kstack[-1][0] if kstack else None
                            r = it.fn(eng, kv, [kk for kk, _ in kstack])
                            r.then_inc(sems[ename], it.inc)
                        prev[0] = it.eng
                        prev[1] = base[it.eng] + it.pos[it.eng] + it.inc
                        prev[2] = list(kstack)
                        if it.eng != ename:
                            prev[3] = it.eng
                            prev[4] = base[it.eng] + it.pos[it.eng] + it.inc
                            prev[5] = list(kstack)
                    else:
                        if it.n == 0:
                            continue
                        assert not kstack, "loops only at root level"
                        new_base = {e: base[e] + it.pos[e] for e in ENGINES}
                        if eng_has_ops(it.items, ename):
                            flush_pending()
                            if RELAXED:
                                if prev[3] is not None:
                                    emit_wait(prev[3], prev[4], prev[5])
                            elif prev[0] is not None:
                                emit_wait(prev[0], prev[1], prev[2])
                            _EXPR_CACHE.clear()
                            _loop_regs = []
                            _cb = nc.register_alloc_callback(
                                lambda *a: _loop_regs.extend(
                                    x for x in a if type(x).__name__ == "RegisterHandle"))
                            with eng.Fori(0, it.n) as kv:
                                kst = kstack + [(kv, it.per)]
                                for e in ENGINES:
                                    if not it.per[e]:
                                        continue
                                    eng.reg_mul(wscr2, kv, it.per[e])
                                    eng.reg_add(wscr2, wscr2, new_base[e])
                                    eng.wait_ge(sems[e], wscr2)
                                walk(it.items, new_base, kst, prev)
                            nc.unregister_alloc_callback(_cb)
                            for e in ENGINES:
                                last_static[e] = -10 ** 9
                            _EXPR_CACHE.clear()
                            try:
                                h = eng.lower_val(kv)
                                nc.free_register(h)
                            except Exception:
                                pass
                            for h in _loop_regs:
                                try:
                                    nc.free_register(h)
                                except Exception:
                                    pass
                        pending.append(
                            {e: new_base[e] + it.n * it.per[e]
                             for e in ENGINES if it.per[e]})
                        prev[0] = None
                        prev[1] = 0
                        prev[2] = []
                        prev[3] = None
                        prev[4] = 0
                        prev[5] = []

            walk(self.root, {e: 0 for e in ENGINES}, [], [None, 0, [], None, 0, []])
            flush_pending()
            for e in ENGINES:
                if e != ename and self._final[e] > last_static[e]:
                    eng.wait_ge(sems[e], self._final[e])

        @block.gpsimd
        def _(g):
            emit_stream(g, "g")

        @block.sync
        def _(sy):
            emit_stream(sy, "sy")

        @block.tensor
        def _(t):
            emit_stream(t, "t")

        @block.vector
        def _(v):
            emit_stream(v, "v")

        @block.scalar
        def _(s):
            emit_stream(s, "s")



def _rope_tables():
    i = np.arange(DH // 2, dtype=np.float32)
    theta = (10000.0 ** (-2.0 * (i - 1.0) / DH)).astype(np.float32)
    m = np.arange(T, dtype=np.float32)
    ang = (m[:, None] * theta[None, :]).astype(np.float32)
    c = np.ascontiguousarray(np.cos(ang).astype(np.float32).T)
    s = np.ascontiguousarray(np.sin(ang).astype(np.float32).T)
    cs = np.concatenate([c, s, c, s], axis=0)
    sc = np.concatenate([s, c, s, c], axis=0)
    return np.ascontiguousarray(cs), np.ascontiguousarray(sc)


def _causal_masks():
    kk = np.arange(128)[:, None]
    qq = np.arange(512)[None, :]
    masks = [(kk <= qq - 128 * j).astype(np.float32) for j in range(4)]
    return np.ascontiguousarray(np.concatenate(masks, axis=1))


def _pair_sliced(wT):
    w4 = wT.reshape(8, 128, 8, 128)
    out = w4.transpose(2, 1, 0, 3).reshape(1024, 1024)
    return np.ascontiguousarray(out)


def _chunk_rows(wT):
    n = wT.shape[1]
    w3 = wT.reshape(8, 128, n).transpose(1, 0, 2).reshape(128, 8 * n)
    return np.ascontiguousarray(out_arr(w3))


def out_arr(a):
    return np.ascontiguousarray(a)


def l0_attn_host_inputs(x_full, w_qkv, w_out, b_out, core):
    xT = np.ascontiguousarray(
        x_full[2 * core:2 * core + 2].transpose(0, 2, 1)).astype(np.float32)
    cs, sc = _rope_tables()
    return {
        "xT": xT,
        "wq_ps": _pair_sliced(np.ascontiguousarray(w_qkv[0:D].T)),
        "wk_ps": _pair_sliced(np.ascontiguousarray(w_qkv[D:2 * D].T)),
        "wv_ps": _pair_sliced(np.ascontiguousarray(w_qkv[2 * D:3 * D].T)),
        "wo_cr": _chunk_rows(np.ascontiguousarray(w_out.T)),
        "bout": np.ascontiguousarray(b_out.reshape(8, 128).T),
        "cs_tab": cs, "sc_tab": sc, "cmask": _causal_masks(),
    }



def emit_l0_attn(p, dr):
    nc = p.nc
    xT = dr["xT"]
    wq_ps = dr["wq_ps"]
    wk_ps = dr["wk_ps"]
    wv_ps = dr["wv_ps"]
    wo_cr = dr["wo_cr"]
    boutd = dr["bout"]
    cs_t = dr["cs_tab"]
    sc_t = dr["sc_tab"]
    cmaskd = dr["cmask"]
    x1 = dr["x1"]
    y_dram = nc.dram_tensor("y_dram", [D, T], F32).ap()

    with ExitStack() as ctx:
        def sb(name, shape, dt=BF16):
            return ctx.enter_context(nc.sbuf_tensor(name, shape, dt))

        def pt(name, shape):
            return ctx.enter_context(nc.psum_tensor(name, shape, F32))

        xb = sb("xb", [128, 8, T])
        wqs = sb("wqs", [128, 8, 128])
        wks = sb("wks", [128, 8, 128])
        wvs = sb("wvs", [128, 8, 128])
        wbuf = sb("wbuf", [128, 8, D])
        qp = sb("qp", [128, T]); kp = sb("kp", [128, T])
        qrp = sb("qrp", [128, T]); krp = sb("krp", [128, T])
        m1t = sb("m1t", [128, T]); m2t = sb("m2t", [128, T])
        rtmp = sb("rtmp", [128, T])
        vpair = sb("vpair", [128, T])
        ybufs = sb("ybufs", [128, 8, 512])
        cs_b = sb("cs_b", [128, T]); sc_b = sb("sc_b", [128, T])
        cm_b = sb("cm_b", [128, 2048])
        ones128 = sb("ones128", [128, 1])
        ones1 = sb("ones1", [1, 64])
        exp_s = sb("exp_s", [128, 512])
        exp_n = sb("exp_n", [128, 512])
        ysc = sb("ysc", [64, 512], F32)
        ypair1 = sb("ypair1", [128, T], F32)
        rden_bf = sb("rden_bf", [1, 512])
        bc_sb = sb("bc_sb", [64, 512])
        sum_bf = sb("sum_bf", [128, 8, 512])
        sq_bf = sb("sq_bf", [128, 512])
        bout_s = sb("bout_s", [128, 8], F32)
        xres = sb("xres", [128, 512], F32)
        sprev = sb("sprev", [128, 512], F32)
        outf = sb("outf", [128, 512], F32)
        rden_f = sb("rden_f", [1, 512], F32)
        mrow = sb("mrow", [1, 512], F32)
        m2row = sb("m2row", [1, 512], F32)
        varr = sb("varr", [1, 512], F32)
        mrow_bf = sb("mrow_bf", [1, 512])
        rstd_bf = sb("rstd_bf", [1, 512])
        mb_sb = sb("mb_sb", [128, 512], F32)
        rs_sb = sb("rs_sb", [128, 512], F32)
        dum = sb("dum", [1, 1], F32)

        psA = pt("psA", [128, 512])
        psB = pt("psB", [128, 512])
        psC = pt("psC", [128, 512])
        psD = pt("psD", [1, 512])
        psE = pt("psE", [1, 512])
        psY = pt("psY", [64, 512])

        p.g(lambda e, k, ks: e.dma_start(out=cs_b[:, :], in_=cs_t))
        p.g(lambda e, k, ks: e.dma_start(out=sc_b[:, :], in_=sc_t))
        p.g(lambda e, k, ks: e.dma_start(out=cm_b[:, :], in_=cmaskd))
        p.g(lambda e, k, ks: e.dma_start(out=bout_s[:, :], in_=boutd))
        p.v(lambda e, k, ks: e.memset(ones128[:, :], 1.0))
        p.v(lambda e, k, ks: e.memset(ones1[:, :], 1.0))

        for b in range(B_CORE):
            for dc in range(8):
                p.g(lambda e, k, ks, b=b, dc=dc: e.dma_start(
                    out=xb[:, dc, :], in_=xT[b, ds(dc * 128, 128), :]))

            with p.Loop(8):
                for wsl, wdr in ((wqs, wq_ps), (wks, wk_ps), (wvs, wv_ps)):
                    p.g(lambda e, k, ks, wsl=wsl, wdr=wdr: e.dma_start(
                        out=wsl[:, :, :].rearrange("p c f -> p (c f)"),
                        in_=wdr[ts(k, 128), :]))
                for tc in range(4):
                    for (wsl, dst, ps) in ((wqs, qp, psA), (wks, kp, psB)):
                        for dc in range(8):
                            p.t(lambda e, k, ks, wsl=wsl, ps=ps, dc=dc, tc=tc: e.matmul(
                                ps[:, :], wsl[:, dc, :], xb[:, dc, ds(tc * 512, 512)],
                                start=(dc == 0), stop=(dc == 7)))
                        p.v(lambda e, k, ks, dst=dst, ps=ps, tc=tc: e.tensor_copy(
                            dst[:, ds(tc * 512, 512)], ps[:, :]))
                for tt in range(16):
                    for dc in range(8):
                        p.t(lambda e, k, ks, dc=dc, tt=tt: e.matmul(
                            psC[:, 0:128], xb[:, dc, ds(tt * 128, 128)], wvs[:, dc, :],
                            start=(dc == 0), stop=(dc == 7)))
                    p.v(lambda e, k, ks, tt=tt: e.tensor_copy(
                        vpair[:, ds(tt * 128, 128)], psC[:, 0:128]))
                for (src, dst) in ((qp, qrp), (kp, krp)):
                    p.v(lambda e, k, ks, src=src: e.tensor_tensor(
                        m1t[:, :], src[:, :], cs_b[:, :], ALU.mult))
                    p.v(lambda e, k, ks, src=src: e.tensor_tensor(
                        m2t[:, :], src[:, :], sc_b[:, :], ALU.mult))
                    for h in range(2):
                        b0 = h * 64
                        p.v(lambda e, k, ks, b0=b0: e.tensor_copy(
                            rtmp[b0:b0 + 32, :], m1t[b0 + 32:b0 + 64, :]))
                        p.v(lambda e, k, ks, dst=dst, b0=b0: e.tensor_tensor(
                            dst[b0:b0 + 32, :], m1t[b0:b0 + 32, :],
                            rtmp[b0:b0 + 32, :], ALU.add))
                        p.v(lambda e, k, ks, b0=b0: e.tensor_copy(
                            rtmp[b0 + 32:b0 + 64, :], m2t[b0:b0 + 32, :]))
                        p.v(lambda e, k, ks, dst=dst, b0=b0: e.tensor_tensor(
                            dst[b0 + 32:b0 + 64, :], m2t[b0 + 32:b0 + 64, :],
                            rtmp[b0 + 32:b0 + 64, :], ALU.subtract))

                for h in range(2):
                    hr = h * 64
                    for qc in range(4):
                        for kc in range(16):
                            p.t(lambda e, k, ks, hr=hr, qc=qc, kc=kc: e.matmul(
                                psA[:, :], kp[hr:hr + 64, ds(kc * 128, 128)],
                                qp[hr:hr + 64, ds(qc * 512, 512)], start=True, stop=True))
                            p.s(lambda e, k, ks: e.activation(
                                exp_s[:, :], psA[:, :], AF.Exp, scale=0.125))
                            p.t(lambda e, k, ks, kc=kc: e.matmul(
                                psD[:, :], ones128[:, :], exp_s[:, :],
                                start=(kc == 0), stop=(kc == 15)))
                            if kc <= 4 * qc + 3:
                                p.t(lambda e, k, ks, hr=hr, qc=qc, kc=kc: e.matmul(
                                    psA[:, :], krp[hr:hr + 64, ds(kc * 128, 128)],
                                    qrp[hr:hr + 64, ds(qc * 512, 512)],
                                    start=True, stop=True))
                                p.s(lambda e, k, ks: e.activation(
                                    exp_n[:, :], psA[:, :], AF.Exp, scale=0.125))
                                if kc >= 4 * qc:
                                    j = kc - 4 * qc
                                    p.v(lambda e, k, ks, j=j: e.tensor_tensor(
                                        exp_n[:, :], exp_n[:, :],
                                        cm_b[:, ds(j * 512, 512)], ALU.mult))
                                p.t(lambda e, k, ks, h=h, kc=kc: e.matmul(
                                    psY[:, :], vpair[:, ds(kc * 128 + h * 64, 64)],
                                    exp_n[:, :], start=(kc == 0), stop=False))
                        p.v(lambda e, k, ks: e.reciprocal(rden_f[:, :], psD[:, :]))
                        p.v(lambda e, k, ks: e.tensor_copy(rden_bf[:, :], rden_f[:, :]))
                        p.t(lambda e, k, ks: e.matmul(
                            psB[0:64, :], ones1[:, :], rden_bf[:, :], start=True, stop=True))
                        p.v(lambda e, k, ks: e.tensor_copy(bc_sb[:, :], psB[0:64, :]))
                        p.v(lambda e, k, ks, h=h, qc=qc: e.tensor_tensor(
                            ypair1[h * 64:h * 64 + 64, ds(qc * 512, 512)],
                            psY[:, :], bc_sb[:, :], ALU.mult))
                p.sy(lambda e, k, ks: e.dma_start(
                    out=y_dram[ts(k, 128), :], in_=ypair1[:, :]))

            p.g(lambda e, k, ks: e.dma_start(
                out=wbuf[:, :, :].rearrange("p c f -> p (c f)"), in_=wo_cr))
            for tc in range(4):
                for dc in range(8):
                    p.g(lambda e, k, ks, dc=dc, tc=tc: e.dma_start(
                        out=ybufs[:, dc, :],
                        in_=y_dram[ds(dc * 128, 128), ds(tc * 512, 512)]))
                for oc in range(8):
                    for dc in range(8):
                        p.t(lambda e, k, ks, dc=dc, oc=oc, tc=tc: e.matmul(
                            psA[:, :], wbuf[:, dc, ds(oc * 128, 128)],
                            ybufs[:, dc, :],
                            start=(dc == 0), stop=(dc == 7)))
                    p.g(lambda e, k, ks, b=b, oc=oc, tc=tc: e.dma_start(
                        out=xres[:, :], in_=xT[b, ds(oc * 128, 128), ds(tc * 512, 512)]))
                    p.v(lambda e, k, ks, oc=oc: e.tensor_scalar(
                        sprev[:, :], psA[:, :], bout_s[:, oc:oc + 1], None, ALU.add))
                    p.v(lambda e, k, ks: e.tensor_tensor(
                        sprev[:, :], sprev[:, :], xres[:, :], ALU.add))
                    p.v(lambda e, k, ks, oc=oc: e.tensor_copy(sum_bf[:, oc, :], sprev[:, :]))
                    p.s(lambda e, k, ks: e.activation(sq_bf[:, :], sprev[:, :], AF.Square))
                    p.t(lambda e, k, ks, oc=oc: e.matmul(
                        psD[:, :], ones128[:, :], sum_bf[:, oc, :],
                        start=(oc == 0), stop=(oc == 7)))
                    p.t(lambda e, k, ks, oc=oc: e.matmul(
                        psE[:, :], ones128[:, :], sq_bf[:, :],
                        start=(oc == 0), stop=(oc == 7)))
                p.v(lambda e, k, ks: e.tensor_scalar(
                    mrow[:, :], psD[:, :], 1.0 / D, None, ALU.mult))
                p.v(lambda e, k, ks: e.tensor_scalar(
                    m2row[:, :], psE[:, :], 1.0 / D, None, ALU.mult))
                p.v(lambda e, k, ks: e.tensor_tensor(
                    varr[:, :], mrow[:, :], mrow[:, :], ALU.mult))
                p.v(lambda e, k, ks: e.tensor_tensor(
                    varr[:, :], m2row[:, :], varr[:, :], ALU.subtract))
                p.v(lambda e, k, ks: e.tensor_scalar(varr[:, :], varr[:, :], 1e-5, None, ALU.add))
                p.s(lambda e, k, ks: e.activation(varr[:, :], varr[:, :], AF.Sqrt))
                p.v(lambda e, k, ks: e.reciprocal(varr[:, :], varr[:, :]))
                p.v(lambda e, k, ks: e.tensor_copy(mrow_bf[:, :], mrow[:, :]))
                p.v(lambda e, k, ks: e.tensor_copy(rstd_bf[:, :], varr[:, :]))
                p.t(lambda e, k, ks: e.matmul(
                    psB[0:64, :], ones1[:, :], mrow_bf[:, :], start=True, stop=True))
                p.t(lambda e, k, ks: e.matmul(
                    psC[0:64, :], ones1[:, :], rstd_bf[:, :], start=True, stop=True))
                p.v(lambda e, k, ks: e.tensor_copy(mb_sb[0:64, :], psB[0:64, :]))
                p.v(lambda e, k, ks: e.tensor_copy(rs_sb[0:64, :], psC[0:64, :]))
                p.v(lambda e, k, ks: e.tensor_copy(mb_sb[64:128, :], psB[0:64, :]))
                p.v(lambda e, k, ks: e.tensor_copy(rs_sb[64:128, :], psC[0:64, :]))
                for oc in range(8):
                    p.v(lambda e, k, ks, oc=oc: e.tensor_tensor(
                        outf[:, :], sum_bf[:, oc, :], mb_sb[:, :], ALU.subtract))
                    p.v(lambda e, k, ks: e.tensor_tensor(
                        outf[:, :], outf[:, :], rs_sb[:, :], ALU.mult))
                    p.sy(lambda e, k, ks, b=b, oc=oc, tc=tc: e.dma_start(
                        out=x1[b, ds(oc * 128, 128), ds(tc * 512, 512)], in_=outf[:, :]))

        p.gc(lambda e, k, ks: e.memset(dum[:, :], 0.0))



def build_l0_attn():
    p = Prog()
    nc = p.nc
    xT = nc.dram_tensor("xT", [B_CORE, D, T], F32, kind="ExternalInput").ap()
    wq_ps = nc.dram_tensor("wq_ps", [D, D], F32, kind="ExternalInput").ap()
    wk_ps = nc.dram_tensor("wk_ps", [D, D], F32, kind="ExternalInput").ap()
    wv_ps = nc.dram_tensor("wv_ps", [D, D], F32, kind="ExternalInput").ap()
    wo_cr = nc.dram_tensor("wo_cr", [128, 8 * D], F32, kind="ExternalInput").ap()
    boutd = nc.dram_tensor("bout", [128, 8], F32, kind="ExternalInput").ap()
    cs_t = nc.dram_tensor("cs_tab", [128, T], F32, kind="ExternalInput").ap()
    sc_t = nc.dram_tensor("sc_tab", [128, T], F32, kind="ExternalInput").ap()
    cmaskd = nc.dram_tensor("cmask", [128, 2048], F32, kind="ExternalInput").ap()
    x1 = nc.dram_tensor("x1", [B_CORE, D, T], F32, kind="ExternalOutput").ap()
    y_dram = nc.dram_tensor("y_dram", [D, T], F32).ap()

    dr = {"xT": xT,
          "wq_ps": wq_ps,
          "wk_ps": wk_ps,
          "wv_ps": wv_ps,
          "wo_cr": wo_cr,
          "bout": boutd,
          "cs_tab": cs_t,
          "sc_tab": sc_t,
          "cmask": cmaskd,
          "x1": x1}
    emit_l0_attn(p, dr)
    with ExitStack() as _ctx2:
        p.emit(_ctx2)
    return p.nc



def l0_ffn_host_inputs(x1_unused, inputs, core):
    wg = inputs["L0_ffn_w_gate"]; wu = inputs["L0_ffn_w_up"]; wd = inputs["L0_ffn_w_down"]
    sg = inputs["L0_ffn_shared_gate_w"]; su = inputs["L0_ffn_shared_up_w"]
    sd = inputs["L0_ffn_shared_down_w"]
    def cr(mat):
        return mat.reshape(8, 128, 1024).transpose(1, 0, 2).reshape(128, 8192)
    wg_cr = np.stack([cr(np.ascontiguousarray(wg[e].T)) for e in range(16)]
                     + [cr(np.ascontiguousarray(sg.T))], 0).reshape(17 * 128, 8192)
    wu_cr = np.stack([cr(np.ascontiguousarray(wu[e].T)) for e in range(16)]
                     + [cr(np.ascontiguousarray(su.T))], 0).reshape(17 * 128, 8192)
    wd_cr = np.stack([cr(np.ascontiguousarray(wd[e].T)) for e in range(16)]
                     + [cr(np.ascontiguousarray(sd.T))], 0).reshape(17 * 128, 8192)
    gwT = np.ascontiguousarray(inputs["L0_ffn_gate_w"].T)
    gw_cr = gwT.reshape(8, 128, 16).transpose(1, 0, 2).reshape(128, 128)
    ebias = np.ascontiguousarray(
        np.broadcast_to(inputs["L0_ffn_expert_bias"][None, :], (128, 16)).copy())
    return {"wg_cr": np.ascontiguousarray(wg_cr.astype(np.float32)),
            "wu_cr": np.ascontiguousarray(wu_cr.astype(np.float32)),
            "wd_cr": np.ascontiguousarray(wd_cr.astype(np.float32)),
            "gw_cr": np.ascontiguousarray(gw_cr.astype(np.float32)),
            "ebias": ebias.astype(np.float32)}


TT = 4096


def emit_l0_ffn(p, dr):
    nc = p.nc
    x1d = dr["x1"]
    wg_cr = dr["wg_cr"]
    wu_cr = dr["wu_cr"]
    wd_cr = dr["wd_cr"]
    gw_crd = dr["gw_cr"]
    ebiasd = dr["ebias"]
    x2 = dr["x2"]
    comb_dram = nc.dram_tensor("comb_dram", [17, 4096], F32).ap()
    y_dram = nc.dram_tensor("y_dram2", [128, 8 * 4096], F32).ap()

    with ExitStack() as ctx:
        def sb(name, shape, dt=BF16):
            return ctx.enter_context(nc.sbuf_tensor(name, shape, dt))

        def pt(name, shape):
            return ctx.enter_context(nc.psum_tensor(name, shape, F32))

        xc = sb("xc", [128, 8, TT])
        wgs = sb("wgs", [128, 8, 1024])
        wus = sb("wus", [128, 8, 1024])
        wds = sb("wds", [128, 8, 1024])
        gws = sb("gws", [128, 8, 16])
        hbuf = sb("hbuf", [128, 4, 1024])
        g_sb = sb("g_sb", [16, TT])
        idtb = sb("idtb", [128, 128])
        gtok = sb("gtok", [128, 32 * 16], F32)
        comb_all = sb("comb_all", [128, 32 * 16], F32)
        gwork = sb("gwork", [128, 16], F32)
        ctmp = sb("ctmp", [16, 128], F32)
        ysc = sb("ysc2", [128, 512], F32)
        scores = sb("scores", [128, 16], F32)
        sel = sb("sel", [128, 16], F32)
        msum = sb("msum", [128, 16], F32)
        mx1 = sb("mx1", [128, 1], F32)
        rs1 = sb("rs1", [128, 1], F32)
        ebias_s = sb("ebias_s", [128, 16], F32)
        ce_sb = sb("ce_sb", [1, 4096], F32)
        ce_bf = sb("ce_bf", [1, 4096])
        ce_bc = sb("ce_bc", [128, 512])
        htmp = sb("htmp", [128, 1024])
        hfin = sb("hfin", [128, 512])
        idt = sb("idt", [128, 128], F32)
        ones1_128 = sb("ones1_128", [1, 128])
        ones128 = sb("ones128b", [128, 1])
        sum_bf = sb("sum_bf2", [128, 8, 512])
        sq_bf = hfin
        xres = sb("xres2", [128, 512], F32)
        sprev = sb("sprev2", [128, 512], F32)
        outf = sb("outf2", [128, 512], F32)
        mrow = sb("mrow2", [1, 512], F32)
        m2row = sb("m2row2", [1, 512], F32)
        varr = sb("varr2", [1, 512], F32)
        mrow_bf = sb("mrow_bf2", [1, 512])
        rstd_bf = sb("rstd_bf2", [1, 512])
        mb_sb = sb("mb_sb2", [128, 512])
        rs_sb = sb("rs_sb2", [128, 512])
        ones1 = sb("ones1b", [1, 64])
        dum = sb("dum2", [1, 1], F32)

        psG = pt("psG", [128, 1024])
        psU = pt("psU", [128, 1024])
        psY2 = pt("psY2", [128, 512])
        psB2 = psY2
        psS = pt("psS", [1, 512])
        psT = ctx.enter_context(nc.psum_tensor("psT", [128, 16], BF16))
        psQ = pt("psQ", [1, 512])

        p.g(lambda e, k, ks: e.dma_start(out=gws[:, :, :].rearrange("p c f -> p (c f)"), in_=gw_crd))
        p.g(lambda e, k, ks: e.dma_start(out=ebias_s[:, :], in_=ebiasd))
        p.v(lambda e, k, ks: e.memset(ones1_128[:, :], 1.0))
        p.v(lambda e, k, ks: e.memset(ones128[:, :], 1.0))
        p.v(lambda e, k, ks: e.memset(ones1[:, :], 1.0))
        p.gc(lambda e, k, ks: e.iota(idt[:, :], pattern=[[-1, 128]], base=0,
                                     channel_multiplier=1,
                                     allow_small_or_imprecise_dtypes=True))
        p.v(lambda e, k, ks: e.tensor_scalar(idt[:, :], idt[:, :], 0.0, None, ALU.is_equal))
        p.v(lambda e, k, ks: e.tensor_copy(idtb[:, :], idt[:, :]))
        for b in range(B_CORE):
            for dc in range(8):
                p.g(lambda e, k, ks, b=b, dc=dc: e.dma_start(
                    out=xc[:, dc, ds(b * T, T)], in_=x1d[b, ds(dc * 128, 128), :]))
        p.v(lambda e, k, ks: e.memset(ce_sb[:, :], 1.0))
        p.sy(lambda e, k, ks: e.dma_start(out=comb_dram[16:17, :], in_=ce_sb[:, :]))

        with p.Loop(8):
            for dc in range(8):
                p.t(lambda e, k, ks, dc=dc: e.matmul(
                    psB2[0:16, :], gws[:, dc, :], xc[:, dc, ts(k, 512)],
                    start=(dc == 0), stop=(dc == 7)))
            p.v(lambda e, k, ks: e.tensor_copy(g_sb[:, ts(k, 512)], psB2[0:16, :]))
        for tile in range(32):
            p.t(lambda e, k, ks, tile=tile: e.transpose(
                psT[:, :], g_sb[:, ds(tile * 128, 128)], idtb[0:16, 0:16]))
            p.v(lambda e, k, ks, tile=tile: e.tensor_copy(
                gtok[:, ds(tile * 16, 16)], psT[:, :]))
        with p.Loop(32):
            p.v(lambda e, k, ks: e.tensor_copy(gwork[:, :], gtok[:, ts(k, 16)]))
            p.s(lambda e, k, ks: e.activation(scores[:, :], gwork[:, :], AF.Sigmoid))
            p.v(lambda e, k, ks: e.tensor_tensor(sel[:, :], scores[:, :], ebias_s[:, :], ALU.add))
            p.v(lambda e, k, ks: e.memset(msum[:, :], 0.0))
            for i in range(8):
                p.v(lambda e, k, ks: e.tensor_reduce(mx1[:, :], sel[:, :], AX.X, ALU.max))
                p.v(lambda e, k, ks: e.tensor_scalar(gwork[:, :], sel[:, :], mx1[:, 0:1], None, ALU.is_ge))
                p.v(lambda e, k, ks: e.tensor_tensor(msum[:, :], msum[:, :], gwork[:, :], ALU.add))
                p.v(lambda e, k, ks: e.scalar_tensor_tensor(
                    sel[:, :], gwork[:, :], -1e30, sel[:, :], ALU.mult, ALU.add))
            p.v(lambda e, k, ks: e.tensor_tensor(gwork[:, :], scores[:, :], msum[:, :], ALU.mult))
            p.v(lambda e, k, ks: e.tensor_reduce(rs1[:, :], gwork[:, :], AX.X, ALU.add))
            p.v(lambda e, k, ks: e.reciprocal(rs1[:, :], rs1[:, :]))
            p.v(lambda e, k, ks: e.tensor_scalar(gwork[:, :], gwork[:, :], rs1[:, 0:1], 2.826, ALU.mult, op1=ALU.mult))
            p.v(lambda e, k, ks: e.tensor_copy(comb_all[:, ts(k, 16)], gwork[:, :]))
        for tile in range(32):
            p.t(lambda e, k, ks, tile=tile: e.transpose(
                psB2[0:16, 0:128], comb_all[:, ds(tile * 16, 16)], idt[:, :]))
            p.v(lambda e, k, ks, tile=tile: e.tensor_copy(
                ctmp[:, :], psB2[0:16, 0:128]))
            p.sy(lambda e, k, ks, tile=tile: e.dma_start(
                out=comb_dram[0:16, ds(tile * 128, 128)], in_=ctmp[:, :]))

        p.v(lambda e, k, ks: e.memset(ysc[:, :], 0.0))
        for oc in range(8):
            for tcg in range(8):
                p.sy(lambda e, k, ks, oc=oc, tcg=tcg: e.dma_start(
                    out=y_dram[:, ds(oc * 4096 + tcg * 512, 512)], in_=ysc[:, :]))

        with p.Loop(17):
            p.g(lambda e, k, ks: e.dma_start(
                out=wgs[:, :, :].rearrange("p c f -> p (c f)"), in_=wg_cr[ts(k, 128), :]))
            p.g(lambda e, k, ks: e.dma_start(
                out=wus[:, :, :].rearrange("p c f -> p (c f)"), in_=wu_cr[ts(k, 128), :]))
            p.g(lambda e, k, ks: e.dma_start(
                out=wds[:, :, :].rearrange("p c f -> p (c f)"), in_=wd_cr[ts(k, 128), :]))
            p.g(lambda e, k, ks: e.dma_start(out=ce_sb[:, :], in_=comb_dram[ts(k, 1), :]))
            p.v(lambda e, k, ks: e.tensor_copy(ce_bf[:, :], ce_sb[:, :]))
            for tcg in range(8):
                t0 = tcg * 512
                p.t(lambda e, k, ks, t0=t0: e.matmul(
                    psB2[:, :], ones1_128[:, :], ce_bf[:, ds(t0, 512)],
                    start=True, stop=True))
                p.v(lambda e, k, ks: e.tensor_copy(ce_bc[:, :], psB2[:, :]))
                for pr in range(4):
                    for h in range(2):
                        ic = 2 * pr + h
                        for dc in range(8):
                            p.t(lambda e, k, ks, dc=dc, ic=ic, h=h, t0=t0: e.matmul(
                                psG[:, ds(h * 512, 512)], wgs[:, dc, ds(ic * 128, 128)],
                                xc[:, dc, ds(t0, 512)], start=(dc == 0), stop=(dc == 7)))
                        for dc in range(8):
                            p.t(lambda e, k, ks, dc=dc, ic=ic, h=h, t0=t0: e.matmul(
                                psU[:, ds(h * 512, 512)], wus[:, dc, ds(ic * 128, 128)],
                                xc[:, dc, ds(t0, 512)], start=(dc == 0), stop=(dc == 7)))
                    p.s(lambda e, k, ks: e.activation(htmp[:, :], psG[:, :], AF.Silu))
                    p.v(lambda e, k, ks, pr=pr: e.tensor_tensor(
                        hbuf[:, pr, :], htmp[:, :], psU[:, :], ALU.mult))
                for oc in range(8):
                    for ic in range(8):
                        p.t(lambda e, k, ks, ic=ic, oc=oc: e.matmul(
                            psY2[:, :], wds[:, ic, ds(oc * 128, 128)],
                            hbuf[:, ic // 2, ds((ic % 2) * 512, 512)],
                            start=(ic == 0), stop=(ic == 7)))
                    p.v(lambda e, k, ks: e.tensor_tensor(
                        ysc[:, :], psY2[:, :], ce_bc[:, :], ALU.mult))
                    p.g(lambda e, k, ks, oc=oc, t0=t0: e.dma_start(
                        out=y_dram[:, ds(oc * 4096 + t0, 512)], in_=ysc[:, :],
                        accum_op=ALU.add))

        for tcg in range(8):
            t0 = tcg * 512
            b = t0 // T
            toff = t0 % T
            for oc in range(8):
                p.g(lambda e, k, ks, oc=oc, t0=t0: e.dma_start(
                    out=sprev[:, :], in_=y_dram[:, ds(oc * 4096 + t0, 512)]))
                p.g(lambda e, k, ks, b=b, oc=oc, toff=toff: e.dma_start(
                    out=xres[:, :], in_=x1d[b, ds(oc * 128, 128), ds(toff, 512)]))
                p.v(lambda e, k, ks: e.tensor_tensor(sprev[:, :], sprev[:, :], xres[:, :], ALU.add))
                p.v(lambda e, k, ks, oc=oc: e.tensor_copy(sum_bf[:, oc, :], sprev[:, :]))
                p.s(lambda e, k, ks: e.activation(sq_bf[:, :], sprev[:, :], AF.Square))
                p.t(lambda e, k, ks, oc=oc: e.matmul(
                    psS[:, :], ones128[:, :], sum_bf[:, oc, :],
                    start=(oc == 0), stop=(oc == 7)))
                p.t(lambda e, k, ks, oc=oc: e.matmul(
                    psQ[:, :], ones128[:, :], sq_bf[:, :],
                    start=(oc == 0), stop=(oc == 7)))
            p.v(lambda e, k, ks: e.tensor_scalar(mrow[:, :], psS[:, :], 1.0 / D, None, ALU.mult))
            p.v(lambda e, k, ks: e.tensor_scalar(m2row[:, :], psQ[:, :], 1.0 / D, None, ALU.mult))
            p.v(lambda e, k, ks: e.tensor_tensor(varr[:, :], mrow[:, :], mrow[:, :], ALU.mult))
            p.v(lambda e, k, ks: e.tensor_tensor(varr[:, :], m2row[:, :], varr[:, :], ALU.subtract))
            p.v(lambda e, k, ks: e.tensor_scalar(varr[:, :], varr[:, :], 1e-5, None, ALU.add))
            p.s(lambda e, k, ks: e.activation(varr[:, :], varr[:, :], AF.Sqrt))
            p.v(lambda e, k, ks: e.reciprocal(varr[:, :], varr[:, :]))
            p.v(lambda e, k, ks: e.tensor_copy(mrow_bf[:, :], mrow[:, :]))
            p.v(lambda e, k, ks: e.tensor_copy(rstd_bf[:, :], varr[:, :]))
            p.t(lambda e, k, ks: e.matmul(psB2[0:64, :], ones1[:, :], mrow_bf[:, :], start=True, stop=True))
            p.t(lambda e, k, ks: e.matmul(psG[0:64, 0:512], ones1[:, :], rstd_bf[:, :], start=True, stop=True))
            p.v(lambda e, k, ks: e.tensor_copy(mb_sb[0:64, :], psB2[0:64, :]))
            p.v(lambda e, k, ks: e.tensor_copy(rs_sb[0:64, :], psG[0:64, 0:512]))
            p.v(lambda e, k, ks: e.tensor_copy(mb_sb[64:128, :], psB2[0:64, :]))
            p.v(lambda e, k, ks: e.tensor_copy(rs_sb[64:128, :], psG[0:64, 0:512]))
            for oc in range(8):
                p.v(lambda e, k, ks, oc=oc: e.tensor_tensor(
                    outf[:, :], sum_bf[:, oc, :], mb_sb[:, :], ALU.subtract))
                p.v(lambda e, k, ks: e.tensor_tensor(outf[:, :], outf[:, :], rs_sb[:, :], ALU.mult))
                p.sy(lambda e, k, ks, b=b, oc=oc, toff=toff: e.dma_start(
                    out=x2[b, ds(oc * 128, 128), ds(toff, 512)], in_=outf[:, :]))

        p.gc(lambda e, k, ks: e.memset(dum[:, :], 0.0))


def build_l0_ffn():
    p = Prog()
    nc = p.nc
    x1d = nc.dram_tensor("x1", [B_CORE, D, T], F32, kind="ExternalInput").ap()
    wg_cr = nc.dram_tensor("wg_cr", [17 * 128, 8192], F32, kind="ExternalInput").ap()
    wu_cr = nc.dram_tensor("wu_cr", [17 * 128, 8192], F32, kind="ExternalInput").ap()
    wd_cr = nc.dram_tensor("wd_cr", [17 * 128, 8192], F32, kind="ExternalInput").ap()
    gw_crd = nc.dram_tensor("gw_cr", [128, 128], F32, kind="ExternalInput").ap()
    ebiasd = nc.dram_tensor("ebias", [128, 16], F32, kind="ExternalInput").ap()
    x2 = nc.dram_tensor("x2", [B_CORE, D, T], F32, kind="ExternalOutput").ap()
    comb_dram = nc.dram_tensor("comb_dram", [17, 4096], F32).ap()
    y_dram = nc.dram_tensor("y_dram2", [128, 8 * 4096], F32).ap()


    TT = 4096

    dr = {"x1": x1d,
          "wg_cr": wg_cr,
          "wu_cr": wu_cr,
          "wd_cr": wd_cr,
          "gw_cr": gw_crd,
          "ebias": ebiasd,
          "x2": x2}
    emit_l0_ffn(p, dr)
    with ExitStack() as _ctx2:
        p.emit(_ctx2)
    return p.nc



def _l1_band_masks():
    kk = np.arange(128)[:, None]
    qq = np.arange(128)[None, :]
    blocks = []
    for dlt in (-1, 0, 1):
        rel = kk + 128 * dlt - qq
        blocks.append((np.abs(rel) <= 3).astype(np.float32))
    m = np.concatenate(blocks, axis=1)
    mL = m.copy(); mL[:, 0:128] = 0.0
    mR = m.copy(); mR[:, 256:384] = 0.0
    return np.ascontiguousarray(np.concatenate([m, mL, mR], axis=1))


def _edge_weights():
    wh = np.zeros((8, 3), np.float32)
    for q in range(3):
        for r in range(6):
            wh[r, q] = 1.0 if r <= q + 3 else 0.0
        wh[6, q] = 3 - q
    wt = np.zeros((8, 3), np.float32)
    for j in range(3):
        for r in range(6):
            wt[r, j] = 1.0 if r >= j else 0.0
        wt[6, j] = j + 1
    return np.ascontiguousarray(wh), np.ascontiguousarray(wt)


def l1_attn_host_inputs(inputs):
    Wq = inputs["L1_attn_W_qkv"]
    wh, wt = _edge_weights()
    b = inputs["L1_attn_b_qkv"]
    return {
        "wq1": _pair_sliced(np.ascontiguousarray(Wq[0:D].T)),
        "wk1": _pair_sliced(np.ascontiguousarray(Wq[D:2 * D].T)),
        "wv1": _pair_sliced(np.ascontiguousarray(Wq[2 * D:3 * D].T)),
        "bq1": np.ascontiguousarray(b[0:D].reshape(8, 128).T),
        "bk1": np.ascontiguousarray(b[D:2 * D].reshape(8, 128).T),
        "bv1": np.ascontiguousarray(b[2 * D:3 * D].reshape(8, 128).T),
        "wlatT": np.ascontiguousarray(inputs["L1_attn_W_lat"].T),
        "blat": np.ascontiguousarray(inputs["L1_attn_b_lat"].reshape(64, 1)),
        "wproj": _chunk_rows(np.ascontiguousarray(inputs["L1_attn_W_proj"].T)),
        "bproj": np.ascontiguousarray(inputs["L1_attn_b_proj"].reshape(8, 128).T),
        "wg1": _chunk_rows(np.ascontiguousarray(inputs["L1_ffn_Wg"])),
        "bands": _l1_band_masks(),
        "ewh": wh, "ewt": wt,
        "U128": np.triu(np.ones((128, 128), np.float32)),
        "U16s": np.triu(np.ones((16, 16), np.float32), 1),
        "U32s": np.triu(np.ones((32, 32), np.float32), 1),
    }


def emit_l1_attn(p, dr):
    nc = p.nc
    x2d = dr["x2"]
    wq1 = dr["wq1"]
    wk1 = dr["wk1"]
    wv1 = dr["wv1"]
    bq1 = dr["bq1"]
    bk1 = dr["bk1"]
    bv1 = dr["bv1"]
    wlatT = dr["wlatT"]
    blatd = dr["blat"]
    wproj = dr["wproj"]
    bprojd = dr["bproj"]
    wg1 = dr["wg1"]
    bandsd = dr["bands"]
    ewhd = dr["ewh"]
    ewtd = dr["ewt"]
    U128d = dr["U128"]
    U16sd = dr["U16s"]
    U32sd = dr["U32s"]
    x3bf = dr["x3bf"]
    hist_o = dr["hist"]
    gates_o = dr["gates"]
    ohT_o = dr["ohT"]
    locT_o = dr["locT"]
    slots_o = dr["slots"]
    t2e_o = dr["t2e"]
    y_dram = nc.dram_tensor("y_dram3", [D, T], F32).ap()
    cb_dram = nc.dram_tensor("cb_dram", [32, 16], F32).ap()

    with ExitStack() as ctx:
        def sb(name, shape, dt=BF16):
            return ctx.enter_context(nc.sbuf_tensor(name, shape, dt))

        def pt(name, shape, dt=F32):
            return ctx.enter_context(nc.psum_tensor(name, shape, dt))

        xb = sb("xb3", [128, 8, T])
        wqs = sb("wqs3", [128, 8, 128])
        wks = sb("wks3", [128, 8, 128])
        wvs = sb("wvs3", [128, 8, 128])
        wbuf = sb("wbuf3", [128, 8, D])
        x3sb = sb("x3sb", [128, 8, 4096])
        qp = sb("qp3", [128, T]); kp = sb("kp3", [128, T]); vp = sb("vp3", [128, T])
        klat = sb("klat3", [128, T])
        vlt = sb("vlt3", [128, 16 * 64])
        wlat_s = sb("wlat_s", [128, 64])
        qe3 = sb("qe3", [64, 3])
        blat_s = sb("blat_s", [64, 1], F32)
        blat_r = sb("blat_r", [1, 64])
        bq_s = sb("bq_s", [128, 8], F32)
        bk_s = sb("bk_s", [128, 8], F32)
        bv_s = sb("bv_s", [128, 8], F32)
        bproj_s = sb("bproj_s", [128, 8], F32)
        bands = sb("bands_s", [128, 1152])
        mexp = sb("mexp", [128, 384])
        denr = sb("denr", [1, 384], F32)
        rden = sb("rden3", [1, 128], F32)
        rdenb = sb("rdenb3", [1, 128])
        bcy = sb("bcy3", [64, 128])
        ysc = sb("ysc3", [64, 128], F32)
        y_pair = sb("y_pair", [128, T], F32)
        ones128 = sb("ones128c", [128, 1])
        ones1 = sb("ones1c", [1, 64])
        ones8 = sb("ones8", [8, 1])
        ones16c = sb("ones16c", [16, 1])
        onesf1 = sb("onesf1", [1, 128], F32)
        ones1t = sb("ones1t", [1, 128])
        ke = sb("ke", [64, 8])
        ve = sb("ve", [8, 64])
        ewh_s = sb("ewh_s", [8, 3], F32)
        ewt_s = sb("ewt_s", [8, 3], F32)
        ee = sb("ee", [8, 3], F32)
        web = sb("web", [8, 3])
        rde = sb("rde", [1, 3], F32)
        rdeb = sb("rdeb", [1, 3])
        bce = sb("bce", [64, 3])
        ysce = sb("ysce", [64, 3], F32)
        ybufs = sb("ybufs3", [128, 8, 512])
        sum_bf = sb("sum_bf3", [128, 8, 512])
        sq_bf = sb("sq_bf3", [128, 512])
        xres = sb("xres3", [128, 512], F32)
        sprev = sb("sprev3", [128, 512], F32)
        outf = sb("outf3", [128, 512], F32)
        mrow = sb("mrow3", [1, 512], F32)
        m2row = sb("m2row3", [1, 512], F32)
        varr = sb("varr3", [1, 512], F32)
        mrow_bf = sb("mrow_bf3", [1, 512])
        rstd_bf = sb("rstd_bf3", [1, 512])
        mb_sb = sb("mb_sb3", [128, 512])
        rs_sb = sb("rs_sb3", [128, 512])
        idt = sb("idt3", [128, 128], F32)
        idb = sb("idb3", [128, 128])
        xtb = sb("xtb", [128, 128])
        wgs = sb("wgs3", [128, 8, 16])
        U128b = sb("U128b", [128, 128])
        U16sb = sb("U16sb", [16, 16])
        oh_tok = sb("oh_tok", [128, 16])
        oh_all = sb("oh_all", [128, 512])
        nmx = sb("nmx", [128, 1], F32)
        mx1 = sb("mx13", [128, 1], F32)
        el = sb("el3", [128, 16], F32)
        den1 = sb("den13", [128, 1], F32)
        gsmat = sb("gsmat", [128, 32], F32)
        rkmat = sb("rkmat", [128, 32], F32)
        cumT = sb("cumT", [16, 4096], F32)
        ohst = sb("ohst", [16, 128])
        ct16 = sb("ct16", [128, 16], F32)
        rk1 = sb("rk1", [128, 1], F32)
        tot = sb("tot", [16, 32], F32)
        totf = sb("totf", [32, 16], F32)
        U32f = sb("U32f", [32, 32], F32)
        baseC = sb("baseC", [16, 32], F32)
        hist_s = sb("hist_s", [16, 1], F32)
        pada = sb("pada", [16, 1], F32)
        io33 = sb("io33", [16, 33], F32)
        cmp33 = sb("cmp33", [16, 33], F32)
        qcol = sb("qcol", [16, 1], F32)
        padc = sb("padc", [16, 1], F32)
        padb = sb("padb", [16, 1])
        segb = sb("segb", [1, 16], F32)
        segc = sb("segc", [16, 1], F32)
        cbmat = sb("cbmat", [16, 32], F32)
        cbrow = sb("cbrow", [1, 16], F32)
        gath = sb("gath", [128, 16], F32)
        slotc = sb("slotc", [128, 1], F32)
        slots_f = sb("slots_f", [128, 32], F32)
        slots_i = sb("slots_i", [128, 32], I32)
        io48 = sb("io48", [16, 48], F32)
        cmp48 = sb("cmp48", [16, 48])
        t2er = sb("t2er", [1, 48], F32)
        t2ei = sb("t2ei", [1, 48], I32)
        gtmp = sb("gtmp", [32, 128], F32)
        dum = sb("dum3", [1, 1], F32)

        psA = pt("psA3", [128, 512])
        psB = pt("psB3", [128, 512])
        psC = pt("psC3", [128, 512])
        psE3 = pt("psE3", [128, 384])
        psY3 = pt("psY3", [64, 128])
        psD3 = pt("psD3", [1, 384])
        psS = pt("psS3", [1, 512])
        psQ = pt("psQ3", [1, 512])
        psAb = psA.bitcast(BF16)
        psYb = psY3.bitcast(BF16)

        for dst, srcd in ((bands, bandsd), (bq_s, bq1), (bk_s, bk1), (bv_s, bv1),
                          (bproj_s, bprojd), (blat_s, blatd),
                          (ewh_s, ewhd), (ewt_s, ewtd), (U128b, U128d),
                          (U16sb, U16sd)):
            p.g(lambda e, k, ks, dst=dst, srcd=srcd: e.dma_start(out=dst[:, :], in_=srcd))
        p.g(lambda e, k, ks: e.dma_start(out=wlat_s[0:64, :], in_=wlatT))
        p.g(lambda e, k, ks: e.dma_start(out=wlat_s[64:128, :], in_=wlatT))
        p.g(lambda e, k, ks: e.dma_start(out=blat_r[:, :], in_=blatd.rearrange("a b -> b a")))
        p.g(lambda e, k, ks: e.dma_start(
            out=wgs[:, :, :].rearrange("p c f -> p (c f)"), in_=wg1))
        for t in (ones128, ones1, ones8, ones16c, onesf1, ones1t):
            p.v(lambda e, k, ks, t=t: e.memset(t[:, :], 1.0))
        p.v(lambda e, k, ks: e.memset(ke[:, :], 0.0))
        p.v(lambda e, k, ks: e.memset(ve[:, :], 0.0))
        p.gc(lambda e, k, ks: e.iota(idt[:, :], pattern=[[-1, 128]], base=0,
                                     channel_multiplier=1,
                                     allow_small_or_imprecise_dtypes=True))
        p.v(lambda e, k, ks: e.tensor_scalar(idt[:, :], idt[:, :], 0.0, None, ALU.is_equal))
        p.v(lambda e, k, ks: e.tensor_copy(idb[:, :], idt[:, :]))
        p.gc(lambda e, k, ks: e.iota(io48[:, :], pattern=[[1, 48]], base=0,
                                     channel_multiplier=0,
                                     allow_small_or_imprecise_dtypes=True))

        for b in range(B_CORE):
            for dc in range(8):
                p.g(lambda e, k, ks, b=b, dc=dc: e.dma_start(
                    out=xb[:, dc, :], in_=x2d[b, ds(dc * 128, 128), :]))

            with p.Loop(8):
                for wsl, wdr in ((wqs, wq1), (wks, wk1), (wvs, wv1)):
                    p.g(lambda e, k, ks, wsl=wsl, wdr=wdr: e.dma_start(
                        out=wsl[:, :, :].rearrange("p c f -> p (c f)"),
                        in_=wdr[ts(k, 128), :]))
                for tc in range(4):
                    for (wsl, dst, ps, bcol) in ((wqs, qp, psA, bq_s), (wks, kp, psB, bk_s),
                                                 (wvs, vp, psC, bv_s)):
                        for dc in range(8):
                            p.t(lambda e, k, ks, wsl=wsl, ps=ps, dc=dc, tc=tc: e.matmul(
                                ps[:, :], wsl[:, dc, :], xb[:, dc, ds(tc * 512, 512)],
                                start=(dc == 0), stop=(dc == 7)))
                        p.v(lambda e, k, ks, dst=dst, ps=ps, tc=tc, bcol=bcol: e.tensor_scalar(
                            dst[:, ds(tc * 512, 512)], ps[:, :], bcol[:, ds(CE(e, k, 1), 1)],
                            None, ALU.add))

                for h in range(2):
                    hr = h * 64
                    for tc in range(4):
                        p.t(lambda e, k, ks, hr=hr, tc=tc: e.matmul(
                            psA[0:64, :], wlat_s[hr:hr + 64, :],
                            kp[hr:hr + 64, ds(tc * 512, 512)], start=True, stop=True))
                        p.v(lambda e, k, ks, hr=hr, tc=tc: e.tensor_scalar(
                            klat[hr:hr + 64, ds(tc * 512, 512)], psA[0:64, :],
                            blat_s[:, 0:1], None, ALU.add))
                    for tt in range(16):
                        p.t(lambda e, k, ks, hr=hr, tt=tt: e.matmul(
                            psC[:, 0:64], vp[hr:hr + 64, ds(tt * 128, 128)],
                            wlat_s[hr:hr + 64, :], start=True, stop=False))
                        p.t(lambda e, k, ks: e.matmul(
                            psC[:, 0:64], ones1t[:, :], blat_r[:, :],
                            start=False, stop=True))
                        p.v(lambda e, k, ks, tt=tt: e.tensor_copy(
                            vlt[:, ds(tt * 64, 64)], psC[:, 0:64]))

                    for qt in range(16):
                        kts = [qt - 1, qt, qt + 1]
                        variant = 0
                        if qt == 0:
                            kts[0] = 15
                            variant = 1
                        if qt == 15:
                            kts[2] = 0
                            variant = 2
                        for pos in range(3):
                            p.t(lambda e, k, ks, hr=hr, qt=qt, kt=kts[pos], pos=pos: e.matmul(
                                psE3[:, ds(pos * 128, 128)],
                                klat[hr:hr + 64, ds(kt * 128, 128)],
                                qp[hr:hr + 64, ds(qt * 128, 128)], start=True, stop=True))
                        p.s(lambda e, k, ks: e.activation(
                            mexp[:, :], psE3[:, :], AF.Exp, scale=0.125))
                        p.v(lambda e, k, ks, variant=variant: e.tensor_tensor(
                            mexp[:, :], mexp[:, :], bands[:, ds(variant * 384, 384)],
                            ALU.mult))
                        p.t(lambda e, k, ks: e.matmul(
                            psD3[:, :], ones128[:, :], mexp[:, :], start=True, stop=True))
                        p.v(lambda e, k, ks: e.tensor_copy(denr[:, :], psD3[:, :]))
                        p.v(lambda e, k, ks: e.tensor_tensor(
                            rden[:, :], denr[0:1, 0:128], denr[0:1, 128:256], ALU.add))
                        p.v(lambda e, k, ks: e.tensor_tensor(
                            rden[:, :], rden[:, :], denr[0:1, 256:384], ALU.add))
                        p.v(lambda e, k, ks: e.reciprocal(rden[:, :], rden[:, :]))
                        p.v(lambda e, k, ks: e.tensor_copy(rdenb[:, :], rden[:, :]))
                        for pos in range(3):
                            p.t(lambda e, k, ks, kt=kts[pos], pos=pos: e.matmul(
                                psY3[:, :], vlt[:, ds(kt * 64, 64)],
                                mexp[:, ds(pos * 128, 128)],
                                start=(pos == 0), stop=(pos == 2)))
                        p.t(lambda e, k, ks: e.matmul(
                            psB[0:64, 0:128], ones1[:, :], rdenb[:, :], start=True, stop=True))
                        p.v(lambda e, k, ks: e.tensor_copy(bcy[:, :], psB[0:64, 0:128]))
                        p.v(lambda e, k, ks, hr=hr, qt=qt: e.tensor_tensor(
                            y_pair[hr:hr + 64, ds(qt * 128, 128)], psY3[:, :],
                            bcy[:, :], ALU.mult))

                    for which in range(2):
                        if which == 0:
                            qoff, ew = 0, ewh_s
                        else:
                            qoff, ew = 2045, ewt_s
                        if which == 0:
                            p.v(lambda e, k, ks, hr=hr: e.tensor_copy(ke[:, 0:6], klat[hr:hr + 64, 0:6]))
                            p.v(lambda e, k, ks, hr=hr: e.tensor_copy(ke[:, 6:7], klat[hr:hr + 64, 2047:2048]))
                            p.g(lambda e, k, ks: e.dma_start(out=ve[0:6, :], in_=vlt[0:6, 0:64]))
                            p.g(lambda e, k, ks: e.dma_start(out=ve[6:7, :], in_=vlt[127:128, ds(15 * 64, 64)]))
                        else:
                            p.v(lambda e, k, ks, hr=hr: e.tensor_copy(ke[:, 0:6], klat[hr:hr + 64, 2042:2048]))
                            p.v(lambda e, k, ks, hr=hr: e.tensor_copy(ke[:, 6:7], klat[hr:hr + 64, 0:1]))
                            p.g(lambda e, k, ks: e.dma_start(out=ve[0:6, :], in_=vlt[106:112, ds(15 * 64, 64)]))
                            p.g(lambda e, k, ks: e.dma_start(out=ve[6:7, :], in_=vlt[0:1, 0:64]))
                        p.v(lambda e, k, ks, hr=hr, qoff=qoff: e.tensor_copy(
                            qe3[:, :], qp[hr:hr + 64, ds(qoff, 3)]))
                        p.t(lambda e, k, ks: e.matmul(
                            psE3[0:8, 0:3], ke[:, :], qe3[:, :],
                            start=True, stop=True))
                        p.s(lambda e, k, ks: e.activation(
                            ee[:, :], psE3[0:8, 0:3], AF.Exp, scale=0.125))
                        p.v(lambda e, k, ks, ew=ew: e.tensor_tensor(ee[:, :], ee[:, :], ew[:, :], ALU.mult))
                        p.v(lambda e, k, ks: e.tensor_copy(web[:, :], ee[:, :]))
                        p.t(lambda e, k, ks: e.matmul(
                            psD3[0:1, 0:3], ones8[:, :], web[:, :], start=True, stop=True))
                        p.v(lambda e, k, ks: e.reciprocal(rde[:, :], psD3[0:1, 0:3]))
                        p.v(lambda e, k, ks: e.tensor_copy(rdeb[:, :], rde[:, :]))
                        p.t(lambda e, k, ks: e.matmul(
                            psY3[0:64, 0:3], ve[:, :], web[:, :], start=True, stop=True))
                        p.t(lambda e, k, ks: e.matmul(
                            psB[0:64, 0:3], ones1[:, :], rdeb[:, :], start=True, stop=True))
                        p.v(lambda e, k, ks: e.tensor_copy(bce[:, :], psB[0:64, 0:3]))
                        p.v(lambda e, k, ks, hr=hr, qoff=qoff: e.tensor_tensor(
                            y_pair[hr:hr + 64, ds(qoff, 3)], psY3[0:64, 0:3],
                            bce[:, :], ALU.mult))

                p.sy(lambda e, k, ks: e.dma_start(
                    out=y_dram[ts(k, 128), :], in_=y_pair[:, :]))

            p.g(lambda e, k, ks: e.dma_start(
                out=wbuf[:, :, :].rearrange("p c f -> p (c f)"), in_=wproj))
            for tc in range(4):
                for dc in range(8):
                    p.g(lambda e, k, ks, dc=dc, tc=tc: e.dma_start(
                        out=ybufs[:, dc, :],
                        in_=y_dram[ds(dc * 128, 128), ds(tc * 512, 512)]))
                for oc in range(8):
                    for dc in range(8):
                        p.t(lambda e, k, ks, dc=dc, oc=oc, tc=tc: e.matmul(
                            psA[:, :], wbuf[:, dc, ds(oc * 128, 128)],
                            ybufs[:, dc, :], start=(dc == 0), stop=(dc == 7)))
                    p.g(lambda e, k, ks, b=b, oc=oc, tc=tc: e.dma_start(
                        out=xres[:, :], in_=x2d[b, ds(oc * 128, 128), ds(tc * 512, 512)]))
                    p.v(lambda e, k, ks, oc=oc: e.tensor_scalar(
                        sprev[:, :], psA[:, :], bproj_s[:, oc:oc + 1], None, ALU.add))
                    p.v(lambda e, k, ks: e.tensor_tensor(
                        sprev[:, :], sprev[:, :], xres[:, :], ALU.add))
                    p.v(lambda e, k, ks, oc=oc: e.tensor_copy(sum_bf[:, oc, :], sprev[:, :]))
                    p.s(lambda e, k, ks: e.activation(sq_bf[:, :], sprev[:, :], AF.Square))
                    p.t(lambda e, k, ks, oc=oc: e.matmul(
                        psS[:, :], ones128[:, :], sum_bf[:, oc, :],
                        start=(oc == 0), stop=(oc == 7)))
                    p.t(lambda e, k, ks, oc=oc: e.matmul(
                        psQ[:, :], ones128[:, :], sq_bf[:, :],
                        start=(oc == 0), stop=(oc == 7)))
                p.v(lambda e, k, ks: e.tensor_scalar(mrow[:, :], psS[:, :], 1.0 / D, None, ALU.mult))
                p.v(lambda e, k, ks: e.tensor_scalar(m2row[:, :], psQ[:, :], 1.0 / D, None, ALU.mult))
                p.v(lambda e, k, ks: e.tensor_tensor(varr[:, :], mrow[:, :], mrow[:, :], ALU.mult))
                p.v(lambda e, k, ks: e.tensor_tensor(varr[:, :], m2row[:, :], varr[:, :], ALU.subtract))
                p.v(lambda e, k, ks: e.tensor_scalar(varr[:, :], varr[:, :], 1e-5, None, ALU.add))
                p.s(lambda e, k, ks: e.activation(varr[:, :], varr[:, :], AF.Sqrt))
                p.v(lambda e, k, ks: e.reciprocal(varr[:, :], varr[:, :]))
                p.v(lambda e, k, ks: e.tensor_copy(mrow_bf[:, :], mrow[:, :]))
                p.v(lambda e, k, ks: e.tensor_copy(rstd_bf[:, :], varr[:, :]))
                p.t(lambda e, k, ks: e.matmul(psB[0:64, :], ones1[:, :], mrow_bf[:, :], start=True, stop=True))
                p.t(lambda e, k, ks: e.matmul(psC[0:64, :], ones1[:, :], rstd_bf[:, :], start=True, stop=True))
                p.v(lambda e, k, ks: e.tensor_copy(mb_sb[0:64, :], psB[0:64, :]))
                p.v(lambda e, k, ks: e.tensor_copy(rs_sb[0:64, :], psC[0:64, :]))
                p.v(lambda e, k, ks: e.tensor_copy(mb_sb[64:128, :], psB[0:64, :]))
                p.v(lambda e, k, ks: e.tensor_copy(rs_sb[64:128, :], psC[0:64, :]))
                for oc in range(8):
                    p.v(lambda e, k, ks, oc=oc: e.tensor_tensor(
                        outf[:, :], sum_bf[:, oc, :], mb_sb[:, :], ALU.subtract))
                    p.v(lambda e, k, ks: e.tensor_tensor(outf[:, :], outf[:, :], rs_sb[:, :], ALU.mult))
                    p.v(lambda e, k, ks, oc=oc, b=b, tc=tc: e.tensor_copy(
                        x3sb[:, oc, ds(b * T + tc * 512, 512)], outf[:, :]))
            for oc in range(8):
                for tt in range(16):
                    p.t(lambda e, k, ks, oc=oc, tt=tt, b=b: e.transpose(
                        psAb[:, 0:128], x3sb[:, oc, ds(b * T + tt * 128, 128)], idb[:, :]))
                    p.v(lambda e, k, ks: e.tensor_copy(xtb[:, :], psAb[:, 0:128]))
                    p.sy(lambda e, k, ks, b=b, oc=oc, tt=tt: e.dma_start(
                        out=x3bf[ds(b * T + tt * 128, 128), ds(oc * 128, 128)],
                        in_=xtb[:, :]))

        for tile in range(32):
            for dc in range(8):
                p.t(lambda e, k, ks, dc=dc, tile=tile: e.matmul(
                    psB[:, 0:16], x3sb[:, dc, ds(tile * 128, 128)], wgs[:, dc, :],
                    start=(dc == 0), stop=(dc == 7)))
            p.v(lambda e, k, ks: e.tensor_reduce(mx1[:, :], psB[:, 0:16], AX.X, ALU.max))
            p.v(lambda e, k, ks: e.tensor_scalar(nmx[:, :], mx1[:, :], -1.0, None, ALU.mult))
            p.s(lambda e, k, ks: e.activation(el[:, :], psB[:, 0:16], AF.Exp, bias=nmx[:, 0:1]))
            p.v(lambda e, k, ks: e.tensor_reduce(den1[:, :], el[:, :], AX.X, ALU.add))
            p.v(lambda e, k, ks: e.reciprocal(den1[:, :], den1[:, :]))
            p.v(lambda e, k, ks, tile=tile: e.tensor_copy(gsmat[:, tile:tile + 1], den1[:, :]))
            p.v(lambda e, k, ks: e.tensor_scalar(oh_tok[:, :], psB[:, 0:16], mx1[:, 0:1], None, ALU.is_ge))
            p.v(lambda e, k, ks, tile=tile: e.tensor_copy(oh_all[:, ds(tile * 16, 16)], oh_tok[:, :]))
            p.t(lambda e, k, ks: e.transpose(psYb[0:16, 0:128], oh_tok[:, :], idb[:, :]))
            p.v(lambda e, k, ks: e.tensor_copy(ohst[:, :], psYb[0:16, 0:128]))
            p.sy(lambda e, k, ks, tile=tile: e.dma_start(
                out=ohT_o[:, ds(tile * 128, 128)], in_=ohst[:, :]))
            p.t(lambda e, k, ks: e.matmul(
                psC[0:16, 0:128], oh_tok[:, :], U128b[:, :], start=True, stop=True))
            p.v(lambda e, k, ks, tile=tile: e.tensor_copy(cumT[:, ds(tile * 128, 128)], psC[0:16, 0:128]))
            p.t(lambda e, k, ks: e.matmul(
                psB[:, 0:16], U128b[:, :], oh_tok[:, :], start=True, stop=True))
            p.v(lambda e, k, ks: e.tensor_tensor(ct16[:, :], psB[:, 0:16], oh_tok[:, :], ALU.mult))
            p.v(lambda e, k, ks, tile=tile: e.tensor_reduce(
                rkmat[:, ds(tile, 1)], ct16[:, :], AX.X, ALU.add))
        p.v(lambda e, k, ks: e.tensor_copy(
            tot[:, :], cumT.tensor_ap()[0:16, 127:4096:128] if False else cumT[:, 127::128]))
        p.g(lambda e, k, ks: e.dma_start(out=U32f[:, :], in_=U32sd))
        p.t(lambda e, k, ks: e.transpose(psB[0:32, 0:16], tot[:, :], idt[0:16, 0:16]))
        p.v(lambda e, k, ks: e.tensor_copy(totf[:, :], psB[0:32, 0:16]))
        p.t(lambda e, k, ks: e.matmul(
            psC[0:16, 0:32], totf[:, :], U32f[:, :], start=True, stop=True))
        p.v(lambda e, k, ks: e.tensor_copy(baseC[:, :], psC[0:16, 0:32]))
        p.v(lambda e, k, ks: e.tensor_tensor(hist_s[:, :], baseC[:, 31:32], tot[:, 31:32], ALU.add))
        p.sy(lambda e, k, ks: e.dma_start(out=hist_o, in_=hist_s[:, :]))
        p.v(lambda e, k, ks: e.tensor_scalar(pada[:, :], hist_s[:, :], 127.0, None, ALU.add))
        p.gc(lambda e, k, ks: e.iota(io33[:, :], pattern=[[1, 33]], base=1,
                                     channel_multiplier=0,
                                     allow_small_or_imprecise_dtypes=True))
        p.v(lambda e, k, ks: e.tensor_scalar(qcol[:, :], pada[:, :], 1.0 / 128.0, None, ALU.mult))
        p.v(lambda e, k, ks: e.tensor_scalar(cmp33[:, :], io33[:, :], qcol[:, 0:1], None, ALU.is_le))
        p.v(lambda e, k, ks: e.tensor_reduce(padc[:, :], cmp33[:, :], AX.X, ALU.add))
        p.v(lambda e, k, ks: e.tensor_scalar(padc[:, :], padc[:, :], 128.0, None, ALU.mult))
        p.v(lambda e, k, ks: e.tensor_copy(padb[:, :], padc[:, :]))
        p.t(lambda e, k, ks: e.matmul(
            psD3[0:1, 0:16], padb[:, :], U16sb[:, :], start=True, stop=True))
        p.v(lambda e, k, ks: e.tensor_copy(segb[:, :], psD3[0:1, 0:16]))
        p.t(lambda e, k, ks: e.transpose(psC[0:16, 0:1], segb[:, :], idt[0:1, 0:1]))
        p.v(lambda e, k, ks: e.tensor_copy(segc[:, :], psC[0:16, 0:1]))
        p.v(lambda e, k, ks: e.tensor_scalar(cbmat[:, :], baseC[:, :], segc[:, 0:1], None, ALU.add))
        p.t(lambda e, k, ks: e.transpose(psB[0:32, 0:16], cbmat[:, :], idt[0:16, 0:16]))
        p.v(lambda e, k, ks: e.tensor_copy(gtmp[:, 0:16], psB[0:32, 0:16]))
        p.sy(lambda e, k, ks: e.dma_start(out=cb_dram, in_=gtmp[:, 0:16]))
        with p.Loop(32):
            p.v(lambda e, k, ks: e.tensor_scalar(
                cumT[:, ts(k, 128)], cumT[:, ts(k, 128)], baseC[:, ds(k, 1)],
                None, ALU.add))
        p.sy(lambda e, k, ks: e.dma_start(out=locT_o, in_=cumT[:, :]))
        p.t(lambda e, k, ks: e.transpose(psC[0:32, 0:128], gsmat[:, :], idt[:, :]))
        p.v(lambda e, k, ks: e.tensor_copy(gtmp[:, :], psC[0:32, 0:128]))
        p.sy(lambda e, k, ks: e.dma_start(out=gates_o, in_=gtmp[:, :]))
        with p.Loop(32):
            p.g(lambda e, k, ks: e.dma_start(out=cbrow[:, :], in_=cb_dram[ts(k, 1), :]))
            p.t(lambda e, k, ks: e.matmul(
                psB[:, 0:16], onesf1[:, :], cbrow[:, :], start=True, stop=True))
            p.v(lambda e, k, ks: e.tensor_tensor(
                gath[:, :], psB[:, 0:16], oh_all[:, ts(k, 16)], ALU.mult))
            p.v(lambda e, k, ks: e.tensor_reduce(slotc[:, :], gath[:, :], AX.X, ALU.add))
            p.v(lambda e, k, ks: e.tensor_tensor(slotc[:, :], slotc[:, :], rkmat[:, ds(k, 1)], ALU.add))
            p.v(lambda e, k, ks: e.tensor_scalar(
                slots_f[:, ds(k, 1)], slotc[:, :], -1.0, None, ALU.add))
        p.v(lambda e, k, ks: e.tensor_copy(slots_i[:, :], slots_f[:, :]))
        p.sy(lambda e, k, ks: e.dma_start(out=slots_o, in_=slots_i[:, :]))
        p.v(lambda e, k, ks: e.tensor_scalar(
            cmp48[:, :], io48[:, :], 128.0, segc[:, 0:1], ALU.mult, op1=ALU.is_ge))
        p.t(lambda e, k, ks: e.matmul(
            psD3[0:1, 0:48], ones16c[:, :], cmp48[:, :], start=True, stop=True))
        p.v(lambda e, k, ks: e.tensor_scalar(t2er[:, :], psD3[0:1, 0:48], -1.0, 128.0, ALU.add, op1=ALU.mult))
        p.v(lambda e, k, ks: e.tensor_copy(t2ei[:, :], t2er[:, :]))
        p.sy(lambda e, k, ks: e.dma_start(out=t2e_o, in_=t2ei[:, :]))

        p.gc(lambda e, k, ks: e.memset(dum[:, :], 0.0))


def build_l1_attn():
    p = Prog()
    nc = p.nc
    x2d = nc.dram_tensor("x2", [B_CORE, D, T], F32, kind="ExternalInput").ap()
    wq1 = nc.dram_tensor("wq1", [D, D], F32, kind="ExternalInput").ap()
    wk1 = nc.dram_tensor("wk1", [D, D], F32, kind="ExternalInput").ap()
    wv1 = nc.dram_tensor("wv1", [D, D], F32, kind="ExternalInput").ap()
    bq1 = nc.dram_tensor("bq1", [128, 8], F32, kind="ExternalInput").ap()
    bk1 = nc.dram_tensor("bk1", [128, 8], F32, kind="ExternalInput").ap()
    bv1 = nc.dram_tensor("bv1", [128, 8], F32, kind="ExternalInput").ap()
    wlatT = nc.dram_tensor("wlatT", [64, 64], F32, kind="ExternalInput").ap()
    blatd = nc.dram_tensor("blat", [64, 1], F32, kind="ExternalInput").ap()
    wproj = nc.dram_tensor("wproj", [128, 8 * D], F32, kind="ExternalInput").ap()
    bprojd = nc.dram_tensor("bproj", [128, 8], F32, kind="ExternalInput").ap()
    wg1 = nc.dram_tensor("wg1", [128, 128], F32, kind="ExternalInput").ap()
    bandsd = nc.dram_tensor("bands", [128, 1152], F32, kind="ExternalInput").ap()
    ewhd = nc.dram_tensor("ewh", [8, 3], F32, kind="ExternalInput").ap()
    ewtd = nc.dram_tensor("ewt", [8, 3], F32, kind="ExternalInput").ap()
    U128d = nc.dram_tensor("U128", [128, 128], F32, kind="ExternalInput").ap()
    U16sd = nc.dram_tensor("U16s", [16, 16], F32, kind="ExternalInput").ap()
    U32sd = nc.dram_tensor("U32s", [32, 32], F32, kind="ExternalInput").ap()

    x3bf = nc.dram_tensor("x3bf", [4096, D], BF16, kind="ExternalOutput").ap()
    hist_o = nc.dram_tensor("hist", [16, 1], F32, kind="ExternalOutput").ap()
    gates_o = nc.dram_tensor("gates", [32, 128], F32, kind="ExternalOutput").ap()
    ohT_o = nc.dram_tensor("ohT", [16, 4096], BF16, kind="ExternalOutput").ap()
    locT_o = nc.dram_tensor("locT", [16, 4096], F32, kind="ExternalOutput").ap()
    slots_o = nc.dram_tensor("slots", [128, 32], I32, kind="ExternalOutput").ap()
    t2e_o = nc.dram_tensor("t2e", [1, 48], I32, kind="ExternalOutput").ap()
    y_dram = nc.dram_tensor("y_dram3", [D, T], F32).ap()
    cb_dram = nc.dram_tensor("cb_dram", [32, 16], F32).ap()

    dr = {"x2": x2d,
          "wq1": wq1,
          "wk1": wk1,
          "wv1": wv1,
          "bq1": bq1,
          "bk1": bk1,
          "bv1": bv1,
          "wlatT": wlatT,
          "blat": blatd,
          "wproj": wproj,
          "bproj": bprojd,
          "wg1": wg1,
          "bands": bandsd,
          "ewh": ewhd,
          "ewt": ewtd,
          "U128": U128d,
          "U16s": U16sd,
          "U32s": U32sd,
          "x3bf": x3bf,
          "hist": hist_o,
          "gates": gates_o,
          "ohT": ohT_o,
          "locT": locT_o,
          "slots": slots_o,
          "t2e": t2e_o}
    emit_l1_attn(p, dr)
    with ExitStack() as _ctx2:
        p.emit(_ctx2)
    return p.nc



def l1_ffn_host_inputs(inputs):
    w1 = inputs["L1_ffn_w1"]
    w2 = inputs["L1_ffn_w2"]
    w1_cr = w1.reshape(16, 8, 128, 4096).transpose(0, 2, 1, 3).reshape(16 * 128, 8 * 4096)
    w2_cr = w2.reshape(16, 32, 128, 1024).transpose(0, 2, 1, 3).reshape(16 * 128, 32 * 1024)
    return {"w1_cr": np.ascontiguousarray(w1_cr.astype(np.float32)),
            "w2_cr": np.ascontiguousarray(w2_cr.astype(np.float32))}


def emit_l1_ffn(p, dr):
    from concourse.bass_core_rust import IndirectOffsetOnAxis
    nc = p.nc
    x3d = dr["x3bf"]
    slotsd = dr["slots"]
    t2ed = dr["t2e"]
    w1d = dr["w1_cr"]
    w2d = dr["w2_cr"]
    yslots = dr["yslots"]
    xg = nc.dram_tensor("xg_dram", [6144, D], BF16).ap()

    with ExitStack() as ctx:
        def sb(name, shape, dt=BF16):
            return ctx.enter_context(nc.sbuf_tensor(name, shape, dt))

        w1s = sb("w1s4", [128, 8, 4096])
        w2s = sb("w2s4", [128, 32, 1024])
        hbuf = sb("hbuf4", [128, 32, 128])
        xstg = sb("xstg", [128, D])
        xgt = sb("xgt", [128, D])
        xf = sb("xf4", [128, 8, 128])
        yfm = sb("yfm", [128, 512])
        ytok = sb("ytok", [128, 8, 128])
        zt = sb("zt4", [128, D])
        idb4 = sb("idb4", [128, 128])
        idf4 = sb("idf4", [128, 128], F32)
        slots_s = ctx.enter_context(nc.sbuf_tensor("slots_s", [128, 32], I32))
        t2es = ctx.enter_context(nc.sbuf_tensor("t2es", [1, 48], I32))
        ecell = ctx.enter_context(nc.sbuf_tensor("ecell", [1, 1], I32))
        dum = sb("dum4", [1, 1], F32)

        psA = ctx.enter_context(nc.psum_tensor("psA4", [128, 512], F32))
        psB = ctx.enter_context(nc.psum_tensor("psB4", [128, 512], F32))
        psAb = psA.bitcast(BF16)

        p.g(lambda e, k, ks: e.dma_start(out=slots_s[:, :], in_=slotsd))
        p.g(lambda e, k, ks: e.dma_start(out=t2es[:, :], in_=t2ed))
        p.v(lambda e, k, ks: e.memset(zt[:, :], 0.0))
        p.gc(lambda e, k, ks: e.iota(idf4[:, :], pattern=[[-1, 128]], base=0,
                                     channel_multiplier=1,
                                     allow_small_or_imprecise_dtypes=True))
        p.v(lambda e, k, ks: e.tensor_scalar(idf4[:, :], idf4[:, :], 0.0, None, ALU.is_equal))
        p.v(lambda e, k, ks: e.tensor_copy(idb4[:, :], idf4[:, :]))
        for j in range(48):
            p.sy(lambda e, k, ks, j=j: e.dma_start(out=xg[ds(j * 128, 128), :], in_=zt[:, :]))
        for tile in range(32):
            p.g(lambda e, k, ks, tile=tile: e.dma_start(
                out=xstg[:, :], in_=x3d[ds(tile * 128, 128), :]))
            p.g(lambda e, k, ks, tile=tile: e.indirect_dma_start(
                xg, IndirectOffsetOnAxis(slots_s[:, tile:tile + 1], 0),
                xstg[:, :], None))
        _esv = [None]

        def _load_w1(e, k, ks):
            _esv[0] = e.value_load(ecell[0:1, 0:1])
            return e.dma_start(
                out=w1s[:, :, :].rearrange("p c f -> p (c f)"),
                in_=w1d[ds(_esv[0], 128), :])

        with p.Loop(48):
            p.v(lambda e, k, ks: e.tensor_copy(ecell[:, :], t2es[0:1, ds(k, 1)]))
            p.g(_load_w1)
            p.g(lambda e, k, ks: e.dma_start(
                out=w2s[:, :, :].rearrange("p c f -> p (c f)"), in_=w2d[ds(_esv[0], 128), :]))
            p.g(lambda e, k, ks: e.dma_start(out=xgt[:, :], in_=xg[ts(k, 128), :]))
            for dc4 in range(2):
                for j in range(4):
                    p.t(lambda e, k, ks, dc4=dc4, j=j: e.transpose(
                        psAb[:, ds(j * 128, 128)], xgt[:, ds((dc4 * 4 + j) * 128, 128)],
                        idb4[:, :]))
                p.v(lambda e, k, ks, dc4=dc4: e.tensor_copy(
                    xf[:, ds(dc4 * 4, 4), :].rearrange("p c f -> p (c f)"),
                    psAb[:, 0:512]))
            for fq in range(8):
                for r in range(4):
                    for dc in range(8):
                        p.t(lambda e, k, ks, dc=dc, fq=fq, r=r: e.matmul(
                            psB[:, ds(r * 128, 128)],
                            w1s[:, dc, ds((fq * 4 + r) * 128, 128)], xf[:, dc, :],
                            start=(dc == 0), stop=(dc == 7)))
                p.s(lambda e, k, ks, fq=fq: e.activation(
                    hbuf[:, ds(fq * 4, 4), :].rearrange("p c f -> p (c f)"),
                    psB[:, :], AF.Relu))
            for og in range(2):
                for r in range(4):
                    for cf in range(32):
                        p.t(lambda e, k, ks, cf=cf, og=og, r=r: e.matmul(
                            psB[:, ds(r * 128, 128)],
                            w2s[:, cf, ds((og * 4 + r) * 128, 128)], hbuf[:, cf, :],
                            start=(cf == 0), stop=(cf == 31)))
                p.v(lambda e, k, ks: e.tensor_copy(yfm[:, :], psB[:, :]))
                for j in range(4):
                    p.t(lambda e, k, ks, j=j: e.transpose(
                        psAb[:, ds(j * 128, 128)], yfm[:, ds(j * 128, 128)], idb4[:, :]))
                p.v(lambda e, k, ks, og=og: e.tensor_copy(
                    ytok[:, ds(og * 4, 4), :].rearrange("p c f -> p (c f)"),
                    psAb[:, 0:512]))
            p.sy(lambda e, k, ks: e.dma_start(
                out=yslots[ts(k, 128), :],
                in_=ytok[:, :, :].rearrange("p c f -> p (c f)")))

        p.gc(lambda e, k, ks: e.memset(dum[:, :], 0.0))


def build_l1_ffn():
    from concourse.bass_core_rust import IndirectOffsetOnAxis
    p = Prog()
    nc = p.nc
    x3d = nc.dram_tensor("x3bf", [4096, D], BF16, kind="ExternalInput").ap()
    slotsd = nc.dram_tensor("slots", [128, 32], I32, kind="ExternalInput").ap()
    t2ed = nc.dram_tensor("t2e", [1, 48], I32, kind="ExternalInput").ap()
    w1d = nc.dram_tensor("w1_cr", [16 * 128, 8 * 4096], F32, kind="ExternalInput").ap()
    w2d = nc.dram_tensor("w2_cr", [16 * 128, 32 * 1024], F32, kind="ExternalInput").ap()
    yslots = nc.dram_tensor("yslots", [6144, D], BF16, kind="ExternalOutput").ap()
    xg = nc.dram_tensor("xg_dram", [6144, D], BF16).ap()

    dr = {"x3bf": x3d,
          "slots": slotsd,
          "t2e": t2ed,
          "w1_cr": w1d,
          "w2_cr": w2d,
          "yslots": yslots}
    emit_l1_ffn(p, dr)
    with ExitStack() as _ctx2:
        p.emit(_ctx2)
    return p.nc



def build_finale():
    from concourse.bass_core_rust import IndirectOffsetOnAxis
    p = Prog()
    nc = p.nc
    x3d = nc.dram_tensor("x3bf", [4096, D], BF16, kind="ExternalInput").ap()
    ysd = nc.dram_tensor("yslots", [6144, D], BF16, kind="ExternalInput").ap()
    slotsd = nc.dram_tensor("slots", [128, 32], I32, kind="ExternalInput").ap()
    gatesd = nc.dram_tensor("gates", [32, 128], F32, kind="ExternalInput").ap()
    ohTd = nc.dram_tensor("ohT", [16, 4096], BF16, kind="ExternalInput").ap()
    locTd = nc.dram_tensor("locT", [16, 4096], F32, kind="ExternalInput").ap()
    histsd = nc.dram_tensor("hists", [8, 16], F32, kind="ExternalInput").ap()
    outd = nc.dram_tensor("out", [4096, D], F32, kind="ExternalOutput").ap()
    pid_dram = nc.partition_id_tensor.ap()

    with ExitStack() as ctx:
        def sb(name, shape, dt=F32):
            return ctx.enter_context(nc.sbuf_tensor(name, shape, dt))

        hsb = sb("hsb", [8, 16])
        pidu = ctx.enter_context(nc.sbuf_tensor("pidu5", [1, 1], U32))
        pidf = sb("pidf5", [1, 1])
        onesf8 = sb("onesf8", [1, 8])
        onesf1 = sb("onesf15", [1, 128])
        ones16 = sb("ones16_5", [16, 1], BF16)
        iota8 = sb("iota85", [8, 1])
        pid8 = sb("pid85", [8, 1])
        cmp8 = sb("cmp85", [8, 1])
        bprev = sb("bprev", [16, 1])
        locs = sb("locs5", [16, 4096])
        ohs = ctx.enter_context(nc.sbuf_tensor("ohs5", [16, 4096], BF16))
        valt = sb("valt", [16, 4096])
        prodb = ctx.enter_context(nc.sbuf_tensor("prodb", [16, 4096], BF16))
        vrow = sb("vrow", [1, 4096])
        grow = sb("grow", [1, 4096])
        wrow = sb("wrow", [1, 4096])
        wcols = sb("wcols", [128, 32])
        id1 = sb("id15", [1, 1])
        slots_s = ctx.enter_context(nc.sbuf_tensor("slots_s5", [128, 32], I32))
        ytile = ctx.enter_context(nc.sbuf_tensor("ytile", [128, D], BF16))
        xt5 = ctx.enter_context(nc.sbuf_tensor("xt5", [128, D], BF16))
        sum5 = sb("sum5", [128, D])
        sqs = sb("sqs", [128, D])
        srow = sb("srow", [128, 1])
        ssq = sb("ssq", [128, 1])
        mean5 = sb("mean5", [128, 1])
        nmean = sb("nmean5", [128, 1])
        var5 = sb("var5", [128, 1])
        outt = sb("outt", [128, D])
        dum = sb("dum5", [1, 1])

        psA = ctx.enter_context(nc.psum_tensor("psA5", [128, 512], F32))
        psS = ctx.enter_context(nc.psum_tensor("psS5", [1, 512], F32))

        p.g(lambda e, k, ks: e.dma_start(out=hsb[:, :], in_=histsd))
        p.g(lambda e, k, ks: e.dma_start(out=pidu[:, :], in_=pid_dram))
        p.g(lambda e, k, ks: e.dma_start(out=locs[:, :], in_=locTd))
        p.g(lambda e, k, ks: e.dma_start(out=ohs[:, :], in_=ohTd))
        p.g(lambda e, k, ks: e.dma_start(out=slots_s[:, :], in_=slotsd))
        p.g(lambda e, k, ks: e.dma_start(out=grow[:, :],
                                         in_=gatesd.rearrange("a b -> (a b)")))
        p.v(lambda e, k, ks: e.memset(onesf8[:, :], 1.0))
        p.v(lambda e, k, ks: e.memset(onesf1[:, :], 1.0))
        p.v(lambda e, k, ks: e.memset(ones16[:, :], 1.0))
        p.v(lambda e, k, ks: e.memset(id1[:, :], 1.0))
        p.gc(lambda e, k, ks: e.iota(iota8[:, :], pattern=[[0, 1]], base=0,
                                     channel_multiplier=1,
                                     allow_small_or_imprecise_dtypes=True))
        p.v(lambda e, k, ks: e.tensor_copy(pidf[:, :], pidu[:, :]))
        p.t(lambda e, k, ks: e.matmul(psA[0:8, 0:1], onesf8[:, :], pidf[:, :],
                                      start=True, stop=True))
        p.v(lambda e, k, ks: e.tensor_copy(pid8[:, :], psA[0:8, 0:1]))
        p.v(lambda e, k, ks: e.tensor_tensor(cmp8[:, :], iota8[:, :], pid8[:, :], ALU.is_lt))
        p.t(lambda e, k, ks: e.matmul(psA[0:16, 0:1], hsb[:, :], cmp8[:, :],
                                      start=True, stop=True))
        p.v(lambda e, k, ks: e.tensor_copy(bprev[:, :], psA[0:16, 0:1]))
        p.v(lambda e, k, ks: e.tensor_scalar(valt[:, :], locs[:, :], bprev[:, 0:1], None, ALU.add))
        p.v(lambda e, k, ks: e.tensor_scalar(valt[:, :], valt[:, :], 2048.0, None, ALU.is_le))
        p.v(lambda e, k, ks: e.tensor_tensor(prodb[:, :], ohs[:, :], valt[:, :], ALU.mult))
        with p.Loop(8):
            p.t(lambda e, k, ks: e.matmul(psS[:, :], ones16[:, :],
                                          prodb[:, ts(k, 512)], start=True, stop=True))
            p.v(lambda e, k, ks: e.tensor_copy(vrow[:, ts(k, 512)], psS[:, :]))
        p.v(lambda e, k, ks: e.tensor_tensor(wrow[:, :], vrow[:, :], grow[:, :], ALU.mult))
        for tile in range(32):
            p.t(lambda e, k, ks, tile=tile: e.transpose(
                psA[:, 0:1], wrow[0:1, ds(tile * 128, 128)], id1[:, :]))
            p.v(lambda e, k, ks, tile=tile: e.tensor_copy(wcols[:, tile:tile + 1], psA[:, 0:1]))
        for tile in range(32):
            p.g(lambda e, k, ks, tile=tile: e.indirect_dma_start(
                ytile[:, :], None, ysd, IndirectOffsetOnAxis(slots_s[:, tile:tile + 1], 0)))
            p.g(lambda e, k, ks, tile=tile: e.dma_start(
                out=xt5[:, :], in_=x3d[ds(tile * 128, 128), :]))
            p.v(lambda e, k, ks, tile=tile: e.tensor_scalar(
                sum5[:, :], ytile[:, :], wcols[:, tile:tile + 1], None, ALU.mult))
            p.v(lambda e, k, ks: e.tensor_tensor(sum5[:, :], sum5[:, :], xt5[:, :], ALU.add))
            p.v(lambda e, k, ks: e.tensor_reduce(srow[:, :], sum5[:, :], AX.X, ALU.add))
            p.v(lambda e, k, ks: e.tensor_scalar(mean5[:, :], srow[:, :], 1.0 / D, None, ALU.mult))
            p.s(lambda e, k, ks: e.activation(sqs[:, :], sum5[:, :], AF.Square, accum_out=ssq[:, :]))
            p.v(lambda e, k, ks: e.tensor_scalar(var5[:, :], ssq[:, :], 1.0 / D, None, ALU.mult))
            p.v(lambda e, k, ks: e.tensor_tensor(nmean[:, :], mean5[:, :], mean5[:, :], ALU.mult))
            p.v(lambda e, k, ks: e.tensor_tensor(var5[:, :], var5[:, :], nmean[:, :], ALU.subtract))
            p.v(lambda e, k, ks: e.tensor_scalar(var5[:, :], var5[:, :], 1e-5, None, ALU.add))
            p.s(lambda e, k, ks: e.activation(var5[:, :], var5[:, :], AF.Sqrt))
            p.v(lambda e, k, ks: e.reciprocal(var5[:, :], var5[:, :]))
            p.v(lambda e, k, ks: e.tensor_scalar(nmean[:, :], mean5[:, :], -1.0, None, ALU.mult))
            p.v(lambda e, k, ks: e.tensor_scalar(outt[:, :], sum5[:, :], nmean[:, 0:1], None, ALU.add))
            p.v(lambda e, k, ks: e.tensor_scalar(outt[:, :], outt[:, :], var5[:, 0:1], None, ALU.mult))
            p.sy(lambda e, k, ks, tile=tile: e.dma_start(
                out=outd[ds(tile * 128, 128), :], in_=outt[:, :]))

        p.gc(lambda e, k, ks: e.memset(dum[:, :], 0.0))
        p.emit(ctx)
    return p.nc



_PROGS = {}


def _get_prog(name, builder):
    if name not in _PROGS:
        _PROGS[name] = builder()
    return _PROGS[name]


def kernel5(**inputs):
    cores = list(range(NC))
    x = inputs["L0_attn_x"]

    nc1 = _get_prog("l0attn", build_l0_attn)
    in1 = [l0_attn_host_inputs(x, inputs["L0_attn_w_qkv"], inputs["L0_attn_w_out"],
                               inputs["L0_attn_b_out"], c) for c in cores]
    r1 = run_bass_kernel_spmd(nc1, in1, core_ids=cores).results

    nc2 = _get_prog("l0ffn", build_l0_ffn)
    wm2 = l0_ffn_host_inputs(None, inputs, 0)
    in2 = [{**wm2, "x1": r1[c]["x1"]} for c in cores]
    r2 = run_bass_kernel_spmd(nc2, in2, core_ids=cores).results

    nc3 = _get_prog("l1attn", build_l1_attn)
    wm3 = l1_attn_host_inputs(inputs)
    in3 = [{**wm3, "x2": r2[c]["x2"]} for c in cores]
    r3 = run_bass_kernel_spmd(nc3, in3, core_ids=cores).results

    nc4 = _get_prog("l1ffn", build_l1_ffn)
    wm4 = l1_ffn_host_inputs(inputs)
    in4 = [{**wm4, "x3bf": r3[c]["x3bf"], "slots": r3[c]["slots"],
            "t2e": r3[c]["t2e"]} for c in cores]
    r4 = run_bass_kernel_spmd(nc4, in4, core_ids=cores).results

    hists = np.ascontiguousarray(
        np.stack([r3[c]["hist"][:, 0] for c in cores], 0)).astype(np.float32)
    nc5 = _get_prog("finale", build_finale)
    in5 = [{"x3bf": r3[c]["x3bf"], "yslots": r4[c]["yslots"],
            "slots": r3[c]["slots"], "gates": r3[c]["gates"],
            "ohT": r3[c]["ohT"], "locT": r3[c]["locT"], "hists": hists}
           for c in cores]
    r5 = run_bass_kernel_spmd(nc5, in5, core_ids=cores).results

    out = np.concatenate([r5[c]["out"].reshape(2, T, D) for c in cores], axis=0)
    return out.astype(np.float32)



def build_mega():
    p = Prog()
    nc = p.nc
    dr = {}

    def ein(name, shape, dt=F32):
        dr[name] = nc.dram_tensor(name, shape, dt, kind="ExternalInput").ap()

    def eout(name, shape, dt=F32):
        dr[name] = nc.dram_tensor(name, shape, dt, kind="ExternalOutput").ap()

    def eint(name, shape, dt=F32):
        dr[name] = nc.dram_tensor(name, shape, dt).ap()

    ein("xT", [B_CORE, D, T]); ein("wq_ps", [D, D]); ein("wk_ps", [D, D])
    ein("wv_ps", [D, D]); ein("wo_cr", [128, 8 * D]); ein("bout", [128, 8])
    ein("cs_tab", [128, T]); ein("sc_tab", [128, T]); ein("cmask", [128, 2048])
    eint("x1", [B_CORE, D, T])
    ein("wg_cr", [17 * 128, 8192]); ein("wu_cr", [17 * 128, 8192])
    ein("wd_cr", [17 * 128, 8192]); ein("gw_cr", [128, 128]); ein("ebias", [128, 16])
    eint("x2", [B_CORE, D, T])
    ein("wq1", [D, D]); ein("wk1", [D, D]); ein("wv1", [D, D])
    ein("bq1", [128, 8]); ein("bk1", [128, 8]); ein("bv1", [128, 8])
    ein("wlatT", [64, 64]); ein("blat", [64, 1]); ein("wproj", [128, 8 * D])
    ein("bproj", [128, 8]); ein("wg1", [128, 128]); ein("bands", [128, 1152])
    ein("ewh", [8, 3]); ein("ewt", [8, 3]); ein("U128", [128, 128])
    ein("U16s", [16, 16]); ein("U32s", [32, 32])
    eout("x3bf", [4096, D], BF16); eout("hist", [16, 1]); eout("gates", [32, 128])
    eout("ohT", [16, 4096], BF16); eout("locT", [16, 4096])
    eout("slots", [128, 32], I32); eout("t2e", [1, 48], I32)
    ein("w1_cr", [16 * 128, 8 * 4096]); ein("w2_cr", [16 * 128, 32 * 1024])
    eout("yslots", [6144, D], BF16)

    emit_l0_attn(p, dr)
    emit_l0_ffn(p, dr)
    emit_l1_attn(p, dr)
    emit_l1_ffn(p, dr)
    with ExitStack() as ctx:
        p.emit(ctx)
    return p.nc


def kernel(**inputs):
    cores = list(range(NC))
    x = inputs["L0_attn_x"]
    ncm = _get_prog("mega", build_mega)
    wm2 = l0_ffn_host_inputs(None, inputs, 0)
    wm3 = l1_attn_host_inputs(inputs)
    wm4 = l1_ffn_host_inputs(inputs)
    in_m = []
    for c in cores:
        m = dict(l0_attn_host_inputs(x, inputs["L0_attn_w_qkv"], inputs["L0_attn_w_out"],
                                     inputs["L0_attn_b_out"], c))
        m.update(wm2); m.update(wm3); m.update(wm4)
        in_m.append(m)
    rm = run_bass_kernel_spmd(ncm, in_m, core_ids=cores).results
    hists = np.ascontiguousarray(
        np.stack([rm[c]["hist"][:, 0] for c in cores], 0)).astype(np.float32)
    nc5 = _get_prog("finale", build_finale)
    in5 = [{"x3bf": rm[c]["x3bf"], "yslots": rm[c]["yslots"],
            "slots": rm[c]["slots"], "gates": rm[c]["gates"],
            "ohT": rm[c]["ohT"], "locT": rm[c]["locT"], "hists": hists}
           for c in cores]
    r5 = run_bass_kernel_spmd(nc5, in5, core_ids=cores).results
    out = np.concatenate([r5[c]["out"].reshape(2, T, D) for c in cores], axis=0)
    return out.astype(np.float32)


# version 06
# speedup vs baseline: 1.7850x; 1.2518x over previous
import numpy as np
import ml_dtypes
import concourse.bass as bass
import concourse.mybir as mybir
from concourse.bass_utils import run_bass_kernel_spmd
from contextlib import ExitStack, contextmanager

F32 = mybir.dt.float32
BF16 = mybir.dt.bfloat16
I32 = mybir.dt.int32
U32 = mybir.dt.uint32
AF = mybir.ActivationFunctionType
ALU = mybir.AluOpType
AX = mybir.AxisListType
ts = bass.ts
ds = bass.ds

B_CORE = 2
T = 2048
D = 1024
NH = 16
DH = 64
NC = 8

ENGINES = ("g", "sy", "t", "v", "s")
_ENG_INC = {"g": 16, "sy": 16, "t": 1, "v": 1, "s": 1}
_EXPR_CACHE = {}



_TPB_DONE = {}


def tpb_prep(e):
    if id(e) not in _TPB_DONE:
        r = e.alloc_register64("tpbb")
        e.tpb_base_ld(r)
        _TPB_DONE[id(e)] = r
    return e.nop()


def CE(e, kv, mult, off=0):
    key = (id(e), id(kv), mult, off)
    sv = _EXPR_CACHE.get(key)
    if sv is None:
        sv = e.compute_val(kv * mult + off)
        _EXPR_CACHE[key] = sv
    return sv


class _Op:
    __slots__ = ("eng", "fn", "inc", "pos")

    def __init__(self, eng, fn, inc, pos):
        self.eng, self.fn, self.inc, self.pos = eng, fn, inc, pos


class _LoopRec:
    __slots__ = ("n", "items", "per", "pos")

    def __init__(self, n, items, per, pos):
        self.n, self.items, self.per, self.pos = n, items, per, pos


class Prog:
    def __init__(self):
        self.nc = bass.Bass("TRN2", target_bir_lowering=False, debug=False)
        self.root = []
        self._stack = [(self.root, {e: 0 for e in ENGINES})]

    def _rec(self, ename, fn, inc):
        items, cnt = self._stack[-1]
        items.append(_Op(ename, fn, inc, dict(cnt)))
        cnt[ename] += inc

    def g(self, fn):
        self._rec("g", fn, 16)

    def gc(self, fn):
        self._rec("g", fn, 1)

    def sy(self, fn):
        self._rec("sy", fn, 16)

    def t(self, fn):
        self._rec("t", fn, 1)

    def v(self, fn):
        self._rec("v", fn, 1)

    def s(self, fn):
        self._rec("s", fn, 1)

    @contextmanager
    def Loop(self, n_iters):
        assert n_iters >= 1
        items = []
        cnt = {e: 0 for e in ENGINES}
        self._stack.append((items, cnt))
        yield
        self._stack.pop()
        parent_items, parent_cnt = self._stack[-1]
        rec = _LoopRec(n_iters, items, dict(cnt), dict(parent_cnt))
        parent_items.append(rec)
        for e in ENGINES:
            parent_cnt[e] += n_iters * cnt[e]

    def emit(self, ctx):
        nc = self.nc
        self._final = dict(self._stack[0][1])
        sems = {e: ctx.enter_context(nc.semaphore(f"sem_{e}")) for e in ENGINES}
        block = ctx.enter_context(nc.Block())

        def eng_has_ops(items, ename):
            for it in items:
                if isinstance(it, _Op):
                    if it.eng == ename:
                        return True
                else:
                    if it.per[ename]:
                        return True
            return False

        def emit_stream(eng, ename):
            wscratch = eng.alloc_register("wscr")
            wscr2 = eng.alloc_register("wscr2")
            last_static = {e: 0 for e in ENGINES}
            last_sym = {}
            pending = []

            def emit_wait(e, static_tgt, kstack_pers):
                terms = [(kv, per[e]) for kv, per in kstack_pers if per[e]]
                if not terms:
                    if static_tgt > last_static[e]:
                        eng.wait_ge(sems[e], static_tgt)
                        last_static[e] = static_tgt
                    return
                key = (e, tuple(p for _, p in terms))
                if key in last_sym and static_tgt <= last_sym[key]:
                    return
                kv0, per0 = terms[0]
                eng.reg_mul(wscr2, kv0, per0)
                for kvx, perx in terms[1:]:
                    eng.reg_mul(wscratch, kvx, perx)
                    eng.reg_alu(wscr2, wscr2, wscratch, ALU.add)
                eng.reg_add(wscr2, wscr2, static_tgt)
                eng.wait_ge(sems[e], wscr2)
                last_sym[key] = static_tgt

            def flush_pending():
                while pending:
                    d = pending.pop(0)
                    for e, tgt in d.items():
                        if tgt > last_static[e]:
                            eng.wait_ge(sems[e], tgt)
                            last_static[e] = tgt

            RELAXED = ename in ("t",)

            def walk(items, base, kstack, prev):
                for it in items:
                    if isinstance(it, _Op):
                        if it.eng == ename:
                            flush_pending()
                            if RELAXED:
                                if prev[3] is not None:
                                    emit_wait(prev[3], prev[4], prev[5])
                            else:
                                if prev[0] is not None:
                                    emit_wait(prev[0], prev[1], prev[2])
                            kv = kstack[-1][0] if kstack else None
                            r = it.fn(eng, kv, [kk for kk, _ in kstack])
                            r.then_inc(sems[ename], it.inc)
                        prev[0] = it.eng
                        prev[1] = base[it.eng] + it.pos[it.eng] + it.inc
                        prev[2] = list(kstack)
                        if it.eng != ename:
                            prev[3] = it.eng
                            prev[4] = base[it.eng] + it.pos[it.eng] + it.inc
                            prev[5] = list(kstack)
                    else:
                        if it.n == 0:
                            continue
                        assert not kstack, "loops only at root level"
                        new_base = {e: base[e] + it.pos[e] for e in ENGINES}
                        if eng_has_ops(it.items, ename):
                            flush_pending()
                            if RELAXED:
                                if prev[3] is not None:
                                    emit_wait(prev[3], prev[4], prev[5])
                            elif prev[0] is not None:
                                emit_wait(prev[0], prev[1], prev[2])
                            _EXPR_CACHE.clear()
                            _loop_regs = []
                            _cb = nc.register_alloc_callback(
                                lambda *a: _loop_regs.extend(
                                    x for x in a if type(x).__name__ == "RegisterHandle"))
                            with eng.Fori(0, it.n) as kv:
                                kst = kstack + [(kv, it.per)]
                                last_sym.clear()
                                for e in ENGINES:
                                    if not it.per[e]:
                                        continue
                                    eng.reg_mul(wscr2, kv, it.per[e])
                                    eng.reg_add(wscr2, wscr2, new_base[e])
                                    eng.wait_ge(sems[e], wscr2)
                                    last_sym[(e, (it.per[e],))] = new_base[e]
                                walk(it.items, new_base, kst, prev)
                            nc.unregister_alloc_callback(_cb)
                            last_sym.clear()
                            for e in ENGINES:
                                last_static[e] = -10 ** 9
                            _EXPR_CACHE.clear()
                            try:
                                h = eng.lower_val(kv)
                                nc.free_register(h)
                            except Exception:
                                pass
                            for h in _loop_regs:
                                try:
                                    nc.free_register(h)
                                except Exception:
                                    pass
                        pending.append(
                            {e: new_base[e] + it.n * it.per[e]
                             for e in ENGINES if it.per[e]})
                        prev[0] = None
                        prev[1] = 0
                        prev[2] = []
                        prev[3] = None
                        prev[4] = 0
                        prev[5] = []

            walk(self.root, {e: 0 for e in ENGINES}, [], [None, 0, [], None, 0, []])
            flush_pending()
            for e in ENGINES:
                if e != ename and self._final[e] > last_static[e]:
                    eng.wait_ge(sems[e], self._final[e])

        @block.gpsimd
        def _(g):
            emit_stream(g, "g")

        @block.sync
        def _(sy):
            emit_stream(sy, "sy")

        @block.tensor
        def _(t):
            emit_stream(t, "t")

        @block.vector
        def _(v):
            emit_stream(v, "v")

        @block.scalar
        def _(s):
            emit_stream(s, "s")



def _rope_tables():
    i = np.arange(DH // 2, dtype=np.float32)
    theta = (10000.0 ** (-2.0 * (i - 1.0) / DH)).astype(np.float32)
    m = np.arange(T, dtype=np.float32)
    ang = (m[:, None] * theta[None, :]).astype(np.float32)
    c = np.ascontiguousarray(np.cos(ang).astype(np.float32).T)
    s = np.ascontiguousarray(np.sin(ang).astype(np.float32).T)
    cs = np.concatenate([c, s, c, s], axis=0)
    sc = np.concatenate([s, c, s, c], axis=0)
    return np.ascontiguousarray(cs), np.ascontiguousarray(sc)


def _causal_masks():
    kk = np.arange(128)[:, None]
    qq = np.arange(512)[None, :]
    masks = [(kk <= qq - 128 * j).astype(np.float32) for j in range(4)]
    return np.ascontiguousarray(np.concatenate(masks, axis=1))


def _pair_sliced(wT):
    w4 = wT.reshape(8, 128, 8, 128)
    out = w4.transpose(2, 1, 0, 3).reshape(1024, 1024)
    return np.ascontiguousarray(out)


def _chunk_rows(wT):
    n = wT.shape[1]
    w3 = wT.reshape(8, 128, n).transpose(1, 0, 2).reshape(128, 8 * n)
    return np.ascontiguousarray(out_arr(w3))


def out_arr(a):
    return np.ascontiguousarray(a)


def l0_attn_host_inputs(x_full, w_qkv, w_out, b_out, core):
    xT = np.ascontiguousarray(
        x_full[2 * core:2 * core + 2].transpose(0, 2, 1)).astype(np.float32)
    cs, sc = _rope_tables()
    return {
        "xT": xT,
        "wq_ps": _pair_sliced(np.ascontiguousarray(w_qkv[0:D].T)),
        "wk_ps": _pair_sliced(np.ascontiguousarray(w_qkv[D:2 * D].T)),
        "wv_ps": _pair_sliced(np.ascontiguousarray(w_qkv[2 * D:3 * D].T)),
        "wo_cr": _chunk_rows(np.ascontiguousarray(w_out.T)),
        "bout": np.ascontiguousarray(b_out.reshape(8, 128).T),
        "cs_tab": cs, "sc_tab": sc, "cmask": _causal_masks(),
    }



def emit_l0_attn(p, dr):
    nc = p.nc
    xT = dr["xT"]
    wq_ps = dr["wq_ps"]
    wk_ps = dr["wk_ps"]
    wv_ps = dr["wv_ps"]
    wo_cr = dr["wo_cr"]
    boutd = dr["bout"]
    cs_t = dr["cs_tab"]
    sc_t = dr["sc_tab"]
    cmaskd = dr["cmask"]
    x1 = dr["x1"]
    y_dram = nc.dram_tensor("y_dram", [D, T], F32).ap()

    with ExitStack() as ctx:
        def sb(name, shape, dt=BF16):
            return ctx.enter_context(nc.sbuf_tensor(name, shape, dt))

        def pt(name, shape):
            return ctx.enter_context(nc.psum_tensor(name, shape, F32))

        xb = sb("xb", [128, 8, T])
        wqs = sb("wqs", [128, 8, 128])
        wks = sb("wks", [128, 8, 128])
        wvs = sb("wvs", [128, 8, 128])
        wbuf = sb("wbuf", [128, 8, D])
        qp = sb("qp", [128, T]); kp = sb("kp", [128, T])
        qrp = sb("qrp", [128, T]); krp = sb("krp", [128, T])
        m1t = sb("m1t", [128, T]); m2t = sb("m2t", [128, T])
        rtmp = sb("rtmp", [128, T])
        vpair = sb("vpair", [128, T])
        ybufs = sb("ybufs", [128, 8, 512])
        cs_b = sb("cs_b", [128, T]); sc_b = sb("sc_b", [128, T])
        cm_b = sb("cm_b", [128, 2048])
        ones128 = sb("ones128", [128, 1])
        ones1 = sb("ones1", [1, 64])
        exp_s = sb("exp_s", [128, 512])
        exp_n = sb("exp_n", [128, 512])
        ysc = sb("ysc", [64, 512], F32)
        ypair1 = sb("ypair1", [128, T], F32)
        rden_bf = sb("rden_bf", [1, 512])
        bc_sb = sb("bc_sb", [64, 512])
        sum_bf = sb("sum_bf", [128, 8, 512])
        sq_bf = sb("sq_bf", [128, 512])
        bout_s = sb("bout_s", [128, 8], F32)
        xres = sb("xres", [128, 512], F32)
        sprev = sb("sprev", [128, 512], F32)
        outf = sb("outf", [128, 512], F32)
        rden_f = sb("rden_f", [1, 512], F32)
        mrow = sb("mrow", [1, 512], F32)
        m2row = sb("m2row", [1, 512], F32)
        varr = sb("varr", [1, 512], F32)
        mrow_bf = sb("mrow_bf", [1, 512])
        rstd_bf = sb("rstd_bf", [1, 512])
        mb_sb = sb("mb_sb", [128, 512], F32)
        rs_sb = sb("rs_sb", [128, 512], F32)
        dum = sb("dum", [1, 1], F32)

        psA = pt("psA", [128, 512])
        psB = pt("psB", [128, 512])
        psC = pt("psC", [128, 512])
        psD = pt("psD", [1, 512])
        psE = pt("psE", [1, 512])
        psY = pt("psY", [64, 512])

        p.g(lambda e, k, ks: e.dma_start(out=cs_b[:, :], in_=cs_t))
        p.g(lambda e, k, ks: e.dma_start(out=sc_b[:, :], in_=sc_t))
        p.g(lambda e, k, ks: e.dma_start(out=cm_b[:, :], in_=cmaskd))
        p.g(lambda e, k, ks: e.dma_start(out=bout_s[:, :], in_=boutd))
        p.v(lambda e, k, ks: e.memset(ones128[:, :], 1.0))
        p.v(lambda e, k, ks: e.memset(ones1[:, :], 1.0))

        for b in range(B_CORE):
            for dc in range(8):
                p.g(lambda e, k, ks, b=b, dc=dc: e.dma_start(
                    out=xb[:, dc, :], in_=xT[b, ds(dc * 128, 128), :]))

            with p.Loop(8):
                for wsl, wdr in ((wqs, wq_ps), (wks, wk_ps), (wvs, wv_ps)):
                    p.g(lambda e, k, ks, wsl=wsl, wdr=wdr: e.dma_start(
                        out=wsl[:, :, :].rearrange("p c f -> p (c f)"),
                        in_=wdr[ts(k, 128), :]))
                for tc in range(4):
                    for (wsl, dst, ps) in ((wqs, qp, psA), (wks, kp, psB)):
                        for dc in range(8):
                            p.t(lambda e, k, ks, wsl=wsl, ps=ps, dc=dc, tc=tc: e.matmul(
                                ps[:, :], wsl[:, dc, :], xb[:, dc, ds(tc * 512, 512)],
                                start=(dc == 0), stop=(dc == 7)))
                        p.v(lambda e, k, ks, dst=dst, ps=ps, tc=tc: e.tensor_copy(
                            dst[:, ds(tc * 512, 512)], ps[:, :]))
                for tt in range(16):
                    for dc in range(8):
                        p.t(lambda e, k, ks, dc=dc, tt=tt: e.matmul(
                            psC[:, 0:128], xb[:, dc, ds(tt * 128, 128)], wvs[:, dc, :],
                            start=(dc == 0), stop=(dc == 7)))
                    p.v(lambda e, k, ks, tt=tt: e.tensor_copy(
                        vpair[:, ds(tt * 128, 128)], psC[:, 0:128]))
                for (src, dst) in ((qp, qrp), (kp, krp)):
                    p.v(lambda e, k, ks, src=src: e.tensor_tensor(
                        m1t[:, :], src[:, :], cs_b[:, :], ALU.mult))
                    p.v(lambda e, k, ks, src=src: e.tensor_tensor(
                        m2t[:, :], src[:, :], sc_b[:, :], ALU.mult))
                    for h in range(2):
                        b0 = h * 64
                        p.v(lambda e, k, ks, b0=b0: e.tensor_copy(
                            rtmp[b0:b0 + 32, :], m1t[b0 + 32:b0 + 64, :]))
                        p.v(lambda e, k, ks, dst=dst, b0=b0: e.tensor_tensor(
                            dst[b0:b0 + 32, :], m1t[b0:b0 + 32, :],
                            rtmp[b0:b0 + 32, :], ALU.add))
                        p.v(lambda e, k, ks, b0=b0: e.tensor_copy(
                            rtmp[b0 + 32:b0 + 64, :], m2t[b0:b0 + 32, :]))
                        p.v(lambda e, k, ks, dst=dst, b0=b0: e.tensor_tensor(
                            dst[b0 + 32:b0 + 64, :], m2t[b0 + 32:b0 + 64, :],
                            rtmp[b0 + 32:b0 + 64, :], ALU.subtract))

                for h in range(2):
                    hr = h * 64
                    for qc in range(4):
                        for kc in range(16):
                            p.t(lambda e, k, ks, hr=hr, qc=qc, kc=kc: e.matmul(
                                psA[:, :], kp[hr:hr + 64, ds(kc * 128, 128)],
                                qp[hr:hr + 64, ds(qc * 512, 512)], start=True, stop=True))
                            p.s(lambda e, k, ks: e.activation(
                                exp_s[:, :], psA[:, :], AF.Exp, scale=0.125))
                            p.t(lambda e, k, ks, kc=kc: e.matmul(
                                psD[:, :], ones128[:, :], exp_s[:, :],
                                start=(kc == 0), stop=(kc == 15)))
                            if kc <= 4 * qc + 3:
                                p.t(lambda e, k, ks, hr=hr, qc=qc, kc=kc: e.matmul(
                                    psA[:, :], krp[hr:hr + 64, ds(kc * 128, 128)],
                                    qrp[hr:hr + 64, ds(qc * 512, 512)],
                                    start=True, stop=True))
                                p.s(lambda e, k, ks: e.activation(
                                    exp_n[:, :], psA[:, :], AF.Exp, scale=0.125))
                                if kc >= 4 * qc:
                                    j = kc - 4 * qc
                                    p.v(lambda e, k, ks, j=j: e.tensor_tensor(
                                        exp_n[:, :], exp_n[:, :],
                                        cm_b[:, ds(j * 512, 512)], ALU.mult))
                                p.t(lambda e, k, ks, h=h, kc=kc: e.matmul(
                                    psY[:, :], vpair[:, ds(kc * 128 + h * 64, 64)],
                                    exp_n[:, :], start=(kc == 0), stop=False))
                        p.v(lambda e, k, ks: e.reciprocal(rden_f[:, :], psD[:, :]))
                        p.v(lambda e, k, ks: e.tensor_copy(rden_bf[:, :], rden_f[:, :]))
                        p.t(lambda e, k, ks: e.matmul(
                            psB[0:64, :], ones1[:, :], rden_bf[:, :], start=True, stop=True))
                        p.v(lambda e, k, ks: e.tensor_copy(bc_sb[:, :], psB[0:64, :]))
                        p.v(lambda e, k, ks, h=h, qc=qc: e.tensor_tensor(
                            ypair1[h * 64:h * 64 + 64, ds(qc * 512, 512)],
                            psY[:, :], bc_sb[:, :], ALU.mult))
                p.sy(lambda e, k, ks: e.dma_start(
                    out=y_dram[ts(k, 128), :], in_=ypair1[:, :]))

            p.g(lambda e, k, ks: e.dma_start(
                out=wbuf[:, :, :].rearrange("p c f -> p (c f)"), in_=wo_cr))
            for tc in range(4):
                for dc in range(8):
                    p.g(lambda e, k, ks, dc=dc, tc=tc: e.dma_start(
                        out=ybufs[:, dc, :],
                        in_=y_dram[ds(dc * 128, 128), ds(tc * 512, 512)]))
                for oc in range(8):
                    for dc in range(8):
                        p.t(lambda e, k, ks, dc=dc, oc=oc, tc=tc: e.matmul(
                            psA[:, :], wbuf[:, dc, ds(oc * 128, 128)],
                            ybufs[:, dc, :],
                            start=(dc == 0), stop=(dc == 7)))
                    p.g(lambda e, k, ks, b=b, oc=oc, tc=tc: e.dma_start(
                        out=xres[:, :], in_=xT[b, ds(oc * 128, 128), ds(tc * 512, 512)]))
                    p.v(lambda e, k, ks, oc=oc: e.tensor_scalar(
                        sprev[:, :], psA[:, :], bout_s[:, oc:oc + 1], None, ALU.add))
                    p.v(lambda e, k, ks: e.tensor_tensor(
                        sprev[:, :], sprev[:, :], xres[:, :], ALU.add))
                    p.v(lambda e, k, ks, oc=oc: e.tensor_copy(sum_bf[:, oc, :], sprev[:, :]))
                    p.s(lambda e, k, ks: e.activation(sq_bf[:, :], sprev[:, :], AF.Square))
                    p.t(lambda e, k, ks, oc=oc: e.matmul(
                        psD[:, :], ones128[:, :], sum_bf[:, oc, :],
                        start=(oc == 0), stop=(oc == 7)))
                    p.t(lambda e, k, ks, oc=oc: e.matmul(
                        psE[:, :], ones128[:, :], sq_bf[:, :],
                        start=(oc == 0), stop=(oc == 7)))
                p.v(lambda e, k, ks: e.tensor_scalar(
                    mrow[:, :], psD[:, :], 1.0 / D, None, ALU.mult))
                p.v(lambda e, k, ks: e.tensor_scalar(
                    m2row[:, :], psE[:, :], 1.0 / D, None, ALU.mult))
                p.v(lambda e, k, ks: e.tensor_tensor(
                    varr[:, :], mrow[:, :], mrow[:, :], ALU.mult))
                p.v(lambda e, k, ks: e.tensor_tensor(
                    varr[:, :], m2row[:, :], varr[:, :], ALU.subtract))
                p.v(lambda e, k, ks: e.tensor_scalar(varr[:, :], varr[:, :], 1e-5, None, ALU.add))
                p.s(lambda e, k, ks: e.activation(varr[:, :], varr[:, :], AF.Sqrt))
                p.v(lambda e, k, ks: e.reciprocal(varr[:, :], varr[:, :]))
                p.v(lambda e, k, ks: e.tensor_copy(mrow_bf[:, :], mrow[:, :]))
                p.v(lambda e, k, ks: e.tensor_copy(rstd_bf[:, :], varr[:, :]))
                p.t(lambda e, k, ks: e.matmul(
                    psB[0:64, :], ones1[:, :], mrow_bf[:, :], start=True, stop=True))
                p.t(lambda e, k, ks: e.matmul(
                    psC[0:64, :], ones1[:, :], rstd_bf[:, :], start=True, stop=True))
                p.v(lambda e, k, ks: e.tensor_copy(mb_sb[0:64, :], psB[0:64, :]))
                p.v(lambda e, k, ks: e.tensor_copy(rs_sb[0:64, :], psC[0:64, :]))
                p.v(lambda e, k, ks: e.tensor_copy(mb_sb[64:128, :], psB[0:64, :]))
                p.v(lambda e, k, ks: e.tensor_copy(rs_sb[64:128, :], psC[0:64, :]))
                for oc in range(8):
                    p.v(lambda e, k, ks, oc=oc: e.tensor_tensor(
                        outf[:, :], sum_bf[:, oc, :], mb_sb[:, :], ALU.subtract))
                    p.v(lambda e, k, ks: e.tensor_tensor(
                        outf[:, :], outf[:, :], rs_sb[:, :], ALU.mult))
                    p.sy(lambda e, k, ks, b=b, oc=oc, tc=tc: e.dma_start(
                        out=x1[b, ds(oc * 128, 128), ds(tc * 512, 512)], in_=outf[:, :]))

        p.gc(lambda e, k, ks: e.memset(dum[:, :], 0.0))



def build_l0_attn():
    p = Prog()
    nc = p.nc
    xT = nc.dram_tensor("xT", [B_CORE, D, T], F32, kind="ExternalInput").ap()
    wq_ps = nc.dram_tensor("wq_ps", [D, D], F32, kind="ExternalInput").ap()
    wk_ps = nc.dram_tensor("wk_ps", [D, D], F32, kind="ExternalInput").ap()
    wv_ps = nc.dram_tensor("wv_ps", [D, D], F32, kind="ExternalInput").ap()
    wo_cr = nc.dram_tensor("wo_cr", [128, 8 * D], F32, kind="ExternalInput").ap()
    boutd = nc.dram_tensor("bout", [128, 8], F32, kind="ExternalInput").ap()
    cs_t = nc.dram_tensor("cs_tab", [128, T], F32, kind="ExternalInput").ap()
    sc_t = nc.dram_tensor("sc_tab", [128, T], F32, kind="ExternalInput").ap()
    cmaskd = nc.dram_tensor("cmask", [128, 2048], F32, kind="ExternalInput").ap()
    x1 = nc.dram_tensor("x1", [B_CORE, D, T], F32, kind="ExternalOutput").ap()
    y_dram = nc.dram_tensor("y_dram", [D, T], F32).ap()

    dr = {"xT": xT,
          "wq_ps": wq_ps,
          "wk_ps": wk_ps,
          "wv_ps": wv_ps,
          "wo_cr": wo_cr,
          "bout": boutd,
          "cs_tab": cs_t,
          "sc_tab": sc_t,
          "cmask": cmaskd,
          "x1": x1}
    emit_l0_attn(p, dr)
    with ExitStack() as _ctx2:
        p.emit(_ctx2)
    return p.nc



def l0_ffn_host_inputs(x1_unused, inputs, core):
    wg = inputs["L0_ffn_w_gate"]; wu = inputs["L0_ffn_w_up"]; wd = inputs["L0_ffn_w_down"]
    sg = inputs["L0_ffn_shared_gate_w"]; su = inputs["L0_ffn_shared_up_w"]
    sd = inputs["L0_ffn_shared_down_w"]
    def cr(mat):
        return mat.reshape(8, 128, 1024).transpose(1, 0, 2).reshape(128, 8192)
    wg_cr = np.stack([cr(np.ascontiguousarray(wg[e].T)) for e in range(16)]
                     + [cr(np.ascontiguousarray(sg.T))], 0).reshape(17 * 128, 8192)
    wu_cr = np.stack([cr(np.ascontiguousarray(wu[e].T)) for e in range(16)]
                     + [cr(np.ascontiguousarray(su.T))], 0).reshape(17 * 128, 8192)
    wd_cr = np.stack([cr(np.ascontiguousarray(wd[e].T)) for e in range(16)]
                     + [cr(np.ascontiguousarray(sd.T))], 0).reshape(17 * 128, 8192)
    gwT = np.ascontiguousarray(inputs["L0_ffn_gate_w"].T)
    gw_cr = gwT.reshape(8, 128, 16).transpose(1, 0, 2).reshape(128, 128)
    ebias = np.ascontiguousarray(
        np.broadcast_to(inputs["L0_ffn_expert_bias"][None, :], (128, 16)).copy())
    return {"wg_cr": np.ascontiguousarray(wg_cr.astype(np.float32)),
            "wu_cr": np.ascontiguousarray(wu_cr.astype(np.float32)),
            "wd_cr": np.ascontiguousarray(wd_cr.astype(np.float32)),
            "gw_cr": np.ascontiguousarray(gw_cr.astype(np.float32)),
            "ebias": ebias.astype(np.float32)}


TT = 4096


def emit_l0_ffn(p, dr):
    nc = p.nc
    x1d = dr["x1"]
    wg_cr = dr["wg_cr"]
    wu_cr = dr["wu_cr"]
    wd_cr = dr["wd_cr"]
    gw_crd = dr["gw_cr"]
    ebiasd = dr["ebias"]
    x2 = dr["x2"]
    comb_dram = nc.dram_tensor("comb_dram", [17, 4096], F32).ap()
    y_dram = nc.dram_tensor("y_dram2", [128, 8 * 4096], F32).ap()

    with ExitStack() as ctx:
        def sb(name, shape, dt=BF16):
            return ctx.enter_context(nc.sbuf_tensor(name, shape, dt))

        def pt(name, shape):
            return ctx.enter_context(nc.psum_tensor(name, shape, F32))

        xc = sb("xc", [128, 8, TT])
        wgs = sb("wgs", [128, 8, 1024])
        wus = sb("wus", [128, 8, 1024])
        wds = sb("wds", [128, 8, 1024])
        gws = sb("gws", [128, 8, 16])
        hbuf = sb("hbuf", [128, 4, 1024])
        g_sb = sb("g_sb", [16, TT])
        idtb = sb("idtb", [128, 128])
        gtok = sb("gtok", [128, 32 * 16], F32)
        comb_all = sb("comb_all", [128, 32 * 16], F32)
        gwork = sb("gwork", [128, 16], F32)
        ctmp = sb("ctmp", [16, 128], F32)
        ysc = sb("ysc2", [128, 512], F32)
        scores = sb("scores", [128, 16], F32)
        sel = sb("sel", [128, 16], F32)
        msum = sb("msum", [128, 16], F32)
        mx1 = sb("mx1", [128, 1], F32)
        rs1 = sb("rs1", [128, 1], F32)
        ebias_s = sb("ebias_s", [128, 16], F32)
        ce_sb = sb("ce_sb", [1, 4096], F32)
        ce_bf = sb("ce_bf", [1, 4096])
        ce_bc = sb("ce_bc", [128, 512])
        htmp = sb("htmp", [128, 1024])
        hfin = sb("hfin", [128, 512])
        idt = sb("idt", [128, 128], F32)
        ones1_128 = sb("ones1_128", [1, 128])
        ones128 = sb("ones128b", [128, 1])
        sum_bf = sb("sum_bf2", [128, 8, 512])
        sq_bf = hfin
        xres = sb("xres2", [128, 512], F32)
        sprev = sb("sprev2", [128, 512], F32)
        outf = sb("outf2", [128, 512], F32)
        mrow = sb("mrow2", [1, 512], F32)
        m2row = sb("m2row2", [1, 512], F32)
        varr = sb("varr2", [1, 512], F32)
        mrow_bf = sb("mrow_bf2", [1, 512])
        rstd_bf = sb("rstd_bf2", [1, 512])
        mb_sb = sb("mb_sb2", [128, 512])
        rs_sb = sb("rs_sb2", [128, 512])
        ones1 = sb("ones1b", [1, 64])
        dum = sb("dum2", [1, 1], F32)

        psG = pt("psG", [128, 1024])
        psU = pt("psU", [128, 1024])
        psY2 = pt("psY2", [128, 512])
        psB2 = psY2
        psS = pt("psS", [1, 512])
        psT = ctx.enter_context(nc.psum_tensor("psT", [128, 16], BF16))
        psQ = pt("psQ", [1, 512])

        p.g(lambda e, k, ks: e.dma_start(out=gws[:, :, :].rearrange("p c f -> p (c f)"), in_=gw_crd))
        p.g(lambda e, k, ks: e.dma_start(out=ebias_s[:, :], in_=ebiasd))
        p.v(lambda e, k, ks: e.memset(ones1_128[:, :], 1.0))
        p.v(lambda e, k, ks: e.memset(ones128[:, :], 1.0))
        p.v(lambda e, k, ks: e.memset(ones1[:, :], 1.0))
        p.gc(lambda e, k, ks: e.iota(idt[:, :], pattern=[[-1, 128]], base=0,
                                     channel_multiplier=1,
                                     allow_small_or_imprecise_dtypes=True))
        p.v(lambda e, k, ks: e.tensor_scalar(idt[:, :], idt[:, :], 0.0, None, ALU.is_equal))
        p.v(lambda e, k, ks: e.tensor_copy(idtb[:, :], idt[:, :]))
        for b in range(B_CORE):
            for dc in range(8):
                p.g(lambda e, k, ks, b=b, dc=dc: e.dma_start(
                    out=xc[:, dc, ds(b * T, T)], in_=x1d[b, ds(dc * 128, 128), :]))
        p.v(lambda e, k, ks: e.memset(ce_sb[:, :], 1.0))
        p.sy(lambda e, k, ks: e.dma_start(out=comb_dram[16:17, :], in_=ce_sb[:, :]))

        with p.Loop(8):
            for dc in range(8):
                p.t(lambda e, k, ks, dc=dc: e.matmul(
                    psB2[0:16, :], gws[:, dc, :], xc[:, dc, ts(k, 512)],
                    start=(dc == 0), stop=(dc == 7)))
            p.v(lambda e, k, ks: e.tensor_copy(g_sb[:, ts(k, 512)], psB2[0:16, :]))
        for tile in range(32):
            p.t(lambda e, k, ks, tile=tile: e.transpose(
                psT[:, :], g_sb[:, ds(tile * 128, 128)], idtb[0:16, 0:16]))
            p.v(lambda e, k, ks, tile=tile: e.tensor_copy(
                gtok[:, ds(tile * 16, 16)], psT[:, :]))
        with p.Loop(32):
            p.v(lambda e, k, ks: e.tensor_copy(gwork[:, :], gtok[:, ts(k, 16)]))
            p.s(lambda e, k, ks: e.activation(scores[:, :], gwork[:, :], AF.Sigmoid))
            p.v(lambda e, k, ks: e.tensor_tensor(sel[:, :], scores[:, :], ebias_s[:, :], ALU.add))
            p.v(lambda e, k, ks: e.memset(msum[:, :], 0.0))
            for i in range(8):
                p.v(lambda e, k, ks: e.tensor_reduce(mx1[:, :], sel[:, :], AX.X, ALU.max))
                p.v(lambda e, k, ks: e.tensor_scalar(gwork[:, :], sel[:, :], mx1[:, 0:1], None, ALU.is_ge))
                p.v(lambda e, k, ks: e.tensor_tensor(msum[:, :], msum[:, :], gwork[:, :], ALU.add))
                p.v(lambda e, k, ks: e.scalar_tensor_tensor(
                    sel[:, :], gwork[:, :], -1e30, sel[:, :], ALU.mult, ALU.add))
            p.v(lambda e, k, ks: e.tensor_tensor(gwork[:, :], scores[:, :], msum[:, :], ALU.mult))
            p.v(lambda e, k, ks: e.tensor_reduce(rs1[:, :], gwork[:, :], AX.X, ALU.add))
            p.v(lambda e, k, ks: e.reciprocal(rs1[:, :], rs1[:, :]))
            p.v(lambda e, k, ks: e.tensor_scalar(gwork[:, :], gwork[:, :], rs1[:, 0:1], 2.826, ALU.mult, op1=ALU.mult))
            p.v(lambda e, k, ks: e.tensor_copy(comb_all[:, ts(k, 16)], gwork[:, :]))
        for tile in range(32):
            p.t(lambda e, k, ks, tile=tile: e.transpose(
                psB2[0:16, 0:128], comb_all[:, ds(tile * 16, 16)], idt[:, :]))
            p.v(lambda e, k, ks, tile=tile: e.tensor_copy(
                ctmp[:, :], psB2[0:16, 0:128]))
            p.sy(lambda e, k, ks, tile=tile: e.dma_start(
                out=comb_dram[0:16, ds(tile * 128, 128)], in_=ctmp[:, :]))

        p.v(lambda e, k, ks: e.memset(ysc[:, :], 0.0))
        for oc in range(8):
            for tcg in range(8):
                p.sy(lambda e, k, ks, oc=oc, tcg=tcg: e.dma_start(
                    out=y_dram[:, ds(oc * 4096 + tcg * 512, 512)], in_=ysc[:, :]))

        with p.Loop(17):
            p.g(lambda e, k, ks: e.dma_start(
                out=wgs[:, :, :].rearrange("p c f -> p (c f)"), in_=wg_cr[ts(k, 128), :]))
            p.g(lambda e, k, ks: e.dma_start(
                out=wus[:, :, :].rearrange("p c f -> p (c f)"), in_=wu_cr[ts(k, 128), :]))
            p.g(lambda e, k, ks: e.dma_start(
                out=wds[:, :, :].rearrange("p c f -> p (c f)"), in_=wd_cr[ts(k, 128), :]))
            p.g(lambda e, k, ks: e.dma_start(out=ce_sb[:, :], in_=comb_dram[ts(k, 1), :]))
            p.v(lambda e, k, ks: e.tensor_copy(ce_bf[:, :], ce_sb[:, :]))
            for tcg in range(8):
                t0 = tcg * 512
                p.t(lambda e, k, ks, t0=t0: e.matmul(
                    psB2[:, :], ones1_128[:, :], ce_bf[:, ds(t0, 512)],
                    start=True, stop=True))
                p.v(lambda e, k, ks: e.tensor_copy(ce_bc[:, :], psB2[:, :]))
                for pr in range(4):
                    for h in range(2):
                        ic = 2 * pr + h
                        for dc in range(8):
                            p.t(lambda e, k, ks, dc=dc, ic=ic, h=h, t0=t0: e.matmul(
                                psG[:, ds(h * 512, 512)], wgs[:, dc, ds(ic * 128, 128)],
                                xc[:, dc, ds(t0, 512)], start=(dc == 0), stop=(dc == 7)))
                        for dc in range(8):
                            p.t(lambda e, k, ks, dc=dc, ic=ic, h=h, t0=t0: e.matmul(
                                psU[:, ds(h * 512, 512)], wus[:, dc, ds(ic * 128, 128)],
                                xc[:, dc, ds(t0, 512)], start=(dc == 0), stop=(dc == 7)))
                    p.s(lambda e, k, ks: e.activation(htmp[:, :], psG[:, :], AF.Silu))
                    p.v(lambda e, k, ks, pr=pr: e.tensor_tensor(
                        hbuf[:, pr, :], htmp[:, :], psU[:, :], ALU.mult))
                for oc in range(8):
                    for ic in range(8):
                        p.t(lambda e, k, ks, ic=ic, oc=oc: e.matmul(
                            psY2[:, :], wds[:, ic, ds(oc * 128, 128)],
                            hbuf[:, ic // 2, ds((ic % 2) * 512, 512)],
                            start=(ic == 0), stop=(ic == 7)))
                    p.v(lambda e, k, ks: e.tensor_tensor(
                        ysc[:, :], psY2[:, :], ce_bc[:, :], ALU.mult))
                    p.g(lambda e, k, ks, oc=oc, t0=t0: e.dma_start(
                        out=y_dram[:, ds(oc * 4096 + t0, 512)], in_=ysc[:, :],
                        accum_op=ALU.add))

        for tcg in range(8):
            t0 = tcg * 512
            b = t0 // T
            toff = t0 % T
            for oc in range(8):
                p.g(lambda e, k, ks, oc=oc, t0=t0: e.dma_start(
                    out=sprev[:, :], in_=y_dram[:, ds(oc * 4096 + t0, 512)]))
                p.g(lambda e, k, ks, b=b, oc=oc, toff=toff: e.dma_start(
                    out=xres[:, :], in_=x1d[b, ds(oc * 128, 128), ds(toff, 512)]))
                p.v(lambda e, k, ks: e.tensor_tensor(sprev[:, :], sprev[:, :], xres[:, :], ALU.add))
                p.v(lambda e, k, ks, oc=oc: e.tensor_copy(sum_bf[:, oc, :], sprev[:, :]))
                p.s(lambda e, k, ks: e.activation(sq_bf[:, :], sprev[:, :], AF.Square))
                p.t(lambda e, k, ks, oc=oc: e.matmul(
                    psS[:, :], ones128[:, :], sum_bf[:, oc, :],
                    start=(oc == 0), stop=(oc == 7)))
                p.t(lambda e, k, ks, oc=oc: e.matmul(
                    psQ[:, :], ones128[:, :], sq_bf[:, :],
                    start=(oc == 0), stop=(oc == 7)))
            p.v(lambda e, k, ks: e.tensor_scalar(mrow[:, :], psS[:, :], 1.0 / D, None, ALU.mult))
            p.v(lambda e, k, ks: e.tensor_scalar(m2row[:, :], psQ[:, :], 1.0 / D, None, ALU.mult))
            p.v(lambda e, k, ks: e.tensor_tensor(varr[:, :], mrow[:, :], mrow[:, :], ALU.mult))
            p.v(lambda e, k, ks: e.tensor_tensor(varr[:, :], m2row[:, :], varr[:, :], ALU.subtract))
            p.v(lambda e, k, ks: e.tensor_scalar(varr[:, :], varr[:, :], 1e-5, None, ALU.add))
            p.s(lambda e, k, ks: e.activation(varr[:, :], varr[:, :], AF.Sqrt))
            p.v(lambda e, k, ks: e.reciprocal(varr[:, :], varr[:, :]))
            p.v(lambda e, k, ks: e.tensor_copy(mrow_bf[:, :], mrow[:, :]))
            p.v(lambda e, k, ks: e.tensor_copy(rstd_bf[:, :], varr[:, :]))
            p.t(lambda e, k, ks: e.matmul(psB2[0:64, :], ones1[:, :], mrow_bf[:, :], start=True, stop=True))
            p.t(lambda e, k, ks: e.matmul(psG[0:64, 0:512], ones1[:, :], rstd_bf[:, :], start=True, stop=True))
            p.v(lambda e, k, ks: e.tensor_copy(mb_sb[0:64, :], psB2[0:64, :]))
            p.v(lambda e, k, ks: e.tensor_copy(rs_sb[0:64, :], psG[0:64, 0:512]))
            p.v(lambda e, k, ks: e.tensor_copy(mb_sb[64:128, :], psB2[0:64, :]))
            p.v(lambda e, k, ks: e.tensor_copy(rs_sb[64:128, :], psG[0:64, 0:512]))
            for oc in range(8):
                p.v(lambda e, k, ks, oc=oc: e.tensor_tensor(
                    outf[:, :], sum_bf[:, oc, :], mb_sb[:, :], ALU.subtract))
                p.v(lambda e, k, ks: e.tensor_tensor(outf[:, :], outf[:, :], rs_sb[:, :], ALU.mult))
                p.sy(lambda e, k, ks, b=b, oc=oc, toff=toff: e.dma_start(
                    out=x2[b, ds(oc * 128, 128), ds(toff, 512)], in_=outf[:, :]))

        p.gc(lambda e, k, ks: e.memset(dum[:, :], 0.0))


def build_l0_ffn():
    p = Prog()
    nc = p.nc
    x1d = nc.dram_tensor("x1", [B_CORE, D, T], F32, kind="ExternalInput").ap()
    wg_cr = nc.dram_tensor("wg_cr", [17 * 128, 8192], F32, kind="ExternalInput").ap()
    wu_cr = nc.dram_tensor("wu_cr", [17 * 128, 8192], F32, kind="ExternalInput").ap()
    wd_cr = nc.dram_tensor("wd_cr", [17 * 128, 8192], F32, kind="ExternalInput").ap()
    gw_crd = nc.dram_tensor("gw_cr", [128, 128], F32, kind="ExternalInput").ap()
    ebiasd = nc.dram_tensor("ebias", [128, 16], F32, kind="ExternalInput").ap()
    x2 = nc.dram_tensor("x2", [B_CORE, D, T], F32, kind="ExternalOutput").ap()
    comb_dram = nc.dram_tensor("comb_dram", [17, 4096], F32).ap()
    y_dram = nc.dram_tensor("y_dram2", [128, 8 * 4096], F32).ap()


    TT = 4096

    dr = {"x1": x1d,
          "wg_cr": wg_cr,
          "wu_cr": wu_cr,
          "wd_cr": wd_cr,
          "gw_cr": gw_crd,
          "ebias": ebiasd,
          "x2": x2}
    emit_l0_ffn(p, dr)
    with ExitStack() as _ctx2:
        p.emit(_ctx2)
    return p.nc



def _l1_band_masks():
    kk = np.arange(128)[:, None]
    qq = np.arange(128)[None, :]
    blocks = []
    for dlt in (-1, 0, 1):
        rel = kk + 128 * dlt - qq
        blocks.append((np.abs(rel) <= 3).astype(np.float32))
    m = np.concatenate(blocks, axis=1)
    mL = m.copy(); mL[:, 0:128] = 0.0
    mR = m.copy(); mR[:, 256:384] = 0.0
    return np.ascontiguousarray(np.concatenate([m, mL, mR], axis=1))


def _edge_weights():
    wh = np.zeros((8, 3), np.float32)
    for q in range(3):
        for r in range(6):
            wh[r, q] = 1.0 if r <= q + 3 else 0.0
        wh[6, q] = 3 - q
    wt = np.zeros((8, 3), np.float32)
    for j in range(3):
        for r in range(6):
            wt[r, j] = 1.0 if r >= j else 0.0
        wt[6, j] = j + 1
    return np.ascontiguousarray(wh), np.ascontiguousarray(wt)


def l1_attn_host_inputs(inputs):
    Wq = inputs["L1_attn_W_qkv"]
    wh, wt = _edge_weights()
    b = inputs["L1_attn_b_qkv"]
    return {
        "wq1": _pair_sliced(np.ascontiguousarray(Wq[0:D].T)),
        "wk1": _pair_sliced(np.ascontiguousarray(Wq[D:2 * D].T)),
        "wv1": _pair_sliced(np.ascontiguousarray(Wq[2 * D:3 * D].T)),
        "bq1": np.ascontiguousarray(b[0:D].reshape(8, 128).T),
        "bk1": np.ascontiguousarray(b[D:2 * D].reshape(8, 128).T),
        "bv1": np.ascontiguousarray(b[2 * D:3 * D].reshape(8, 128).T),
        "wlatT": np.ascontiguousarray(inputs["L1_attn_W_lat"].T),
        "blat": np.ascontiguousarray(inputs["L1_attn_b_lat"].reshape(64, 1)),
        "wproj": _chunk_rows(np.ascontiguousarray(inputs["L1_attn_W_proj"].T)),
        "bproj": np.ascontiguousarray(inputs["L1_attn_b_proj"].reshape(8, 128).T),
        "wg1": _chunk_rows(np.ascontiguousarray(inputs["L1_ffn_Wg"])),
        "bands": _l1_band_masks(),
        "ewh": wh, "ewt": wt,
        "U128": np.triu(np.ones((128, 128), np.float32)),
        "U16s": np.triu(np.ones((16, 16), np.float32), 1),
        "U32s": np.triu(np.ones((32, 32), np.float32), 1),
    }


def emit_l1_attn(p, dr):
    nc = p.nc
    x2d = dr["x2"]
    wq1 = dr["wq1"]
    wk1 = dr["wk1"]
    wv1 = dr["wv1"]
    bq1 = dr["bq1"]
    bk1 = dr["bk1"]
    bv1 = dr["bv1"]
    wlatT = dr["wlatT"]
    blatd = dr["blat"]
    wproj = dr["wproj"]
    bprojd = dr["bproj"]
    wg1 = dr["wg1"]
    bandsd = dr["bands"]
    ewhd = dr["ewh"]
    ewtd = dr["ewt"]
    U128d = dr["U128"]
    U16sd = dr["U16s"]
    U32sd = dr["U32s"]
    x3bf = dr["x3bf"]
    hist_o = dr["hist"]
    gates_o = dr["gates"]
    ohT_o = dr["ohT"]
    locT_o = dr["locT"]
    slots_o = dr["slots"]
    t2e_o = dr["t2e"]
    y_dram = nc.dram_tensor("y_dram3", [D, T], F32).ap()
    cb_dram = nc.dram_tensor("cb_dram", [32, 16], F32).ap()

    with ExitStack() as ctx:
        def sb(name, shape, dt=BF16):
            return ctx.enter_context(nc.sbuf_tensor(name, shape, dt))

        def pt(name, shape, dt=F32):
            return ctx.enter_context(nc.psum_tensor(name, shape, dt))

        xb = sb("xb3", [128, 8, T])
        wqs = sb("wqs3", [128, 8, 128])
        wks = sb("wks3", [128, 8, 128])
        wvs = sb("wvs3", [128, 8, 128])
        wbuf = sb("wbuf3", [128, 8, D])
        x3sb = sb("x3sb", [128, 8, 4096])
        qp = sb("qp3", [128, T]); kp = sb("kp3", [128, T]); vp = sb("vp3", [128, T])
        klat = sb("klat3", [128, T])
        vlt = sb("vlt3", [128, 16 * 64])
        wlat_s = sb("wlat_s", [128, 64])
        qe3 = sb("qe3", [64, 3])
        blat_s = sb("blat_s", [64, 1], F32)
        blat_r = sb("blat_r", [1, 64])
        bq_s = sb("bq_s", [128, 8], F32)
        bk_s = sb("bk_s", [128, 8], F32)
        bv_s = sb("bv_s", [128, 8], F32)
        bproj_s = sb("bproj_s", [128, 8], F32)
        bands = sb("bands_s", [128, 1152])
        mexp = sb("mexp", [128, 384])
        denr = sb("denr", [1, 384], F32)
        rden = sb("rden3", [1, 128], F32)
        rdenb = sb("rdenb3", [1, 128])
        bcy = sb("bcy3", [64, 128])
        ysc = sb("ysc3", [64, 128], F32)
        y_pair = sb("y_pair", [128, T], F32)
        ones128 = sb("ones128c", [128, 1])
        ones1 = sb("ones1c", [1, 64])
        ones8 = sb("ones8", [8, 1])
        ones16c = sb("ones16c", [16, 1])
        onesf1 = sb("onesf1", [1, 128], F32)
        ones1t = sb("ones1t", [1, 128])
        ke = sb("ke", [64, 8])
        ve = sb("ve", [8, 64])
        ewh_s = sb("ewh_s", [8, 3], F32)
        ewt_s = sb("ewt_s", [8, 3], F32)
        ee = sb("ee", [8, 3], F32)
        web = sb("web", [8, 3])
        rde = sb("rde", [1, 3], F32)
        rdeb = sb("rdeb", [1, 3])
        bce = sb("bce", [64, 3])
        ysce = sb("ysce", [64, 3], F32)
        ybufs = sb("ybufs3", [128, 8, 512])
        sum_bf = sb("sum_bf3", [128, 8, 512])
        sq_bf = sb("sq_bf3", [128, 512])
        xres = sb("xres3", [128, 512], F32)
        sprev = sb("sprev3", [128, 512], F32)
        outf = sb("outf3", [128, 512], F32)
        mrow = sb("mrow3", [1, 512], F32)
        m2row = sb("m2row3", [1, 512], F32)
        varr = sb("varr3", [1, 512], F32)
        mrow_bf = sb("mrow_bf3", [1, 512])
        rstd_bf = sb("rstd_bf3", [1, 512])
        mb_sb = sb("mb_sb3", [128, 512])
        rs_sb = sb("rs_sb3", [128, 512])
        idt = sb("idt3", [128, 128], F32)
        idb = sb("idb3", [128, 128])
        xtb = sb("xtb", [128, 128])
        wgs = sb("wgs3", [128, 8, 16])
        U128b = sb("U128b", [128, 128])
        U16sb = sb("U16sb", [16, 16])
        oh_tok = sb("oh_tok", [128, 16])
        oh_all = sb("oh_all", [128, 512])
        nmx = sb("nmx", [128, 1], F32)
        mx1 = sb("mx13", [128, 1], F32)
        el = sb("el3", [128, 16], F32)
        den1 = sb("den13", [128, 1], F32)
        gsmat = sb("gsmat", [128, 32], F32)
        rkmat = sb("rkmat", [128, 32], F32)
        cumT = sb("cumT", [16, 4096], F32)
        ohst = sb("ohst", [16, 128])
        ct16 = sb("ct16", [128, 16], F32)
        rk1 = sb("rk1", [128, 1], F32)
        tot = sb("tot", [16, 32], F32)
        totf = sb("totf", [32, 16], F32)
        U32f = sb("U32f", [32, 32], F32)
        baseC = sb("baseC", [16, 32], F32)
        hist_s = sb("hist_s", [16, 1], F32)
        pada = sb("pada", [16, 1], F32)
        io33 = sb("io33", [16, 33], F32)
        cmp33 = sb("cmp33", [16, 33], F32)
        qcol = sb("qcol", [16, 1], F32)
        padc = sb("padc", [16, 1], F32)
        padb = sb("padb", [16, 1])
        segb = sb("segb", [1, 16], F32)
        segc = sb("segc", [16, 1], F32)
        cbmat = sb("cbmat", [16, 32], F32)
        cbrow = sb("cbrow", [1, 16], F32)
        gath = sb("gath", [128, 16], F32)
        slotc = sb("slotc", [128, 1], F32)
        slots_f = sb("slots_f", [128, 32], F32)
        slots_i = sb("slots_i", [128, 32], I32)
        io48 = sb("io48", [16, 48], F32)
        cmp48 = sb("cmp48", [16, 48])
        t2er = sb("t2er", [1, 48], F32)
        t2ei = sb("t2ei", [1, 48], I32)
        gtmp = sb("gtmp", [32, 128], F32)
        dum = sb("dum3", [1, 1], F32)

        psA = pt("psA3", [128, 512])
        psB = pt("psB3", [128, 512])
        psC = pt("psC3", [128, 512])
        psE3 = pt("psE3", [128, 384])
        psY3 = pt("psY3", [64, 128])
        psD3 = pt("psD3", [1, 384])
        psS = pt("psS3", [1, 512])
        psQ = pt("psQ3", [1, 512])
        psAb = psA.bitcast(BF16)
        psYb = psY3.bitcast(BF16)

        for dst, srcd in ((bands, bandsd), (bq_s, bq1), (bk_s, bk1), (bv_s, bv1),
                          (bproj_s, bprojd), (blat_s, blatd),
                          (ewh_s, ewhd), (ewt_s, ewtd), (U128b, U128d),
                          (U16sb, U16sd)):
            p.g(lambda e, k, ks, dst=dst, srcd=srcd: e.dma_start(out=dst[:, :], in_=srcd))
        p.g(lambda e, k, ks: e.dma_start(out=wlat_s[0:64, :], in_=wlatT))
        p.g(lambda e, k, ks: e.dma_start(out=wlat_s[64:128, :], in_=wlatT))
        p.g(lambda e, k, ks: e.dma_start(out=blat_r[:, :], in_=blatd.rearrange("a b -> b a")))
        p.g(lambda e, k, ks: e.dma_start(
            out=wgs[:, :, :].rearrange("p c f -> p (c f)"), in_=wg1))
        for t in (ones128, ones1, ones8, ones16c, onesf1, ones1t):
            p.v(lambda e, k, ks, t=t: e.memset(t[:, :], 1.0))
        p.v(lambda e, k, ks: e.memset(ke[:, :], 0.0))
        p.v(lambda e, k, ks: e.memset(ve[:, :], 0.0))
        p.gc(lambda e, k, ks: e.iota(idt[:, :], pattern=[[-1, 128]], base=0,
                                     channel_multiplier=1,
                                     allow_small_or_imprecise_dtypes=True))
        p.v(lambda e, k, ks: e.tensor_scalar(idt[:, :], idt[:, :], 0.0, None, ALU.is_equal))
        p.v(lambda e, k, ks: e.tensor_copy(idb[:, :], idt[:, :]))
        p.gc(lambda e, k, ks: e.iota(io48[:, :], pattern=[[1, 48]], base=0,
                                     channel_multiplier=0,
                                     allow_small_or_imprecise_dtypes=True))

        for b in range(B_CORE):
            for dc in range(8):
                p.g(lambda e, k, ks, b=b, dc=dc: e.dma_start(
                    out=xb[:, dc, :], in_=x2d[b, ds(dc * 128, 128), :]))

            with p.Loop(8):
                for wsl, wdr in ((wqs, wq1), (wks, wk1), (wvs, wv1)):
                    p.g(lambda e, k, ks, wsl=wsl, wdr=wdr: e.dma_start(
                        out=wsl[:, :, :].rearrange("p c f -> p (c f)"),
                        in_=wdr[ts(k, 128), :]))
                for tc in range(4):
                    for (wsl, dst, ps, bcol) in ((wqs, qp, psA, bq_s), (wks, kp, psB, bk_s),
                                                 (wvs, vp, psC, bv_s)):
                        for dc in range(8):
                            p.t(lambda e, k, ks, wsl=wsl, ps=ps, dc=dc, tc=tc: e.matmul(
                                ps[:, :], wsl[:, dc, :], xb[:, dc, ds(tc * 512, 512)],
                                start=(dc == 0), stop=(dc == 7)))
                        p.v(lambda e, k, ks, dst=dst, ps=ps, tc=tc, bcol=bcol: e.tensor_scalar(
                            dst[:, ds(tc * 512, 512)], ps[:, :], bcol[:, ds(CE(e, k, 1), 1)],
                            None, ALU.add))

                for h in range(2):
                    hr = h * 64
                    for tc in range(4):
                        p.t(lambda e, k, ks, hr=hr, tc=tc: e.matmul(
                            psA[0:64, :], wlat_s[hr:hr + 64, :],
                            kp[hr:hr + 64, ds(tc * 512, 512)], start=True, stop=True))
                        p.v(lambda e, k, ks, hr=hr, tc=tc: e.tensor_scalar(
                            klat[hr:hr + 64, ds(tc * 512, 512)], psA[0:64, :],
                            blat_s[:, 0:1], None, ALU.add))
                    for tt in range(16):
                        p.t(lambda e, k, ks, hr=hr, tt=tt: e.matmul(
                            psC[:, 0:64], vp[hr:hr + 64, ds(tt * 128, 128)],
                            wlat_s[hr:hr + 64, :], start=True, stop=False))
                        p.t(lambda e, k, ks: e.matmul(
                            psC[:, 0:64], ones1t[:, :], blat_r[:, :],
                            start=False, stop=True))
                        p.v(lambda e, k, ks, tt=tt: e.tensor_copy(
                            vlt[:, ds(tt * 64, 64)], psC[:, 0:64]))

                    for qt in range(16):
                        kts = [qt - 1, qt, qt + 1]
                        variant = 0
                        if qt == 0:
                            kts[0] = 15
                            variant = 1
                        if qt == 15:
                            kts[2] = 0
                            variant = 2
                        for pos in range(3):
                            p.t(lambda e, k, ks, hr=hr, qt=qt, kt=kts[pos], pos=pos: e.matmul(
                                psE3[:, ds(pos * 128, 128)],
                                klat[hr:hr + 64, ds(kt * 128, 128)],
                                qp[hr:hr + 64, ds(qt * 128, 128)], start=True, stop=True))
                        p.s(lambda e, k, ks: e.activation(
                            mexp[:, :], psE3[:, :], AF.Exp, scale=0.125))
                        p.v(lambda e, k, ks, variant=variant: e.tensor_tensor(
                            mexp[:, :], mexp[:, :], bands[:, ds(variant * 384, 384)],
                            ALU.mult))
                        p.t(lambda e, k, ks: e.matmul(
                            psD3[:, :], ones128[:, :], mexp[:, :], start=True, stop=True))
                        p.v(lambda e, k, ks: e.tensor_copy(denr[:, :], psD3[:, :]))
                        p.v(lambda e, k, ks: e.tensor_tensor(
                            rden[:, :], denr[0:1, 0:128], denr[0:1, 128:256], ALU.add))
                        p.v(lambda e, k, ks: e.tensor_tensor(
                            rden[:, :], rden[:, :], denr[0:1, 256:384], ALU.add))
                        p.v(lambda e, k, ks: e.reciprocal(rden[:, :], rden[:, :]))
                        p.v(lambda e, k, ks: e.tensor_copy(rdenb[:, :], rden[:, :]))
                        for pos in range(3):
                            p.t(lambda e, k, ks, kt=kts[pos], pos=pos: e.matmul(
                                psY3[:, :], vlt[:, ds(kt * 64, 64)],
                                mexp[:, ds(pos * 128, 128)],
                                start=(pos == 0), stop=(pos == 2)))
                        p.t(lambda e, k, ks: e.matmul(
                            psB[0:64, 0:128], ones1[:, :], rdenb[:, :], start=True, stop=True))
                        p.v(lambda e, k, ks: e.tensor_copy(bcy[:, :], psB[0:64, 0:128]))
                        p.v(lambda e, k, ks, hr=hr, qt=qt: e.tensor_tensor(
                            y_pair[hr:hr + 64, ds(qt * 128, 128)], psY3[:, :],
                            bcy[:, :], ALU.mult))

                    for which in range(2):
                        if which == 0:
                            qoff, ew = 0, ewh_s
                        else:
                            qoff, ew = 2045, ewt_s
                        if which == 0:
                            p.v(lambda e, k, ks, hr=hr: e.tensor_copy(ke[:, 0:6], klat[hr:hr + 64, 0:6]))
                            p.v(lambda e, k, ks, hr=hr: e.tensor_copy(ke[:, 6:7], klat[hr:hr + 64, 2047:2048]))
                            p.g(lambda e, k, ks: e.dma_start(out=ve[0:6, :], in_=vlt[0:6, 0:64]))
                            p.g(lambda e, k, ks: e.dma_start(out=ve[6:7, :], in_=vlt[127:128, ds(15 * 64, 64)]))
                        else:
                            p.v(lambda e, k, ks, hr=hr: e.tensor_copy(ke[:, 0:6], klat[hr:hr + 64, 2042:2048]))
                            p.v(lambda e, k, ks, hr=hr: e.tensor_copy(ke[:, 6:7], klat[hr:hr + 64, 0:1]))
                            p.g(lambda e, k, ks: e.dma_start(out=ve[0:6, :], in_=vlt[106:112, ds(15 * 64, 64)]))
                            p.g(lambda e, k, ks: e.dma_start(out=ve[6:7, :], in_=vlt[0:1, 0:64]))
                        p.v(lambda e, k, ks, hr=hr, qoff=qoff: e.tensor_copy(
                            qe3[:, :], qp[hr:hr + 64, ds(qoff, 3)]))
                        p.t(lambda e, k, ks: e.matmul(
                            psE3[0:8, 0:3], ke[:, :], qe3[:, :],
                            start=True, stop=True))
                        p.s(lambda e, k, ks: e.activation(
                            ee[:, :], psE3[0:8, 0:3], AF.Exp, scale=0.125))
                        p.v(lambda e, k, ks, ew=ew: e.tensor_tensor(ee[:, :], ee[:, :], ew[:, :], ALU.mult))
                        p.v(lambda e, k, ks: e.tensor_copy(web[:, :], ee[:, :]))
                        p.t(lambda e, k, ks: e.matmul(
                            psD3[0:1, 0:3], ones8[:, :], web[:, :], start=True, stop=True))
                        p.v(lambda e, k, ks: e.reciprocal(rde[:, :], psD3[0:1, 0:3]))
                        p.v(lambda e, k, ks: e.tensor_copy(rdeb[:, :], rde[:, :]))
                        p.t(lambda e, k, ks: e.matmul(
                            psY3[0:64, 0:3], ve[:, :], web[:, :], start=True, stop=True))
                        p.t(lambda e, k, ks: e.matmul(
                            psB[0:64, 0:3], ones1[:, :], rdeb[:, :], start=True, stop=True))
                        p.v(lambda e, k, ks: e.tensor_copy(bce[:, :], psB[0:64, 0:3]))
                        p.v(lambda e, k, ks, hr=hr, qoff=qoff: e.tensor_tensor(
                            y_pair[hr:hr + 64, ds(qoff, 3)], psY3[0:64, 0:3],
                            bce[:, :], ALU.mult))

                p.sy(lambda e, k, ks: e.dma_start(
                    out=y_dram[ts(k, 128), :], in_=y_pair[:, :]))

            p.g(lambda e, k, ks: e.dma_start(
                out=wbuf[:, :, :].rearrange("p c f -> p (c f)"), in_=wproj))
            for tc in range(4):
                for dc in range(8):
                    p.g(lambda e, k, ks, dc=dc, tc=tc: e.dma_start(
                        out=ybufs[:, dc, :],
                        in_=y_dram[ds(dc * 128, 128), ds(tc * 512, 512)]))
                for oc in range(8):
                    for dc in range(8):
                        p.t(lambda e, k, ks, dc=dc, oc=oc, tc=tc: e.matmul(
                            psA[:, :], wbuf[:, dc, ds(oc * 128, 128)],
                            ybufs[:, dc, :], start=(dc == 0), stop=(dc == 7)))
                    p.g(lambda e, k, ks, b=b, oc=oc, tc=tc: e.dma_start(
                        out=xres[:, :], in_=x2d[b, ds(oc * 128, 128), ds(tc * 512, 512)]))
                    p.v(lambda e, k, ks, oc=oc: e.tensor_scalar(
                        sprev[:, :], psA[:, :], bproj_s[:, oc:oc + 1], None, ALU.add))
                    p.v(lambda e, k, ks: e.tensor_tensor(
                        sprev[:, :], sprev[:, :], xres[:, :], ALU.add))
                    p.v(lambda e, k, ks, oc=oc: e.tensor_copy(sum_bf[:, oc, :], sprev[:, :]))
                    p.s(lambda e, k, ks: e.activation(sq_bf[:, :], sprev[:, :], AF.Square))
                    p.t(lambda e, k, ks, oc=oc: e.matmul(
                        psS[:, :], ones128[:, :], sum_bf[:, oc, :],
                        start=(oc == 0), stop=(oc == 7)))
                    p.t(lambda e, k, ks, oc=oc: e.matmul(
                        psQ[:, :], ones128[:, :], sq_bf[:, :],
                        start=(oc == 0), stop=(oc == 7)))
                p.v(lambda e, k, ks: e.tensor_scalar(mrow[:, :], psS[:, :], 1.0 / D, None, ALU.mult))
                p.v(lambda e, k, ks: e.tensor_scalar(m2row[:, :], psQ[:, :], 1.0 / D, None, ALU.mult))
                p.v(lambda e, k, ks: e.tensor_tensor(varr[:, :], mrow[:, :], mrow[:, :], ALU.mult))
                p.v(lambda e, k, ks: e.tensor_tensor(varr[:, :], m2row[:, :], varr[:, :], ALU.subtract))
                p.v(lambda e, k, ks: e.tensor_scalar(varr[:, :], varr[:, :], 1e-5, None, ALU.add))
                p.s(lambda e, k, ks: e.activation(varr[:, :], varr[:, :], AF.Sqrt))
                p.v(lambda e, k, ks: e.reciprocal(varr[:, :], varr[:, :]))
                p.v(lambda e, k, ks: e.tensor_copy(mrow_bf[:, :], mrow[:, :]))
                p.v(lambda e, k, ks: e.tensor_copy(rstd_bf[:, :], varr[:, :]))
                p.t(lambda e, k, ks: e.matmul(psB[0:64, :], ones1[:, :], mrow_bf[:, :], start=True, stop=True))
                p.t(lambda e, k, ks: e.matmul(psC[0:64, :], ones1[:, :], rstd_bf[:, :], start=True, stop=True))
                p.v(lambda e, k, ks: e.tensor_copy(mb_sb[0:64, :], psB[0:64, :]))
                p.v(lambda e, k, ks: e.tensor_copy(rs_sb[0:64, :], psC[0:64, :]))
                p.v(lambda e, k, ks: e.tensor_copy(mb_sb[64:128, :], psB[0:64, :]))
                p.v(lambda e, k, ks: e.tensor_copy(rs_sb[64:128, :], psC[0:64, :]))
                for oc in range(8):
                    p.v(lambda e, k, ks, oc=oc: e.tensor_tensor(
                        outf[:, :], sum_bf[:, oc, :], mb_sb[:, :], ALU.subtract))
                    p.v(lambda e, k, ks: e.tensor_tensor(outf[:, :], outf[:, :], rs_sb[:, :], ALU.mult))
                    p.v(lambda e, k, ks, oc=oc, b=b, tc=tc: e.tensor_copy(
                        x3sb[:, oc, ds(b * T + tc * 512, 512)], outf[:, :]))
            for oc in range(8):
                for tt in range(16):
                    p.t(lambda e, k, ks, oc=oc, tt=tt, b=b: e.transpose(
                        psAb[:, 0:128], x3sb[:, oc, ds(b * T + tt * 128, 128)], idb[:, :]))
                    p.v(lambda e, k, ks: e.tensor_copy(xtb[:, :], psAb[:, 0:128]))
                    p.sy(lambda e, k, ks, b=b, oc=oc, tt=tt: e.dma_start(
                        out=x3bf[ds(b * T + tt * 128, 128), ds(oc * 128, 128)],
                        in_=xtb[:, :]))

        for tile in range(32):
            for dc in range(8):
                p.t(lambda e, k, ks, dc=dc, tile=tile: e.matmul(
                    psB[:, 0:16], x3sb[:, dc, ds(tile * 128, 128)], wgs[:, dc, :],
                    start=(dc == 0), stop=(dc == 7)))
            p.v(lambda e, k, ks: e.tensor_reduce(mx1[:, :], psB[:, 0:16], AX.X, ALU.max))
            p.v(lambda e, k, ks: e.tensor_scalar(nmx[:, :], mx1[:, :], -1.0, None, ALU.mult))
            p.s(lambda e, k, ks: e.activation(el[:, :], psB[:, 0:16], AF.Exp, bias=nmx[:, 0:1]))
            p.v(lambda e, k, ks: e.tensor_reduce(den1[:, :], el[:, :], AX.X, ALU.add))
            p.v(lambda e, k, ks: e.reciprocal(den1[:, :], den1[:, :]))
            p.v(lambda e, k, ks, tile=tile: e.tensor_copy(gsmat[:, tile:tile + 1], den1[:, :]))
            p.v(lambda e, k, ks: e.tensor_scalar(oh_tok[:, :], psB[:, 0:16], mx1[:, 0:1], None, ALU.is_ge))
            p.v(lambda e, k, ks, tile=tile: e.tensor_copy(oh_all[:, ds(tile * 16, 16)], oh_tok[:, :]))
            p.t(lambda e, k, ks: e.transpose(psYb[0:16, 0:128], oh_tok[:, :], idb[:, :]))
            p.v(lambda e, k, ks: e.tensor_copy(ohst[:, :], psYb[0:16, 0:128]))
            p.sy(lambda e, k, ks, tile=tile: e.dma_start(
                out=ohT_o[:, ds(tile * 128, 128)], in_=ohst[:, :]))
            p.t(lambda e, k, ks: e.matmul(
                psC[0:16, 0:128], oh_tok[:, :], U128b[:, :], start=True, stop=True))
            p.v(lambda e, k, ks, tile=tile: e.tensor_copy(cumT[:, ds(tile * 128, 128)], psC[0:16, 0:128]))
            p.t(lambda e, k, ks: e.matmul(
                psB[:, 0:16], U128b[:, :], oh_tok[:, :], start=True, stop=True))
            p.v(lambda e, k, ks: e.tensor_tensor(ct16[:, :], psB[:, 0:16], oh_tok[:, :], ALU.mult))
            p.v(lambda e, k, ks, tile=tile: e.tensor_reduce(
                rkmat[:, ds(tile, 1)], ct16[:, :], AX.X, ALU.add))
        p.v(lambda e, k, ks: e.tensor_copy(
            tot[:, :], cumT.tensor_ap()[0:16, 127:4096:128] if False else cumT[:, 127::128]))
        p.g(lambda e, k, ks: e.dma_start(out=U32f[:, :], in_=U32sd))
        p.t(lambda e, k, ks: e.transpose(psB[0:32, 0:16], tot[:, :], idt[0:16, 0:16]))
        p.v(lambda e, k, ks: e.tensor_copy(totf[:, :], psB[0:32, 0:16]))
        p.t(lambda e, k, ks: e.matmul(
            psC[0:16, 0:32], totf[:, :], U32f[:, :], start=True, stop=True))
        p.v(lambda e, k, ks: e.tensor_copy(baseC[:, :], psC[0:16, 0:32]))
        p.v(lambda e, k, ks: e.tensor_tensor(hist_s[:, :], baseC[:, 31:32], tot[:, 31:32], ALU.add))
        p.sy(lambda e, k, ks: e.dma_start(out=hist_o, in_=hist_s[:, :]))
        p.v(lambda e, k, ks: e.tensor_scalar(pada[:, :], hist_s[:, :], 127.0, None, ALU.add))
        p.gc(lambda e, k, ks: e.iota(io33[:, :], pattern=[[1, 33]], base=1,
                                     channel_multiplier=0,
                                     allow_small_or_imprecise_dtypes=True))
        p.v(lambda e, k, ks: e.tensor_scalar(qcol[:, :], pada[:, :], 1.0 / 128.0, None, ALU.mult))
        p.v(lambda e, k, ks: e.tensor_scalar(cmp33[:, :], io33[:, :], qcol[:, 0:1], None, ALU.is_le))
        p.v(lambda e, k, ks: e.tensor_reduce(padc[:, :], cmp33[:, :], AX.X, ALU.add))
        p.v(lambda e, k, ks: e.tensor_scalar(padc[:, :], padc[:, :], 128.0, None, ALU.mult))
        p.v(lambda e, k, ks: e.tensor_copy(padb[:, :], padc[:, :]))
        p.t(lambda e, k, ks: e.matmul(
            psD3[0:1, 0:16], padb[:, :], U16sb[:, :], start=True, stop=True))
        p.v(lambda e, k, ks: e.tensor_copy(segb[:, :], psD3[0:1, 0:16]))
        p.t(lambda e, k, ks: e.transpose(psC[0:16, 0:1], segb[:, :], idt[0:1, 0:1]))
        p.v(lambda e, k, ks: e.tensor_copy(segc[:, :], psC[0:16, 0:1]))
        p.v(lambda e, k, ks: e.tensor_scalar(cbmat[:, :], baseC[:, :], segc[:, 0:1], None, ALU.add))
        p.t(lambda e, k, ks: e.transpose(psB[0:32, 0:16], cbmat[:, :], idt[0:16, 0:16]))
        p.v(lambda e, k, ks: e.tensor_copy(gtmp[:, 0:16], psB[0:32, 0:16]))
        p.sy(lambda e, k, ks: e.dma_start(out=cb_dram, in_=gtmp[:, 0:16]))
        with p.Loop(32):
            p.v(lambda e, k, ks: e.tensor_scalar(
                cumT[:, ts(k, 128)], cumT[:, ts(k, 128)], baseC[:, ds(k, 1)],
                None, ALU.add))
        p.sy(lambda e, k, ks: e.dma_start(out=locT_o, in_=cumT[:, :]))
        p.t(lambda e, k, ks: e.transpose(psC[0:32, 0:128], gsmat[:, :], idt[:, :]))
        p.v(lambda e, k, ks: e.tensor_copy(gtmp[:, :], psC[0:32, 0:128]))
        p.sy(lambda e, k, ks: e.dma_start(out=gates_o, in_=gtmp[:, :]))
        with p.Loop(32):
            p.g(lambda e, k, ks: e.dma_start(out=cbrow[:, :], in_=cb_dram[ts(k, 1), :]))
            p.t(lambda e, k, ks: e.matmul(
                psB[:, 0:16], onesf1[:, :], cbrow[:, :], start=True, stop=True))
            p.v(lambda e, k, ks: e.tensor_tensor(
                gath[:, :], psB[:, 0:16], oh_all[:, ts(k, 16)], ALU.mult))
            p.v(lambda e, k, ks: e.tensor_reduce(slotc[:, :], gath[:, :], AX.X, ALU.add))
            p.v(lambda e, k, ks: e.tensor_tensor(slotc[:, :], slotc[:, :], rkmat[:, ds(k, 1)], ALU.add))
            p.v(lambda e, k, ks: e.tensor_scalar(
                slots_f[:, ds(k, 1)], slotc[:, :], -1.0, None, ALU.add))
        p.v(lambda e, k, ks: e.tensor_copy(slots_i[:, :], slots_f[:, :]))
        p.sy(lambda e, k, ks: e.dma_start(out=slots_o, in_=slots_i[:, :]))
        p.v(lambda e, k, ks: e.tensor_scalar(
            cmp48[:, :], io48[:, :], 128.0, segc[:, 0:1], ALU.mult, op1=ALU.is_ge))
        p.t(lambda e, k, ks: e.matmul(
            psD3[0:1, 0:48], ones16c[:, :], cmp48[:, :], start=True, stop=True))
        p.v(lambda e, k, ks: e.tensor_scalar(t2er[:, :], psD3[0:1, 0:48], -1.0, 128.0, ALU.add, op1=ALU.mult))
        p.v(lambda e, k, ks: e.tensor_copy(t2ei[:, :], t2er[:, :]))
        p.sy(lambda e, k, ks: e.dma_start(out=t2e_o, in_=t2ei[:, :]))

        p.gc(lambda e, k, ks: e.memset(dum[:, :], 0.0))


def build_l1_attn():
    p = Prog()
    nc = p.nc
    x2d = nc.dram_tensor("x2", [B_CORE, D, T], F32, kind="ExternalInput").ap()
    wq1 = nc.dram_tensor("wq1", [D, D], F32, kind="ExternalInput").ap()
    wk1 = nc.dram_tensor("wk1", [D, D], F32, kind="ExternalInput").ap()
    wv1 = nc.dram_tensor("wv1", [D, D], F32, kind="ExternalInput").ap()
    bq1 = nc.dram_tensor("bq1", [128, 8], F32, kind="ExternalInput").ap()
    bk1 = nc.dram_tensor("bk1", [128, 8], F32, kind="ExternalInput").ap()
    bv1 = nc.dram_tensor("bv1", [128, 8], F32, kind="ExternalInput").ap()
    wlatT = nc.dram_tensor("wlatT", [64, 64], F32, kind="ExternalInput").ap()
    blatd = nc.dram_tensor("blat", [64, 1], F32, kind="ExternalInput").ap()
    wproj = nc.dram_tensor("wproj", [128, 8 * D], F32, kind="ExternalInput").ap()
    bprojd = nc.dram_tensor("bproj", [128, 8], F32, kind="ExternalInput").ap()
    wg1 = nc.dram_tensor("wg1", [128, 128], F32, kind="ExternalInput").ap()
    bandsd = nc.dram_tensor("bands", [128, 1152], F32, kind="ExternalInput").ap()
    ewhd = nc.dram_tensor("ewh", [8, 3], F32, kind="ExternalInput").ap()
    ewtd = nc.dram_tensor("ewt", [8, 3], F32, kind="ExternalInput").ap()
    U128d = nc.dram_tensor("U128", [128, 128], F32, kind="ExternalInput").ap()
    U16sd = nc.dram_tensor("U16s", [16, 16], F32, kind="ExternalInput").ap()
    U32sd = nc.dram_tensor("U32s", [32, 32], F32, kind="ExternalInput").ap()

    x3bf = nc.dram_tensor("x3bf", [4096, D], BF16, kind="ExternalOutput").ap()
    hist_o = nc.dram_tensor("hist", [16, 1], F32, kind="ExternalOutput").ap()
    gates_o = nc.dram_tensor("gates", [32, 128], F32, kind="ExternalOutput").ap()
    ohT_o = nc.dram_tensor("ohT", [16, 4096], BF16, kind="ExternalOutput").ap()
    locT_o = nc.dram_tensor("locT", [16, 4096], F32, kind="ExternalOutput").ap()
    slots_o = nc.dram_tensor("slots", [128, 32], I32, kind="ExternalOutput").ap()
    t2e_o = nc.dram_tensor("t2e", [1, 48], I32, kind="ExternalOutput").ap()
    y_dram = nc.dram_tensor("y_dram3", [D, T], F32).ap()
    cb_dram = nc.dram_tensor("cb_dram", [32, 16], F32).ap()

    dr = {"x2": x2d,
          "wq1": wq1,
          "wk1": wk1,
          "wv1": wv1,
          "bq1": bq1,
          "bk1": bk1,
          "bv1": bv1,
          "wlatT": wlatT,
          "blat": blatd,
          "wproj": wproj,
          "bproj": bprojd,
          "wg1": wg1,
          "bands": bandsd,
          "ewh": ewhd,
          "ewt": ewtd,
          "U128": U128d,
          "U16s": U16sd,
          "U32s": U32sd,
          "x3bf": x3bf,
          "hist": hist_o,
          "gates": gates_o,
          "ohT": ohT_o,
          "locT": locT_o,
          "slots": slots_o,
          "t2e": t2e_o}
    emit_l1_attn(p, dr)
    with ExitStack() as _ctx2:
        p.emit(_ctx2)
    return p.nc



def l1_ffn_host_inputs(inputs):
    w1 = inputs["L1_ffn_w1"]
    w2 = inputs["L1_ffn_w2"]
    w1_cr = w1.reshape(16, 8, 128, 4096).transpose(0, 2, 1, 3).reshape(16 * 128, 8 * 4096)
    w2_cr = w2.reshape(16, 32, 128, 1024).transpose(0, 2, 1, 3).reshape(16 * 128, 32 * 1024)
    return {"w1_cr": np.ascontiguousarray(w1_cr.astype(np.float32)),
            "w2_cr": np.ascontiguousarray(w2_cr.astype(np.float32))}


def emit_l1_ffn(p, dr):
    from concourse.bass_core_rust import IndirectOffsetOnAxis
    nc = p.nc
    x3d = dr["x3bf"]
    slotsd = dr["slots"]
    t2ed = dr["t2e"]
    w1d = dr["w1_cr"]
    w2d = dr["w2_cr"]
    yslots = dr["yslots"]
    xg = nc.dram_tensor("xg_dram", [6144, D], BF16).ap()

    with ExitStack() as ctx:
        def sb(name, shape, dt=BF16):
            return ctx.enter_context(nc.sbuf_tensor(name, shape, dt))

        w1s = sb("w1s4", [128, 8, 4096])
        w2s = sb("w2s4", [128, 32, 1024])
        hbuf = sb("hbuf4", [128, 32, 128])
        xstg = sb("xstg", [128, D])
        xgt = sb("xgt", [128, D])
        xf = sb("xf4", [128, 8, 128])
        yfm = sb("yfm", [128, 512])
        ytok = sb("ytok", [128, 8, 128])
        zt = sb("zt4", [128, D])
        idb4 = sb("idb4", [128, 128])
        idf4 = sb("idf4", [128, 128], F32)
        slots_s = ctx.enter_context(nc.sbuf_tensor("slots_s", [128, 32], I32))
        t2es = ctx.enter_context(nc.sbuf_tensor("t2es", [1, 48], I32))
        ecell = ctx.enter_context(nc.sbuf_tensor("ecell", [1, 1], I32))
        dum = sb("dum4", [1, 1], F32)

        psA = ctx.enter_context(nc.psum_tensor("psA4", [128, 512], F32))
        psB = ctx.enter_context(nc.psum_tensor("psB4", [128, 512], F32))
        psAb = psA.bitcast(BF16)

        p.g(lambda e, k, ks: e.dma_start(out=slots_s[:, :], in_=slotsd))
        p.g(lambda e, k, ks: e.dma_start(out=t2es[:, :], in_=t2ed))
        p.v(lambda e, k, ks: e.memset(zt[:, :], 0.0))
        p.gc(lambda e, k, ks: e.iota(idf4[:, :], pattern=[[-1, 128]], base=0,
                                     channel_multiplier=1,
                                     allow_small_or_imprecise_dtypes=True))
        p.v(lambda e, k, ks: e.tensor_scalar(idf4[:, :], idf4[:, :], 0.0, None, ALU.is_equal))
        p.v(lambda e, k, ks: e.tensor_copy(idb4[:, :], idf4[:, :]))
        for j in range(48):
            p.sy(lambda e, k, ks, j=j: e.dma_start(out=xg[ds(j * 128, 128), :], in_=zt[:, :]))
        for tile in range(32):
            p.g(lambda e, k, ks, tile=tile: e.dma_start(
                out=xstg[:, :], in_=x3d[ds(tile * 128, 128), :]))
            p.g(lambda e, k, ks, tile=tile: e.indirect_dma_start(
                xg, IndirectOffsetOnAxis(slots_s[:, tile:tile + 1], 0),
                xstg[:, :], None))
        _esv = [None]

        def _load_w1(e, k, ks):
            _esv[0] = e.value_load(ecell[0:1, 0:1])
            return e.dma_start(
                out=w1s[:, :, :].rearrange("p c f -> p (c f)"),
                in_=w1d[ds(_esv[0], 128), :])

        with p.Loop(48):
            p.v(lambda e, k, ks: e.tensor_copy(ecell[:, :], t2es[0:1, ds(k, 1)]))
            p.g(_load_w1)
            p.g(lambda e, k, ks: e.dma_start(
                out=w2s[:, :, :].rearrange("p c f -> p (c f)"), in_=w2d[ds(_esv[0], 128), :]))
            p.g(lambda e, k, ks: e.dma_start(out=xgt[:, :], in_=xg[ts(k, 128), :]))
            for dc4 in range(2):
                for j in range(4):
                    p.t(lambda e, k, ks, dc4=dc4, j=j: e.transpose(
                        psAb[:, ds(j * 128, 128)], xgt[:, ds((dc4 * 4 + j) * 128, 128)],
                        idb4[:, :]))
                p.v(lambda e, k, ks, dc4=dc4: e.tensor_copy(
                    xf[:, ds(dc4 * 4, 4), :].rearrange("p c f -> p (c f)"),
                    psAb[:, 0:512]))
            for fq in range(8):
                for r in range(4):
                    for dc in range(8):
                        p.t(lambda e, k, ks, dc=dc, fq=fq, r=r: e.matmul(
                            psB[:, ds(r * 128, 128)],
                            w1s[:, dc, ds((fq * 4 + r) * 128, 128)], xf[:, dc, :],
                            start=(dc == 0), stop=(dc == 7)))
                p.s(lambda e, k, ks, fq=fq: e.activation(
                    hbuf[:, ds(fq * 4, 4), :].rearrange("p c f -> p (c f)"),
                    psB[:, :], AF.Relu))
            for og in range(2):
                for r in range(4):
                    for cf in range(32):
                        p.t(lambda e, k, ks, cf=cf, og=og, r=r: e.matmul(
                            psB[:, ds(r * 128, 128)],
                            w2s[:, cf, ds((og * 4 + r) * 128, 128)], hbuf[:, cf, :],
                            start=(cf == 0), stop=(cf == 31)))
                p.v(lambda e, k, ks: e.tensor_copy(yfm[:, :], psB[:, :]))
                for j in range(4):
                    p.t(lambda e, k, ks, j=j: e.transpose(
                        psAb[:, ds(j * 128, 128)], yfm[:, ds(j * 128, 128)], idb4[:, :]))
                p.v(lambda e, k, ks, og=og: e.tensor_copy(
                    ytok[:, ds(og * 4, 4), :].rearrange("p c f -> p (c f)"),
                    psAb[:, 0:512]))
            p.sy(lambda e, k, ks: e.dma_start(
                out=yslots[ts(k, 128), :],
                in_=ytok[:, :, :].rearrange("p c f -> p (c f)")))

        p.gc(lambda e, k, ks: e.memset(dum[:, :], 0.0))


def build_l1_ffn():
    from concourse.bass_core_rust import IndirectOffsetOnAxis
    p = Prog()
    nc = p.nc
    x3d = nc.dram_tensor("x3bf", [4096, D], BF16, kind="ExternalInput").ap()
    slotsd = nc.dram_tensor("slots", [128, 32], I32, kind="ExternalInput").ap()
    t2ed = nc.dram_tensor("t2e", [1, 48], I32, kind="ExternalInput").ap()
    w1d = nc.dram_tensor("w1_cr", [16 * 128, 8 * 4096], F32, kind="ExternalInput").ap()
    w2d = nc.dram_tensor("w2_cr", [16 * 128, 32 * 1024], F32, kind="ExternalInput").ap()
    yslots = nc.dram_tensor("yslots", [6144, D], BF16, kind="ExternalOutput").ap()
    xg = nc.dram_tensor("xg_dram", [6144, D], BF16).ap()

    dr = {"x3bf": x3d,
          "slots": slotsd,
          "t2e": t2ed,
          "w1_cr": w1d,
          "w2_cr": w2d,
          "yslots": yslots}
    emit_l1_ffn(p, dr)
    with ExitStack() as _ctx2:
        p.emit(_ctx2)
    return p.nc



def build_finale():
    from concourse.bass_core_rust import IndirectOffsetOnAxis
    p = Prog()
    nc = p.nc
    x3d = nc.dram_tensor("x3bf", [4096, D], BF16, kind="ExternalInput").ap()
    ysd = nc.dram_tensor("yslots", [6144, D], BF16, kind="ExternalInput").ap()
    slotsd = nc.dram_tensor("slots", [128, 32], I32, kind="ExternalInput").ap()
    gatesd = nc.dram_tensor("gates", [32, 128], F32, kind="ExternalInput").ap()
    ohTd = nc.dram_tensor("ohT", [16, 4096], BF16, kind="ExternalInput").ap()
    locTd = nc.dram_tensor("locT", [16, 4096], F32, kind="ExternalInput").ap()
    histsd = nc.dram_tensor("hists", [8, 16], F32, kind="ExternalInput").ap()
    outd = nc.dram_tensor("out", [4096, D], F32, kind="ExternalOutput").ap()
    pid_dram = nc.partition_id_tensor.ap()

    with ExitStack() as ctx:
        def sb(name, shape, dt=F32):
            return ctx.enter_context(nc.sbuf_tensor(name, shape, dt))

        hsb = sb("hsb", [8, 16])
        pidu = ctx.enter_context(nc.sbuf_tensor("pidu5", [1, 1], U32))
        pidf = sb("pidf5", [1, 1])
        onesf8 = sb("onesf8", [1, 8])
        onesf1 = sb("onesf15", [1, 128])
        ones16 = sb("ones16_5", [16, 1], BF16)
        iota8 = sb("iota85", [8, 1])
        pid8 = sb("pid85", [8, 1])
        cmp8 = sb("cmp85", [8, 1])
        bprev = sb("bprev", [16, 1])
        locs = sb("locs5", [16, 4096])
        ohs = ctx.enter_context(nc.sbuf_tensor("ohs5", [16, 4096], BF16))
        valt = sb("valt", [16, 4096])
        prodb = ctx.enter_context(nc.sbuf_tensor("prodb", [16, 4096], BF16))
        vrow = sb("vrow", [1, 4096])
        grow = sb("grow", [1, 4096])
        wrow = sb("wrow", [1, 4096])
        wcols = sb("wcols", [128, 32])
        id1 = sb("id15", [1, 1])
        slots_s = ctx.enter_context(nc.sbuf_tensor("slots_s5", [128, 32], I32))
        ytile = ctx.enter_context(nc.sbuf_tensor("ytile", [128, D], BF16))
        xt5 = ctx.enter_context(nc.sbuf_tensor("xt5", [128, D], BF16))
        sum5 = sb("sum5", [128, D])
        sqs = sb("sqs", [128, D])
        srow = sb("srow", [128, 1])
        ssq = sb("ssq", [128, 1])
        mean5 = sb("mean5", [128, 1])
        nmean = sb("nmean5", [128, 1])
        var5 = sb("var5", [128, 1])
        outt = sb("outt", [128, D])
        dum = sb("dum5", [1, 1])

        psA = ctx.enter_context(nc.psum_tensor("psA5", [128, 512], F32))
        psS = ctx.enter_context(nc.psum_tensor("psS5", [1, 512], F32))

        p.g(lambda e, k, ks: e.dma_start(out=hsb[:, :], in_=histsd))
        p.g(lambda e, k, ks: e.dma_start(out=pidu[:, :], in_=pid_dram))
        p.g(lambda e, k, ks: e.dma_start(out=locs[:, :], in_=locTd))
        p.g(lambda e, k, ks: e.dma_start(out=ohs[:, :], in_=ohTd))
        p.g(lambda e, k, ks: e.dma_start(out=slots_s[:, :], in_=slotsd))
        p.g(lambda e, k, ks: e.dma_start(out=grow[:, :],
                                         in_=gatesd.rearrange("a b -> (a b)")))
        p.v(lambda e, k, ks: e.memset(onesf8[:, :], 1.0))
        p.v(lambda e, k, ks: e.memset(onesf1[:, :], 1.0))
        p.v(lambda e, k, ks: e.memset(ones16[:, :], 1.0))
        p.v(lambda e, k, ks: e.memset(id1[:, :], 1.0))
        p.gc(lambda e, k, ks: e.iota(iota8[:, :], pattern=[[0, 1]], base=0,
                                     channel_multiplier=1,
                                     allow_small_or_imprecise_dtypes=True))
        p.v(lambda e, k, ks: e.tensor_copy(pidf[:, :], pidu[:, :]))
        p.t(lambda e, k, ks: e.matmul(psA[0:8, 0:1], onesf8[:, :], pidf[:, :],
                                      start=True, stop=True))
        p.v(lambda e, k, ks: e.tensor_copy(pid8[:, :], psA[0:8, 0:1]))
        p.v(lambda e, k, ks: e.tensor_tensor(cmp8[:, :], iota8[:, :], pid8[:, :], ALU.is_lt))
        p.t(lambda e, k, ks: e.matmul(psA[0:16, 0:1], hsb[:, :], cmp8[:, :],
                                      start=True, stop=True))
        p.v(lambda e, k, ks: e.tensor_copy(bprev[:, :], psA[0:16, 0:1]))
        p.v(lambda e, k, ks: e.tensor_scalar(valt[:, :], locs[:, :], bprev[:, 0:1], None, ALU.add))
        p.v(lambda e, k, ks: e.tensor_scalar(valt[:, :], valt[:, :], 2048.0, None, ALU.is_le))
        p.v(lambda e, k, ks: e.tensor_tensor(prodb[:, :], ohs[:, :], valt[:, :], ALU.mult))
        with p.Loop(8):
            p.t(lambda e, k, ks: e.matmul(psS[:, :], ones16[:, :],
                                          prodb[:, ts(k, 512)], start=True, stop=True))
            p.v(lambda e, k, ks: e.tensor_copy(vrow[:, ts(k, 512)], psS[:, :]))
        p.v(lambda e, k, ks: e.tensor_tensor(wrow[:, :], vrow[:, :], grow[:, :], ALU.mult))
        for tile in range(32):
            p.t(lambda e, k, ks, tile=tile: e.transpose(
                psA[:, 0:1], wrow[0:1, ds(tile * 128, 128)], id1[:, :]))
            p.v(lambda e, k, ks, tile=tile: e.tensor_copy(wcols[:, tile:tile + 1], psA[:, 0:1]))
        for tile in range(32):
            p.g(lambda e, k, ks, tile=tile: e.indirect_dma_start(
                ytile[:, :], None, ysd, IndirectOffsetOnAxis(slots_s[:, tile:tile + 1], 0)))
            p.g(lambda e, k, ks, tile=tile: e.dma_start(
                out=xt5[:, :], in_=x3d[ds(tile * 128, 128), :]))
            p.v(lambda e, k, ks, tile=tile: e.tensor_scalar(
                sum5[:, :], ytile[:, :], wcols[:, tile:tile + 1], None, ALU.mult))
            p.v(lambda e, k, ks: e.tensor_tensor(sum5[:, :], sum5[:, :], xt5[:, :], ALU.add))
            p.v(lambda e, k, ks: e.tensor_reduce(srow[:, :], sum5[:, :], AX.X, ALU.add))
            p.v(lambda e, k, ks: e.tensor_scalar(mean5[:, :], srow[:, :], 1.0 / D, None, ALU.mult))
            p.s(lambda e, k, ks: e.activation(sqs[:, :], sum5[:, :], AF.Square, accum_out=ssq[:, :]))
            p.v(lambda e, k, ks: e.tensor_scalar(var5[:, :], ssq[:, :], 1.0 / D, None, ALU.mult))
            p.v(lambda e, k, ks: e.tensor_tensor(nmean[:, :], mean5[:, :], mean5[:, :], ALU.mult))
            p.v(lambda e, k, ks: e.tensor_tensor(var5[:, :], var5[:, :], nmean[:, :], ALU.subtract))
            p.v(lambda e, k, ks: e.tensor_scalar(var5[:, :], var5[:, :], 1e-5, None, ALU.add))
            p.s(lambda e, k, ks: e.activation(var5[:, :], var5[:, :], AF.Sqrt))
            p.v(lambda e, k, ks: e.reciprocal(var5[:, :], var5[:, :]))
            p.v(lambda e, k, ks: e.tensor_scalar(nmean[:, :], mean5[:, :], -1.0, None, ALU.mult))
            p.v(lambda e, k, ks: e.tensor_scalar(outt[:, :], sum5[:, :], nmean[:, 0:1], None, ALU.add))
            p.v(lambda e, k, ks: e.tensor_scalar(outt[:, :], outt[:, :], var5[:, 0:1], None, ALU.mult))
            p.sy(lambda e, k, ks, tile=tile: e.dma_start(
                out=outd[ds(tile * 128, 128), :], in_=outt[:, :]))

        p.gc(lambda e, k, ks: e.memset(dum[:, :], 0.0))
        p.emit(ctx)
    return p.nc



_PROGS = {}


def _get_prog(name, builder):
    if name not in _PROGS:
        _PROGS[name] = builder()
    return _PROGS[name]


def kernel5(**inputs):
    cores = list(range(NC))
    x = inputs["L0_attn_x"]

    nc1 = _get_prog("l0attn", build_l0_attn)
    in1 = [l0_attn_host_inputs(x, inputs["L0_attn_w_qkv"], inputs["L0_attn_w_out"],
                               inputs["L0_attn_b_out"], c) for c in cores]
    r1 = run_bass_kernel_spmd(nc1, in1, core_ids=cores).results

    nc2 = _get_prog("l0ffn", build_l0_ffn)
    wm2 = l0_ffn_host_inputs(None, inputs, 0)
    in2 = [{**wm2, "x1": r1[c]["x1"]} for c in cores]
    r2 = run_bass_kernel_spmd(nc2, in2, core_ids=cores).results

    nc3 = _get_prog("l1attn", build_l1_attn)
    wm3 = l1_attn_host_inputs(inputs)
    in3 = [{**wm3, "x2": r2[c]["x2"]} for c in cores]
    r3 = run_bass_kernel_spmd(nc3, in3, core_ids=cores).results

    nc4 = _get_prog("l1ffn", build_l1_ffn)
    wm4 = l1_ffn_host_inputs(inputs)
    in4 = [{**wm4, "x3bf": r3[c]["x3bf"], "slots": r3[c]["slots"],
            "t2e": r3[c]["t2e"]} for c in cores]
    r4 = run_bass_kernel_spmd(nc4, in4, core_ids=cores).results

    hists = np.ascontiguousarray(
        np.stack([r3[c]["hist"][:, 0] for c in cores], 0)).astype(np.float32)
    nc5 = _get_prog("finale", build_finale)
    in5 = [{"x3bf": r3[c]["x3bf"], "yslots": r4[c]["yslots"],
            "slots": r3[c]["slots"], "gates": r3[c]["gates"],
            "ohT": r3[c]["ohT"], "locT": r3[c]["locT"], "hists": hists}
           for c in cores]
    r5 = run_bass_kernel_spmd(nc5, in5, core_ids=cores).results

    out = np.concatenate([r5[c]["out"].reshape(2, T, D) for c in cores], axis=0)
    return out.astype(np.float32)



def build_mega():
    p = Prog()
    nc = p.nc
    dr = {}

    def ein(name, shape, dt=F32):
        dr[name] = nc.dram_tensor(name, shape, dt, kind="ExternalInput").ap()

    def eout(name, shape, dt=F32):
        dr[name] = nc.dram_tensor(name, shape, dt, kind="ExternalOutput").ap()

    def eint(name, shape, dt=F32):
        dr[name] = nc.dram_tensor(name, shape, dt).ap()

    ein("xT", [B_CORE, D, T]); ein("wq_ps", [D, D]); ein("wk_ps", [D, D])
    ein("wv_ps", [D, D]); ein("wo_cr", [128, 8 * D]); ein("bout", [128, 8])
    ein("cs_tab", [128, T]); ein("sc_tab", [128, T]); ein("cmask", [128, 2048])
    eint("x1", [B_CORE, D, T])
    ein("wg_cr", [17 * 128, 8192]); ein("wu_cr", [17 * 128, 8192])
    ein("wd_cr", [17 * 128, 8192]); ein("gw_cr", [128, 128]); ein("ebias", [128, 16])
    eint("x2", [B_CORE, D, T])
    ein("wq1", [D, D]); ein("wk1", [D, D]); ein("wv1", [D, D])
    ein("bq1", [128, 8]); ein("bk1", [128, 8]); ein("bv1", [128, 8])
    ein("wlatT", [64, 64]); ein("blat", [64, 1]); ein("wproj", [128, 8 * D])
    ein("bproj", [128, 8]); ein("wg1", [128, 128]); ein("bands", [128, 1152])
    ein("ewh", [8, 3]); ein("ewt", [8, 3]); ein("U128", [128, 128])
    ein("U16s", [16, 16]); ein("U32s", [32, 32])
    eout("x3bf", [4096, D], BF16); eout("hist", [16, 1]); eout("gates", [32, 128])
    eout("ohT", [16, 4096], BF16); eout("locT", [16, 4096])
    eout("slots", [128, 32], I32); eout("t2e", [1, 48], I32)
    ein("w1_cr", [16 * 128, 8 * 4096]); ein("w2_cr", [16 * 128, 32 * 1024])
    eout("yslots", [6144, D], BF16)

    emit_l0_attn(p, dr)
    emit_l0_ffn(p, dr)
    emit_l1_attn(p, dr)
    emit_l1_ffn(p, dr)
    with ExitStack() as ctx:
        p.emit(ctx)
    return p.nc


def kernel(**inputs):
    cores = list(range(NC))
    x = inputs["L0_attn_x"]
    ncm = _get_prog("mega", build_mega)
    wm2 = l0_ffn_host_inputs(None, inputs, 0)
    wm3 = l1_attn_host_inputs(inputs)
    wm4 = l1_ffn_host_inputs(inputs)
    in_m = []
    for c in cores:
        m = dict(l0_attn_host_inputs(x, inputs["L0_attn_w_qkv"], inputs["L0_attn_w_out"],
                                     inputs["L0_attn_b_out"], c))
        m.update(wm2); m.update(wm3); m.update(wm4)
        in_m.append(m)
    rm = run_bass_kernel_spmd(ncm, in_m, core_ids=cores).results
    hists = np.ascontiguousarray(
        np.stack([rm[c]["hist"][:, 0] for c in cores], 0)).astype(np.float32)
    nc5 = _get_prog("finale", build_finale)
    in5 = [{"x3bf": rm[c]["x3bf"], "yslots": rm[c]["yslots"],
            "slots": rm[c]["slots"], "gates": rm[c]["gates"],
            "ohT": rm[c]["ohT"], "locT": rm[c]["locT"], "hists": hists}
           for c in cores]
    r5 = run_bass_kernel_spmd(nc5, in5, core_ids=cores).results
    out = np.concatenate([r5[c]["out"].reshape(2, T, D) for c in cores], axis=0)
    return out.astype(np.float32)
